# version 06
# baseline (speedup 1.0000x reference)
import numpy as np
import ml_dtypes
from contextlib import ExitStack

import concourse.bass as bass
import concourse.mybir as mybir
import concourse.bass_utils as bass_utils

F32 = mybir.dt.float32
BF16 = mybir.dt.bfloat16
I32 = mybir.dt.int32
AF = mybir.ActivationFunctionType
ALU = mybir.AluOpType
AX = mybir.AxisListType

D = 4096
NCORES = 8
SEQ = 16384
TOK = SEQ // NCORES
KT = D // 128
NG = 8
NSL = D // NCORES
NMOD = 6 * D
WMK = D // NCORES
ER = 64


class Sem:
    def __init__(self, handle):
        self.h = handle
        self.v = 0

    def inc(self, ins, n=1):
        ins.then_inc(self.h, n)
        self.v += n
        return self.v


def build_program(tok=TOK, nsub=4):
    sub = tok // nsub
    st_n = sub // 128
    nc = bass.Bass("TRN2", target_bir_lowering=False)

    x_d = nc.dram_tensor("x", [tok, D], F32, kind="ExternalInput")
    wT_d = nc.dram_tensor("wt", [D, NSL], F32, kind="ExternalInput")
    mT_d = nc.dram_tensor("maskt", [D, NSL], I32, kind="ExternalInput")
    fT_d = nc.dram_tensor("wbt", [D, NSL], F32, kind="ExternalInput")
    wm_d = nc.dram_tensor("wmod", [WMK, NMOD], F32, kind="ExternalInput")
    cs_d = nc.dram_tensor("csl", [4, 128], F32, kind="ExternalInput")
    gw_d = nc.dram_tensor("gwn", [8, D], F32, kind="ExternalInput")
    A_d = nc.dram_tensor("afl", [ER, D], F32, kind="ExternalInput")
    Bf_d = nc.dram_tensor("bfl", [ER, D], F32, kind="ExternalInput")
    idb_d = nc.dram_tensor("identb", [128, 128], BF16, kind="ExternalInput")
    idf_d = nc.dram_tensor("identf", [128, 128], F32, kind="ExternalInput")
    gsel_d = nc.dram_tensor("grpsel", [128, 4], F32, kind="ExternalInput")
    ex4_d = nc.dram_tensor("expand4", [4, 128], F32, kind="ExternalInput")
    ex8_d = nc.dram_tensor("expand8", [8, 64], F32, kind="ExternalInput")
    onesb_d = nc.dram_tensor("onesb", [1, 128], BF16, kind="ExternalInput")
    onesf_d = nc.dram_tensor("onesf", [1, 128], F32, kind="ExternalInput")
    osel_d = nc.dram_tensor("osel", [8, 1], F32, kind="ExternalInput")
    y_d = nc.dram_tensor("y", [tok, D], F32, kind="ExternalOutput")

    modp_d = nc.dram_tensor("modp", [NMOD], F32)
    modf_d = nc.dram_tensor("modf", [NMOD], F32)
    mproc_d = nc.dram_tensor("mproc", [2 * D], F32)
    wag_i = nc.dram_tensor("wag_i", [128, KT * NSL], BF16)
    wag_o = nc.dram_tensor("wag_o", [8 * 128, KT * NSL], BF16)
    fag_i = nc.dram_tensor("fag_i", [128, KT * NSL], BF16)
    fag_o = nc.dram_tensor("fag_o", [8 * 128, KT * NSL], BF16)
    baga_i = nc.dram_tensor("baga_i", [NSL], F32)
    baga_o = nc.dram_tensor("baga_o", [D], F32)
    bagf_i = nc.dram_tensor("bagf_i", [NSL], F32)
    bagf_o = nc.dram_tensor("bagf_o", [D], F32)

    rg = [list(range(NCORES))]
    M = {}

    with ExitStack() as ctx:
        en = ctx.enter_context

        x_sb = en(nc.sbuf_tensor("x_sb", [128, st_n * D], BF16))
        xnT = en(nc.sbuf_tensor("xnT", [128, st_n * KT * 128], BF16))
        wcol = [en(nc.sbuf_tensor(f"wcol{i}", [128, KT * 512], BF16))
                for i in range(2)]
        xn_st = [en(nc.sbuf_tensor(f"xn{i}", [128, D], BF16)) for i in range(2)]
        idb = en(nc.sbuf_tensor("idb", [128, 128], BF16))
        idf = en(nc.sbuf_tensor("idf", [128, 128], F32))
        onesb = en(nc.sbuf_tensor("onesb_s", [1, 128], BF16))
        onesf = en(nc.sbuf_tensor("onesf_s", [1, 128], F32))
        gsel = en(nc.sbuf_tensor("gsel", [128, 4], F32))
        ex4 = en(nc.sbuf_tensor("ex4", [4, 128], F32))
        ex8 = en(nc.sbuf_tensor("ex8", [8, 64], F32))
        ca1c = en(nc.sbuf_tensor("ca1c", [128, KT], F32))
        cb1c = en(nc.sbuf_tensor("cb1c", [128, KT], F32))
        sapc = en(nc.sbuf_tensor("sapc", [128, KT], BF16))
        sbpc = en(nc.sbuf_tensor("sbpc", [128, KT], BF16))
        ga_b = en(nc.sbuf_tensor("ga_b", [128, 512], F32))
        gb_b = en(nc.sbuf_tensor("gb_b", [128, 512], F32))
        bbuf = [en(nc.sbuf_tensor(f"bbuf{i}", [1, 512], BF16))
                for i in range(2)]
        SgT = en(nc.sbuf_tensor("SgT", [128, KT * 72], BF16))
        bA_c = en(nc.sbuf_tensor("bA_c", [72, 1], F32))
        BfT = en(nc.sbuf_tensor("BfT", [64, D], BF16))
        hterT = en(nc.sbuf_tensor("hterT", [64, sub], BF16))
        logit = en(nc.sbuf_tensor("logit", [8, sub], F32))
        lg_tm = en(nc.sbuf_tensor("lg_tm", [128, st_n * 8], F32))
        wd_tm = en(nc.sbuf_tensor("wd_tm", [128, st_n * 8], F32))
        lg2 = en(nc.sbuf_tensor("lg2", [128, st_n * 8], F32))
        ggs = en(nc.sbuf_tensor("ggs", [128, 4 * st_n], F32))
        wdT = en(nc.sbuf_tensor("wdT", [8, sub], F32))
        hsT = en(nc.sbuf_tensor("hsT", [64, sub], BF16))
        stt = en(nc.sbuf_tensor("stt", [128, 48], F32))
        lsc = en(nc.sbuf_tensor("lsc", [128, 17], F32))
        mvar = en(nc.sbuf_tensor("mvar", [128, 4 * st_n], F32))
        epsc = en(nc.sbuf_tensor("epsc", [128, 1], F32))
        ostg = [en(nc.sbuf_tensor(f"ostg{i}", [128, 512], F32))
                for i in range(2)]

        ps = [en(nc.psum_tensor(f"ps{i}", [128, 512], F32)) for i in range(8)]

        sems = {}
        for name in ["s_init", "s_mw0", "s_mw1", "s_mvec", "s_ar", "s_vec",
                     "s_norm", "s_var", "s_tp", "s_xnt", "s_wc0", "s_wc1",
                     "s_mm", "s_ev", "s_gt", "s_gv", "s_gs", "s_out",
                     "s_y0", "s_y1", "s_x0", "s_x1", "s_x2", "s_x3",
                     "s_mk0", "s_mk1", "s_wt0", "s_wt1", "s_sm",
                     "s_bias", "s_ag", "s_misc", "s_bl0", "s_bl1",
                     "s_std", "s_nb", "s_mp0", "s_mp1", "s_cc", "s_mw2", "s_mw3",
                     "s_sm2", "s_mw4", "s_mw5", "s_mw6", "s_mw7",
                     "s_mp2", "s_mp3"]:
            sems[name] = Sem(en(nc.semaphore(name)))
        S = lambda n: sems[n]
        s_x = [S(f"s_x{i}") for i in range(4)]
        s_mw8 = [S(f"s_mw{i}") for i in range(8)]
        s_mk = [S("s_mk0"), S("s_mk1")]
        s_wt = [S("s_wt0"), S("s_wt1")]
        s_wcp = [S("s_wc0"), S("s_wc1")]
        s_bl = [S("s_bl0"), S("s_bl1")]
        s_yp = [S("s_y0"), S("s_y1")]
        s_mp = [S("s_mp0"), S("s_mp1"), S("s_mp2"), S("s_mp3")]
        s_smp = [S("s_sm"), S("s_sm2")]

        block = en(nc.Block())

        csl_s = en(nc.sbuf_tensor("csl_s", [4, 128], F32))
        sT = en(nc.sbuf_tensor("sT", [128, 4], F32))
        modv = en(nc.sbuf_tensor("modv", [32, 6 * 128], F32))
        gbf64 = en(nc.sbuf_tensor("gbf64", [64, D], BF16))
        wtile = en(nc.sbuf_tensor("wtile", [128, 2 * 512], F32))
        mtile = en(nc.sbuf_tensor("mtile", [128, 2 * 512], I32))
        red1 = en(nc.sbuf_tensor("red1", [128, 16], F32))
        osel = en(nc.sbuf_tensor("osel_s", [8, 1], F32))
        btmp2 = ostg[1][0:1, 0:512]
        xnTf = xnT[:].bitcast(F32)
        smfr = [xnTf[0:64, 0:1024], xnTf[0:64, 1024:2048]]
        s_smr = None
        brow = ostg[0][0:1, 0:512]
        btmp = ostg[1][0:1, 0:72]
        gsl8 = en(nc.sbuf_tensor("gsl8", [8, 2 * 512], F32))
        wmblk = wtile
        bm4 = en(nc.sbuf_tensor("bm4", [4, KT * 16], F32))
        em16 = en(nc.sbuf_tensor("em16", [128, 16], F32))

        @block.gpsimd
        def _(g):
            for dst, src in ((idb, idb_d), (idf, idf_d), (onesb, onesb_d),
                             (onesf, onesf_d), (gsel, gsel_d), (ex4, ex4_d),
                             (ex8, ex8_d), (csl_s, cs_d), (osel, osel_d)):
                S("s_init").inc(g.dma_start(out=dst[:], in_=src.ap()), 16)
            for st_i in range(st_n):
                s_x[st_i].inc(
                    g.dma_start(out=x_sb[:, st_i * D:(st_i + 1) * D],
                                in_=x_d.ap()[st_i * 128:(st_i + 1) * 128, :]), 16)
        M["init"] = S("s_init").v

        @block.scalar
        def _(s):
            s.wait_ge(S("s_init").h, M["init"])
            S("s_misc").inc(s.activation(csl_s[:], csl_s[:], AF.Silu))
        M["silu"] = S("s_misc").v

        @block.tensor
        def _(t):
            t.wait_ge(S("s_misc").h, M["silu"])
            S("s_misc").inc(t.transpose(ps[0][0:128, 0:4], csl_s[:],
                                        idf[0:4, 0:4]))
        M["sT_ps"] = S("s_misc").v

        @block.vector
        def _(v):
            v.memset(epsc[:], 1e-5)
            v.wait_ge(S("s_misc").h, M["sT_ps"])
            S("s_misc").inc(v.tensor_copy(sT[:], ps[0][0:128, 0:4]))
        M["sT"] = S("s_misc").v

        mw_marks = {}
        mvslot = [wcol[0][:].bitcast(F32), wcol[1][:].bitcast(F32)]

        @block.sync
        def _(sy):
            for q in range(12):
                for kt in range(4):
                    if q >= 2:
                        sy.wait_ge(S("s_mvec").h, (q - 2) * 16 + 12 + kt + 1)
                    mw_marks[(q, kt)] = s_mw8[4 * (q % 2) + kt].inc(
                        sy.dma_start(
                            out=mvslot[q % 2][:, kt * 2048:(kt + 1) * 2048],
                            in_=wm_d.ap()[kt * 128:(kt + 1) * 128,
                                          q * 2048:(q + 1) * 2048]), 16)

        @block.tensor
        def _(t):
            t.wait_ge(S("s_misc").h, M["sT"])
            for q in range(12):
                for j in range(4):
                    nt = q * 4 + j
                    pbank = ps[1] if nt % 2 == 0 else ps[2]
                    if nt >= 2:
                        t.wait_ge(S("s_vec").h, S("s_vec").v + nt - 1)
                    for kt in range(4):
                        t.wait_ge(s_mw8[4 * (q % 2) + kt].h, mw_marks[(q, kt)])
                        S("s_mvec").inc(
                            t.matmul(pbank[0:1, :], sT[:, kt:kt + 1],
                                     mvslot[q % 2][:, kt * 2048 + j * 512:
                                                   kt * 2048 + (j + 1) * 512],
                                     start=(kt == 0), stop=(kt == 3)))
        mv_base = S("s_mvec").v - 192

        mp_marks = []
        mps_v0 = S("s_vec").v
        xnTf0 = xnT[:].bitcast(F32)
        mpslot = [ostg[0][0:1, 0:512], ostg[1][0:1, 0:512],
                  xnTf0[0:1, 2048:2560], xnTf0[0:1, 2560:3072]]

        @block.vector
        def _(v):
            for nt in range(48):
                pbank = ps[1] if nt % 2 == 0 else ps[2]
                v.wait_ge(S("s_mvec").h, mv_base + (nt + 1) * 4)
                if nt >= 4:
                    v.wait_ge(s_mp[nt % 4].h, 16 * (nt // 4))
                S("s_vec").inc(v.tensor_copy(mpslot[nt % 4], pbank[0:1, :]))
        M["modps"] = S("s_vec").v

        @block.gpsimd
        def _(g):
            for nt in range(48):
                g.wait_ge(S("s_vec").h, mps_v0 + nt + 1)
                mp_marks.append(s_mp[nt % 4].inc(
                    g.dma_start(
                        out=modp_d.ap()[bass.ds(nt * 512, 512)].unsqueeze(0),
                        in_=mpslot[nt % 4]), 16))
            for i in range(4):
                g.wait_ge(s_mp[i].h, s_mp[i].v)
            S("s_cc").inc(g.collective_compute(
                "AllReduce", ALU.add, replica_groups=rg,
                ins=[modp_d.ap()], outs=[modf_d.ap()]), 1)
            g.wait_ge(S("s_cc").h, S("s_cc").v)
            S("s_ar").inc(
                g.dma_start(out=modv[:].rearrange("j (v p) -> j v p", v=6),
                            in_=modf_d.ap().rearrange("(v j p) -> j v p",
                                                      v=6, p=128)), 16)
        M["modv"] = S("s_ar").v

        def vrow(i):
            return modv[:, i * 128:(i + 1) * 128]

        @block.vector
        def _(v):
            v.wait_ge(S("s_ar").h, M["modv"])
            v.tensor_scalar_add(vrow(1), vrow(1), 1.0)
            v.tensor_scalar_add(vrow(4), vrow(4), 1.0)
            v.drain()
            v.reciprocal(wtile[0:32, 0:128], vrow(1))
            v.drain()
            v.tensor_mul(vrow(0), vrow(0), wtile[0:32, 0:128])
            v.drain()
            v.reciprocal(wtile[0:32, 0:128], vrow(4))
            v.drain()
            v.tensor_mul(vrow(3), vrow(3), wtile[0:32, 0:128])
            v.drain()
            for src in (2, 5):
                gv = vrow(src)
                dd = wtile[0:32, 0:128]
                v.tensor_scalar(dd, gv, 0.0, None, ALU.is_ge)
                v.drain()
                v.tensor_scalar(dd, dd, 2.0, -1.0, ALU.mult, ALU.add)
                v.drain()
                v.tensor_mul(gv, gv, dd)
                v.drain()
                v.tensor_scalar_max(gv, gv, 0.25)
                v.drain()
                v.tensor_mul(gv, gv, dd)
                v.drain()
            S("s_vec").inc(v.drain())
        M["mrows"] = S("s_vec").v

        @block.tensor
        def _(t):
            t.wait_ge(S("s_vec").h, M["mrows"])
            for i, src in enumerate((1, 4, 0, 3)):
                S("s_gt").inc(t.transpose(ps[3 + i][:, 0:32], vrow(src),
                                          idf[0:32, 0:32]))
        M["mcolT"] = S("s_gt").v

        @block.vector
        def _(v):
            v.wait_ge(S("s_gt").h, M["mcolT"])
            S("s_vec").inc(v.tensor_copy(ca1c[:], ps[3][:, 0:32]))
            S("s_vec").inc(v.tensor_copy(cb1c[:], ps[4][:, 0:32]))
            S("s_vec").inc(v.tensor_copy(sapc[:], ps[5][:, 0:32]))
            S("s_vec").inc(v.tensor_copy(sbpc[:], ps[6][:, 0:32]))
        M["mcols"] = S("s_vec").v

        @block.gpsimd
        def _(g):
            g.wait_ge(S("s_vec").h, M["mcols"])
            mp = mproc_d.ap().rearrange("(j p) -> j p", j=64)
            S("s_ar").inc(g.dma_start(out=mp[0:32, :], in_=vrow(2)), 16)
            S("s_ar").inc(g.dma_start(out=mp[32:64, :], in_=vrow(5)), 16)
            g.wait_ge(S("s_ar").h, S("s_ar").v)
            S("s_ar").inc(
                g.dma_start(out=gsl8[:, 0:512],
                            in_=mproc_d.ap().rearrange("(h j) -> h j", h=16)
                            [0:8, :]), 16)
            S("s_ar").inc(
                g.dma_start(out=gsl8[:, 512:1024],
                            in_=mproc_d.ap().rearrange("(h j) -> h j", h=16)
                            [8:16, :]), 16)
            g.wait_ge(S("s_gt").h, S("s_gt").v + 4)
            for i in range(8):
                S("s_ar").inc(
                    g.dma_start(out=btmp2,
                                in_=mproc_d.ap()[bass.ds(D + i * 512, 512)]
                                .unsqueeze(0)), 16)
                g.wait_ge(S("s_gt").h, S("s_gt").v + 5 + i)
        ar_rows0 = S("s_ar").v - 10 * 16

        gt_rows0 = S("s_gt").v
        v_rows0 = S("s_vec").v

        @block.tensor
        def _(t):
            t.wait_ge(S("s_ar").h, ar_rows0 + 32)
            S("s_gt").inc(t.matmul(ps[4][0:1, :], osel[:], gsl8[:, 0:512],
                                   start=True, stop=True))
            S("s_gt").inc(t.matmul(ps[5][0:1, :], osel[:], gsl8[:, 512:1024],
                                   start=True, stop=True))
            t.wait_ge(S("s_vec").h, v_rows0 + 1)
            S("s_gt").inc(t.matmul(ps[0][:], onesf[:], brow,
                                   start=True, stop=True))
            t.wait_ge(S("s_vec").h, v_rows0 + 2)
            S("s_gt").inc(t.matmul(ps[1][:], onesf[:], btmp2,
                                   start=True, stop=True))
            for i in range(8):
                pbank = ps[2] if i % 2 == 0 else ps[3]
                t.wait_ge(S("s_ar").h, ar_rows0 + 32 + (i + 1) * 16)
                if i >= 2:
                    t.wait_ge(S("s_vec").h, v_rows0 + 4 + i - 1)
                S("s_gt").inc(t.matmul(pbank[0:64, :], onesf[0:1, 0:64],
                                       btmp2, start=True, stop=True))

        @block.vector
        def _(v):
            v.drain()
            v.wait_ge(S("s_gt").h, gt_rows0 + 1)
            S("s_vec").inc(v.tensor_copy(brow, ps[4][0:1, :]))
            v.wait_ge(S("s_gt").h, gt_rows0 + 2)
            S("s_vec").inc(v.tensor_copy(btmp2, ps[5][0:1, :]))
            v.wait_ge(S("s_gt").h, gt_rows0 + 3)
            S("s_vec").inc(v.tensor_copy(ga_b[:], ps[0][:]))
            v.wait_ge(S("s_gt").h, gt_rows0 + 4)
            S("s_vec").inc(v.tensor_copy(gb_b[:], ps[1][:]))
            for i in range(8):
                pbank = ps[2] if i % 2 == 0 else ps[3]
                v.wait_ge(S("s_gt").h, gt_rows0 + 5 + i)
                S("s_vec").inc(v.tensor_copy(gbf64[:, i * 512:(i + 1) * 512],
                                             pbank[0:64, :]))
        M["gbf64"] = S("s_vec").v
        M["gbc"] = M["gbf64"]

        mk_v0 = S("s_vec").v
        mk_marks = []

        @block.sync
        def _(sy):
            for kt in range(KT):
                buf = (kt % 2) * 512
                if kt >= 2:
                    sy.wait_ge(S("s_vec").h, mk_v0 + 2 * kt - 3)
                mk_marks.append(s_mk[kt % 2].inc(
                    sy.dma_start(out=mtile[:, buf:buf + 512],
                                 in_=mT_d.ap()[kt * 128:(kt + 1) * 128, :]),
                    16))
        bm_gt0 = S("s_gt").v

        @block.tensor
        def _(t):
            t.wait_ge(S("s_init").h, M["init"])
            for kt in range(KT):
                t.wait_ge(S("s_vec").h, mk_v0 + 2 * kt + 1)
                pbank = ps[4] if kt % 2 == 0 else ps[5]
                S("s_gt").inc(t.matmul(pbank[0:4, 0:16], gsel[:], red1[:],
                                       start=True, stop=True))

        @block.vector
        def _(v):
            v.drain()
            for kt in range(KT):
                buf = (kt % 2) * 512
                v.wait_ge(s_mk[kt % 2].h, mk_marks[kt])
                S("s_vec").inc(v.tensor_reduce(
                    red1[:],
                    mtile[:, buf:buf + 512].rearrange("p (nb i) -> p nb i", i=32),
                    AX.X, ALU.add))
                pbank = ps[4] if kt % 2 == 0 else ps[5]
                v.wait_ge(S("s_gt").h, bm_gt0 + kt + 1)
                S("s_vec").inc(v.tensor_copy(bm4[:, kt * 16:(kt + 1) * 16],
                                             pbank[0:4, 0:16]))
            v.drain()
            v.tensor_scalar(bm4[:], bm4[:], 0.5, None, ALU.is_ge)
            S("s_vec").inc(v.drain())
        M["bmb"] = S("s_vec").v

        def prep_matrix(mat_idx, src_d, dst_sb, ag_in, c1col, g_bcast, masked):
            gs_base = S("s_gs").v
            wt_marks = []

            @block.sync
            def _(sy):
                for kt in range(KT):
                    buf = (kt % 2) * 512
                    if kt < 2:
                        sy.wait_ge(S("s_vec").h, M["mrows"])
                    if gs_base + kt - 1 > 0:
                        sy.wait_ge(S("s_gs").h, gs_base + kt - 1)
                    wt_marks.append(s_wt[kt % 2].inc(
                        sy.dma_start(out=wtile[:, buf:buf + 512],
                                     in_=src_d.ap()[kt * 128:(kt + 1) * 128, :]),
                        16))

            em_gt0 = S("s_gt").v

            if masked:
                @block.tensor
                def _(t):
                    t.wait_ge(S("s_vec").h, M["bmb"])
                    for kt in range(KT):
                        pbank = ps[6] if kt % 2 == 0 else ps[7]
                        if kt >= 2:
                            t.wait_ge(S("s_gs").h, gs_base + kt - 1)
                        S("s_gt").inc(t.matmul(
                            pbank[:, 0:16], ex4[:],
                            bm4[0:4, kt * 16:(kt + 1) * 16],
                            start=True, stop=True))

            @block.vector
            def _(v):
                v.drain()
                v.wait_ge(S("s_vec").h, M["gbf64"])
                for kt in range(KT):
                    buf = (kt % 2) * 512
                    v.wait_ge(s_wt[kt % 2].h, wt_marks[kt])
                    dst = dst_sb[:, kt * 512:(kt + 1) * 512]
                    wt = wtile[:, buf:buf + 512]
                    if masked:
                        pbank = ps[6] if kt % 2 == 0 else ps[7]
                        v.wait_ge(S("s_gt").h, em_gt0 + kt + 1)
                        v.tensor_copy(em16[:], pbank[:, 0:16])
                        v.drain()
                        v.tensor_tensor(
                            wt.rearrange("p (nb i) -> p nb i", i=32),
                            wt.rearrange("p (nb i) -> p nb i", i=32),
                            em16[:].unsqueeze(2).broadcast_to([128, 16, 32]),
                            ALU.mult)
                        v.drain()
                    v.tensor_scalar(wt, wt, c1col[:, kt:kt + 1], None, ALU.mult)
                    v.drain()
                    S("s_gs").inc(v.tensor_tensor(dst, wt, g_bcast[:], ALU.mult))

            gsv = S("s_gs").v
            sc = sapc if mat_idx == 0 else sbpc

            @block.tensor
            def _(t):
                t.wait_ge(S("s_gs").h, gsv)
                for kt in range(KT):
                    S("s_gt").inc(t.matmul(ps[0][0:1, :], sc[:, kt:kt + 1],
                                           dst_sb[:, kt * 512:(kt + 1) * 512],
                                           start=(kt == 0), stop=(kt == KT - 1)))
            bv = S("s_gt").v

            @block.vector
            def _(v):
                v.wait_ge(S("s_gt").h, bv)
                if mat_idx == 1:
                    v.wait_ge(S("s_bias").h, 32)
                S("s_vec").inc(v.tensor_copy(brow, ps[0][0:1, :]))
            bvv = S("s_vec").v

            bag_in = baga_i if mat_idx == 0 else bagf_i
            bag_out = baga_o if mat_idx == 0 else bagf_o
            ag_out = wag_o if mat_idx == 0 else fag_o

            @block.gpsimd
            def _(g):
                g.wait_ge(S("s_vec").h, bvv)
                S("s_bias").inc(
                    g.dma_start(out=bag_in.ap().unsqueeze(0), in_=brow), 16)
                S("s_bias").inc(g.dma_start(out=ag_in.ap(), in_=dst_sb[:]), 16)
                g.wait_ge(S("s_bias").h, S("s_bias").v)
                S("s_ag").inc(g.collective_compute(
                    "AllGather", ALU.bypass, replica_groups=rg,
                    ins=[ag_in.ap()], outs=[ag_out.ap()]), 1)
                S("s_ag").inc(g.collective_compute(
                    "AllGather", ALU.bypass, replica_groups=rg,
                    ins=[bag_in.ap()], outs=[bag_out.ap()]), 1)
            M[f"ag_{mat_idx}"] = S("s_ag").v

        prep_matrix(0, wT_d, wcol[0], wag_i, ca1c, ga_b, True)
        prep_matrix(1, fT_d, wcol[1], fag_i, cb1c, gb_b, False)
        M["prep_done"] = S("s_bias").v

        a_gt0 = S("s_gt").v
        a_v0 = S("s_vec").v
        a_marks = []

        @block.sync
        def _(sy):
            for q in range(4):
                if q >= 2:
                    sy.wait_ge(S("s_gt").h, a_gt0 + (q - 1) * 8)
                a_marks.append(s_smp[q % 2].inc(
                    sy.dma_start(out=smfr[q % 2],
                                 in_=A_d.ap()[:, q * 1024:(q + 1) * 1024]), 16))

        @block.tensor
        def _(t):
            t.wait_ge(S("s_vec").h, M["gbf64"])
            for q in range(4):
                t.wait_ge(s_smp[q % 2].h, a_marks[q])
                for j in range(8):
                    kt = q * 8 + j
                    pbank = ps[1] if kt % 2 == 0 else ps[2]
                    if kt >= 2:
                        t.wait_ge(S("s_vec").h, a_v0 + kt - 1)
                    S("s_gt").inc(t.transpose(
                        pbank[:, 0:64], smfr[q % 2][:, j * 128:(j + 1) * 128],
                        idf[0:64, 0:64]))

        @block.vector
        def _(v):
            v.wait_ge(S("s_vec").h, M["gbf64"])
            for kt in range(KT):
                pbank = ps[1] if kt % 2 == 0 else ps[2]
                v.wait_ge(S("s_gt").h, a_gt0 + kt + 1)
                S("s_vec").inc(v.tensor_scalar(
                    SgT[:, kt * 72:kt * 72 + 64], pbank[:, 0:64],
                    cb1c[:, kt:kt + 1], None, ALU.mult))
        M["A_T"] = S("s_vec").v

        gw_gt0 = S("s_gt").v
        gw_v0 = S("s_vec").v
        gw_marks = []

        @block.sync
        def _(sy):
            for q in range(4):
                if q < 2:
                    sy.wait_ge(S("s_gt").h, a_gt0 + 32)
                else:
                    sy.wait_ge(S("s_gt").h, gw_gt0 + (q - 1) * 8)
                gw_marks.append(s_smp[q % 2].inc(
                    sy.dma_start(out=smfr[q % 2][0:8, :],
                                 in_=gw_d.ap()[:, q * 1024:(q + 1) * 1024]), 16))

        @block.tensor
        def _(t):
            t.wait_ge(S("s_vec").h, gw_v0)
            for q in range(4):
                t.wait_ge(s_smp[q % 2].h, gw_marks[q])
                for j in range(8):
                    kt = q * 8 + j
                    pbank = ps[1] if kt % 2 == 0 else ps[2]
                    if kt >= 2:
                        t.wait_ge(S("s_vec").h, gw_v0 + kt - 1)
                    S("s_gt").inc(t.transpose(
                        pbank[:, 0:8], smfr[q % 2][0:8, j * 128:(j + 1) * 128],
                        idf[0:8, 0:8]))

        @block.vector
        def _(v):
            for kt in range(KT):
                pbank = ps[1] if kt % 2 == 0 else ps[2]
                v.wait_ge(S("s_gt").h, gw_gt0 + kt + 1)
                S("s_vec").inc(v.tensor_scalar(
                    SgT[:, kt * 72 + 64:kt * 72 + 72], pbank[:, 0:8],
                    cb1c[:, kt:kt + 1], None, ALU.mult))
        M["Sg"] = S("s_vec").v

        bf_v0 = S("s_vec").v
        bf_marks = []

        @block.sync
        def _(sy):
            for q in range(4):
                if q < 2:
                    sy.wait_ge(S("s_gt").h, gw_gt0 + 32)
                else:
                    sy.wait_ge(S("s_vec").h, bf_v0 + q - 1)
                bf_marks.append(s_smp[q % 2].inc(
                    sy.dma_start(out=smfr[q % 2],
                                 in_=Bf_d.ap()[:, q * 1024:(q + 1) * 1024]), 16))

        @block.vector
        def _(v):
            v.drain()
            for q in range(4):
                v.wait_ge(s_smp[q % 2].h, bf_marks[q])
                S("s_vec").inc(v.tensor_tensor(
                    BfT[:, q * 1024:(q + 1) * 1024], smfr[q % 2],
                    gbf64[:, q * 1024:(q + 1) * 1024], ALU.mult))
        M["BfT"] = S("s_vec").v

        bA_gt0 = S("s_gt").v

        @block.tensor
        def _(t):
            t.wait_ge(S("s_vec").h, M["Sg"])
            for kt in range(KT):
                S("s_gt").inc(t.matmul(ps[3][0:1, 0:72], sbpc[:, kt:kt + 1],
                                       SgT[:, kt * 72:(kt + 1) * 72],
                                       start=(kt == 0), stop=(kt == KT - 1)))
        M["bA_ps"] = S("s_gt").v

        @block.vector
        def _(v):
            v.wait_ge(S("s_gt").h, M["bA_ps"])
            S("s_vec").inc(v.tensor_copy(btmp, ps[3][0:1, 0:72]))
        M["bA_sb"] = S("s_vec").v

        @block.tensor
        def _(t):
            t.wait_ge(S("s_vec").h, M["bA_sb"])
            S("s_gt").inc(t.transpose(ps[4][0:72, 0:1], btmp,
                                      idf[0:1, 0:1]))
        M["bA_T"] = S("s_gt").v

        @block.vector
        def _(v):
            v.wait_ge(S("s_gt").h, M["bA_T"])
            S("s_vec").inc(v.tensor_copy(bA_c[:], ps[4][0:72, 0:1]))
        M["bA"] = S("s_vec").v

        M["ag"] = M["ag_1"]

        def xsb(st_i):
            return x_sb[:, st_i * D:(st_i + 1) * D]

        def xnT_blk(st_i, kt):
            off = (st_i * KT + kt) * 128
            return xnT[:, off:off + 128]

        oc_total = 2 * NG * st_n * 0 + 0
        oc = [0]

        for b in range(nsub):
            if b > 0:
                ev_need = S("s_out").v

                @block.gpsimd
                def _(g, b=b, ev_need=ev_need):
                    g.wait_ge(S("s_out").h, ev_need)
                    for st_i in range(st_n):
                        r0 = b * sub + st_i * 128
                        s_x[st_i].inc(
                            g.dma_start(out=xsb(st_i),
                                        in_=x_d.ap()[r0:r0 + 128, :]), 16)

            x_marks = [s_x[i].v for i in range(st_n)]

            for layer in range(2):
                var_base = S("s_var").v
                nb_base = S("s_nb").v
                std_base0 = S("s_std").v

                @block.vector
                def _(v, b=b, layer=layer, x_marks=tuple(x_marks)):
                    v.drain()
                    for st_i in range(st_n):
                        if layer == 0:
                            v.wait_ge(s_x[st_i].h, x_marks[st_i])
                        for cch in range(8):
                            v.bn_stats(stt[:, cch * 6:(cch + 1) * 6],
                                       xsb(st_i)[:, cch * 512:(cch + 1) * 512])
                        v.drain()
                        means = stt[:].rearrange("p (c t) -> p c t", t=3)[:, :, 1]
                        cvs = stt[:].rearrange("p (c t) -> p c t", t=3)[:, :, 2]
                        c0 = st_i * 4
                        v.tensor_reduce(mvar[:, c0:c0 + 1], means, AX.X,
                                        ALU.add, negate=True)
                        v.tensor_reduce(mvar[:, c0 + 1:c0 + 2], cvs, AX.X,
                                        ALU.add)
                        v.tensor_tensor(lsc[:, 0:16], means, means, ALU.mult)
                        v.drain()
                        v.tensor_reduce(lsc[:, 16:17], lsc[:, 0:16], AX.X,
                                        ALU.add)
                        v.tensor_scalar(mvar[:, c0:c0 + 1], mvar[:, c0:c0 + 1],
                                        1.0 / 16, None, ALU.mult)
                        v.tensor_scalar(mvar[:, c0 + 1:c0 + 2],
                                        mvar[:, c0 + 1:c0 + 2], 1.0 / D, None,
                                        ALU.mult)
                        v.drain()
                        v.tensor_scalar(lsc[:, 16:17], lsc[:, 16:17],
                                        1.0 / 16, None, ALU.mult)
                        v.tensor_tensor(lsc[:, 0:1], mvar[:, c0:c0 + 1],
                                        mvar[:, c0:c0 + 1], ALU.mult)
                        v.drain()
                        v.tensor_tensor(mvar[:, c0 + 1:c0 + 2],
                                        mvar[:, c0 + 1:c0 + 2],
                                        lsc[:, 16:17], ALU.add)
                        v.drain()
                        S("s_var").inc(v.tensor_tensor(
                            mvar[:, c0 + 1:c0 + 2], mvar[:, c0 + 1:c0 + 2],
                            lsc[:, 0:1], ALU.subtract))
                        v.wait_ge(S("s_std").h, S("s_std").v + st_i + 1)
                        v.reciprocal(mvar[:, c0 + 2:c0 + 3],
                                     mvar[:, c0 + 1:c0 + 2])
                        v.drain()
                        v.tensor_tensor(mvar[:, c0 + 3:c0 + 4],
                                        mvar[:, c0:c0 + 1],
                                        mvar[:, c0 + 2:c0 + 3], ALU.mult)
                        S("s_nb").inc(v.drain())

                norm_base = S("s_norm").v
                tp_base = S("s_tp").v
                xnt_base = S("s_xnt").v
                std_base = std_base0

                @block.scalar
                def _(s, var_base=var_base, std_base=std_base):
                    for st_i in range(st_n):
                        s.wait_ge(S("s_var").h, var_base + st_i + 1)
                        c0 = st_i * 4
                        S("s_std").inc(
                            s.activation(mvar[:, c0 + 1:c0 + 2],
                                         mvar[:, c0 + 1:c0 + 2], AF.Sqrt,
                                         bias=epsc[:]))

                @block.scalar
                def _(s, b=b, layer=layer, nb_base=nb_base,
                      norm_base=norm_base, tp_base=tp_base,
                      xnt_base=xnt_base):
                    def emit_norm(st_i):
                        s.wait_ge(S("s_nb").h, nb_base + st_i + 1)
                        if st_i >= 2:
                            s.wait_ge(S("s_tp").h, tp_base + (st_i - 1) * 32)
                        c0 = st_i * 4
                        ins = s.activation(xn_st[st_i % 2][:], xsb(st_i),
                                           AF.Identity,
                                           bias=mvar[:, c0 + 3:c0 + 4],
                                           scale=mvar[:, c0 + 2:c0 + 3])
                        S("s_norm").inc(ins)

                    emit_norm(0)
                    if st_n > 1:
                        emit_norm(1)
                    for st_i in range(st_n):
                        for gi in range(8):
                            gidx = st_i * 8 + gi
                            pbank = ps[6 + (gi % 2)]
                            s.wait_ge(S("s_tp").h, tp_base + (gidx + 1) * 4)
                            off = (st_i * KT + gi * 4) * 128
                            S("s_xnt").inc(
                                s.copy(xnT[:, off:off + 512],
                                       pbank[:].bitcast(BF16)[:, 0:512]))
                        if st_i + 2 < st_n:
                            emit_norm(st_i + 2)

                @block.tensor
                def _(t, b=b, layer=layer, norm_base=norm_base,
                      tp_base=tp_base, xnt_base=xnt_base):
                    for st_i in range(st_n):
                        t.wait_ge(S("s_norm").h, norm_base + st_i + 1)
                        for gi in range(8):
                            gidx = st_i * 8 + gi
                            pbank = ps[6 + (gi % 2)]
                            if gidx >= 2:
                                t.wait_ge(S("s_xnt").h, xnt_base + gidx - 1)
                            for j in range(4):
                                kt = gi * 4 + j
                                S("s_tp").inc(t.transpose(
                                    pbank[:].bitcast(BF16)[:, j * 128:(j + 1) * 128],
                                    xn_st[st_i % 2][:, kt * 128:(kt + 1) * 128],
                                    idb[:]))
                S("s_xnt").v = xnt_base + st_n * 8
                xnt_done = S("s_xnt").v

                if layer == 1:
                    g_gt0 = S("s_gt").v

                    @block.tensor
                    def _(t, b=b, xnt_done=xnt_done):
                        t.wait_ge(S("s_xnt").h, xnt_done)
                        if b == 0:
                            t.wait_ge(S("s_vec").h, M["bA"])
                        for kt in range(KT):
                            rhs = xnT[:].rearrange("p (s k c) -> p s k c",
                                                   s=st_n, k=KT)[:, :, kt, :]
                            S("s_gt").inc(t.matmul(
                                ps[0][0:72, 0:sub],
                                SgT[:, kt * 72:(kt + 1) * 72],
                                rhs, start=(kt == 0), stop=(kt == KT - 1)))
                    g_ps = S("s_gt").v

                    @block.scalar
                    def _(s, b=b, g_ps=g_ps):
                        s.wait_ge(S("s_gt").h, g_ps)
                        S("s_gv").inc(
                            s.activation(hterT[:], ps[0][0:64, 0:sub],
                                         AF.Identity, bias=bA_c[0:64, :]))
                        S("s_gv").inc(
                            s.activation(logit[:], ps[0][64:72, 0:sub],
                                         AF.Identity, bias=bA_c[64:72, :]))
                    g_ev = S("s_gv").v

                    lgT_gt0 = S("s_gt").v
                    lgT_v0 = S("s_gv").v

                    @block.tensor
                    def _(t, b=b, g_ev=g_ev):
                        t.wait_ge(S("s_gv").h, g_ev)
                        for st_i in range(st_n):
                            pbank = ps[1] if st_i % 2 == 0 else ps[2]
                            if st_i >= 2:
                                t.wait_ge(S("s_gv").h, lgT_v0 + st_i - 1)
                            S("s_gt").inc(t.transpose(
                                pbank[:, 0:8],
                                logit[:, st_i * 128:(st_i + 1) * 128],
                                idf[0:8, 0:8]))

                    @block.vector
                    def _(v, b=b):
                        for st_i in range(st_n):
                            pbank = ps[1] if st_i % 2 == 0 else ps[2]
                            v.wait_ge(S("s_gt").h, lgT_gt0 + st_i + 1)
                            S("s_gv").inc(v.tensor_copy(
                                lg_tm[:, st_i * 8:(st_i + 1) * 8],
                                pbank[:, 0:8]))
                        v.drain()
                        for st_i in range(st_n):
                            l = lg_tm[:, st_i * 8:(st_i + 1) * 8]
                            wdc = wd_tm[:, st_i * 8:(st_i + 1) * 8]
                            l2 = lg2[:, st_i * 8:(st_i + 1) * 8]
                            m1 = ggs[:, st_i * 4:st_i * 4 + 1]
                            nm1 = ggs[:, st_i * 4 + 1:st_i * 4 + 2]
                            m2 = ggs[:, st_i * 4 + 2:st_i * 4 + 3]
                            v.tensor_reduce(m1, l, AX.X, ALU.max)
                            v.drain()
                            v.tensor_scalar(nm1, m1, -1.0, None, ALU.mult)
                            v.tensor_scalar(wdc, l, m1, None, ALU.is_ge)
                            v.drain()
                            v.scalar_tensor_tensor(l2, wdc, -1e9, l,
                                                   ALU.mult, ALU.add)
                            v.drain()
                            v.tensor_reduce(m2, l2, AX.X, ALU.max)
                            v.drain()
                            v.tensor_scalar(wdc, l, m2, None, ALU.is_ge)
                            S("s_gv").inc(v.drain())
                    keep_base = S("s_gv").v - st_n

                    @block.scalar
                    def _(s, b=b, keep_base=keep_base):
                        for st_i in range(st_n):
                            s.wait_ge(S("s_gv").h, keep_base + st_i + 1)
                            l = lg_tm[:, st_i * 8:(st_i + 1) * 8]
                            nm1 = ggs[:, st_i * 4 + 1:st_i * 4 + 2]
                            S("s_gs").inc(
                                s.activation(l, l, AF.Exp, bias=nm1, scale=1.0))
                    exp_base = S("s_gs").v - st_n

                    @block.vector
                    def _(v, b=b, exp_base=exp_base):
                        for st_i in range(st_n):
                            v.wait_ge(S("s_gs").h, exp_base + st_i + 1)
                            l = lg_tm[:, st_i * 8:(st_i + 1) * 8]
                            wdc = wd_tm[:, st_i * 8:(st_i + 1) * 8]
                            den = ggs[:, st_i * 4 + 3:st_i * 4 + 4]
                            v.tensor_tensor(wdc, wdc, l, ALU.mult)
                            v.drain()
                            v.tensor_reduce(den, wdc, AX.X, ALU.add)
                            v.drain()
                            v.reciprocal(den, den)
                            v.drain()
                            v.tensor_scalar(wdc, wdc, den, None, ALU.mult)
                            S("s_gv").inc(v.drain())
                    wd_base = S("s_gv").v - st_n

                    wdT_gt0 = S("s_gt").v
                    wdT_v0 = S("s_gv").v

                    @block.tensor
                    def _(t, b=b, wd_base=wd_base):
                        for st_i in range(st_n):
                            pbank = ps[1] if st_i % 2 == 0 else ps[2]
                            t.wait_ge(S("s_gv").h, wd_base + st_i + 1)
                            if st_i >= 2:
                                t.wait_ge(S("s_gv").h, wdT_v0 + st_i - 1)
                            S("s_gt").inc(t.transpose(
                                pbank[0:8, 0:128],
                                wd_tm[:, st_i * 8:(st_i + 1) * 8],
                                idf[:]))

                    @block.vector
                    def _(v, b=b):
                        for st_i in range(st_n):
                            pbank = ps[1] if st_i % 2 == 0 else ps[2]
                            v.wait_ge(S("s_gt").h, wdT_gt0 + st_i + 1)
                            S("s_gv").inc(v.tensor_copy(
                                wdT[:, st_i * 128:(st_i + 1) * 128],
                                pbank[0:8, 0:128]))
                    wdT_done = S("s_gv").v

                    @block.tensor
                    def _(t, b=b, wdT_done=wdT_done):
                        t.wait_ge(S("s_gv").h, wdT_done)
                        S("s_gt").inc(t.matmul(ps[3][0:64, 0:sub], ex8[:],
                                               wdT[:], start=True, stop=True))
                    ex_ps = S("s_gt").v

                    @block.vector
                    def _(v, b=b, ex_ps=ex_ps):
                        v.wait_ge(S("s_gt").h, ex_ps)
                        S("s_gv").inc(v.tensor_tensor(hsT[:], hterT[:],
                                                      ps[3][0:64, 0:sub],
                                                      ALU.mult))
                    M[f"hsT_{b}"] = S("s_gv").v

                ag_src = wag_o if layer == 0 else fag_o
                mm_base = S("s_mm").v
                ev_base = S("s_ev").v
                phase_idx = b * 2 + layer

                bl_marks = []

                @block.gpsimd
                def _(g, layer=layer, phase_idx=phase_idx):
                    if phase_idx <= 1:
                        g.wait_ge(S("s_ag").h, M[f"ag_{layer}"])
                    bo = (baga_o if layer == 0 else bagf_o).ap()\
                        .rearrange("(g j) -> g j", g=8)
                    for gh in range(NG):
                        g_gh = phase_idx * NG + gh
                        if g_gh >= 2:
                            g.wait_ge(S("s_mm").h, (g_gh - 1) * st_n)
                        bl_marks.append(s_bl[gh % 2].inc(
                            g.dma_start(out=bbuf[gh % 2][:],
                                        in_=bo[gh:gh + 1, :]), 16))

                wc_marks = []

                @block.sync
                def _(sy, b=b, layer=layer, mm_base=mm_base,
                      phase_idx=phase_idx):
                    if b == 0:
                        sy.wait_ge(S("s_ag").h, M[f"ag_{layer}"])
                    for gh in range(NG):
                        buf = gh % 2
                        if phase_idx == 0 and gh == 1:
                            sy.wait_ge(S("s_bias").h, 64)
                        g_gh = phase_idx * NG + gh
                        if g_gh >= 2:
                            sy.wait_ge(S("s_mm").h, (g_gh - 1) * st_n)
                        wc_marks.append(s_wcp[buf].inc(
                            sy.dma_start(out=wcol[buf][:],
                                         in_=ag_src.ap()[gh * 128:(gh + 1) * 128,
                                                         :]), 16))

                @block.tensor
                def _(t, b=b, layer=layer, ev_base=ev_base,
                      out_base0=S("s_out").v, xnt_done=xnt_done,
                      phase_idx=phase_idx, bl_marks=bl_marks,
                      wc_marks=wc_marks):
                    t.wait_ge(S("s_xnt").h, xnt_done)
                    if layer == 1:
                        t.wait_ge(S("s_gv").h, M[f"hsT_{b}"])
                        if b == 0:
                            t.wait_ge(S("s_vec").h, M["BfT"])
                    if b == 0:
                        t.wait_ge(S("s_ag").h, M[f"ag_{layer}"])
                    for gh in range(NG):
                        buf = gh % 2
                        t.wait_ge(s_wcp[buf].h, wc_marks[gh])
                        if gh >= 2:
                            if layer == 0:
                                t.wait_ge(S("s_ev").h,
                                          ev_base + (gh - 1) * st_n)
                            else:
                                t.wait_ge(S("s_out").h,
                                          out_base0 + (gh - 1) * st_n)
                        bset = 0 if gh % 2 == 0 else 4
                        for kk in range(KT):
                            for st_i in range(st_n):
                                t.matmul(ps[bset + st_i][:],
                                         xnT_blk(st_i, kk),
                                         wcol[buf][:, kk * 512:(kk + 1) * 512],
                                         start=(kk == 0), stop=False)
                        t.wait_ge(s_bl[gh % 2].h, bl_marks[gh])
                        for st_i in range(st_n):
                            if layer == 1:
                                t.matmul(ps[bset + st_i][:],
                                         hsT[:, st_i * 128:(st_i + 1) * 128],
                                         BfT[:, gh * 512:(gh + 1) * 512],
                                         start=False, stop=False)
                            S("s_mm").inc(t.matmul(ps[bset + st_i][:],
                                                   onesb[:], bbuf[gh % 2][:],
                                                   start=False, stop=True))

                @block.vector
                def _(v, b=b, layer=layer, mm_base=mm_base):
                    v.drain()
                    for gh in range(NG):
                        bset = 0 if gh % 2 == 0 else 4
                        for st_i in range(st_n):
                            v.wait_ge(S("s_mm").h,
                                      mm_base + gh * st_n + st_i + 1)
                            pps = ps[bset + st_i]
                            dst = xsb(st_i)[:, gh * 512:(gh + 1) * 512]
                            if layer == 0:
                                S("s_ev").inc(
                                    v.tensor_tensor(dst, pps[:], dst, ALU.add))
                            else:
                                i = oc[0] + gh * st_n + st_i
                                if i >= 2:
                                    v.wait_ge(s_yp[i % 2].h, 16 * (i // 2))
                                ob = ostg[i % 2]
                                ins = v.tensor_tensor(ob[:], pps[:], dst,
                                                      ALU.add)
                                S("s_out").inc(ins)

                if layer == 1:
                    out_base = S("s_out").v - NG * st_n

                    @block.scalar
                    def _(sc_e, b=b, out_base=out_base):
                        for gh in range(NG):
                            for st_i in range(st_n):
                                i = gh * st_n + st_i
                                sc_e.wait_ge(S("s_out").h, out_base + i + 1)
                                r0 = b * sub + st_i * 128
                                gi_o = oc[0] + i
                                ins = sc_e.dma_start(
                                    out=y_d.ap()[r0:r0 + 128,
                                                 gh * 512:(gh + 1) * 512],
                                    in_=ostg[gi_o % 2][:])
                                s_yp[gi_o % 2].inc(ins, 16)
                    oc[0] += NG * st_n

        @block.scalar
        def _(sc_e):
            sc_e.wait_ge(s_yp[0].h, s_yp[0].v)
            sc_e.wait_ge(s_yp[1].h, s_yp[1].v)

    return nc


def _host_inputs(L0_attn_data, L0_attn_weight, L0_attn_mask, L0_ffn_W_base,
                 L0_ffn_gate_W, L0_ffn_A, L0_ffn_B, L0_c, L0_wmod,
                 tok=TOK):
    x = np.ascontiguousarray(np.asarray(L0_attn_data, dtype=np.float32)
                             .reshape(-1, D))
    assert x.shape[0] == NCORES * tok
    wT = np.ascontiguousarray(np.asarray(L0_attn_weight, np.float32).T)
    mT = np.ascontiguousarray(np.asarray(L0_attn_mask, np.int32).T)
    fT = np.ascontiguousarray(np.asarray(L0_ffn_W_base, np.float32).T)
    wmod = np.asarray(L0_wmod, np.float32)
    cvec = np.asarray(L0_c, np.float32).reshape(D)
    gw = np.ascontiguousarray(np.asarray(L0_ffn_gate_W, np.float32))
    Afl = np.ascontiguousarray(np.asarray(L0_ffn_A, np.float32).reshape(ER, D))
    Bf = np.ascontiguousarray(np.asarray(L0_ffn_B, np.float32)
                              .transpose(0, 2, 1).reshape(ER, D))
    identb = np.eye(128, dtype=np.float32).astype(ml_dtypes.bfloat16)
    identf = np.eye(128, dtype=np.float32)
    grpsel = np.zeros((128, 4), np.float32)
    grpsel[np.arange(128), np.arange(128) // 32] = 1.0
    expand4 = np.ascontiguousarray(grpsel.T)
    expand8 = np.zeros((8, 64), np.float32)
    expand8[np.arange(64) // 8, np.arange(64)] = 2.0
    onesb = np.ones((1, 128), np.float32).astype(ml_dtypes.bfloat16)
    onesf = np.ones((1, 128), np.float32)

    in_maps = []
    for c in range(NCORES):
        n0 = c * NSL
        in_maps.append({
            "x": x[c * tok:(c + 1) * tok],
            "wt": np.ascontiguousarray(wT[:, n0:n0 + NSL]),
            "maskt": np.ascontiguousarray(mT[:, n0:n0 + NSL]),
            "wbt": np.ascontiguousarray(fT[:, n0:n0 + NSL]),
            "wmod": np.ascontiguousarray(wmod[c * WMK:(c + 1) * WMK]),
            "csl": np.ascontiguousarray(
                cvec[c * WMK:(c + 1) * WMK].reshape(4, 128)),
            "gwn": gw, "afl": Afl, "bfl": Bf,
            "identb": identb, "identf": identf, "grpsel": grpsel,
            "expand4": expand4, "expand8": expand8, "onesb": onesb,
            "onesf": onesf,
            "osel": np.eye(8, dtype=np.float32)[:, c:c + 1].copy(),
        })
    return in_maps


def kernel(L0_attn_data, L0_attn_weight, L0_attn_mask, L0_ffn_W_base,
           L0_ffn_gate_W, L0_ffn_A, L0_ffn_B, L0_c, L0_wmod):
    in_maps = _host_inputs(L0_attn_data, L0_attn_weight, L0_attn_mask,
                           L0_ffn_W_base, L0_ffn_gate_W, L0_ffn_A, L0_ffn_B,
                           L0_c, L0_wmod)
    nc = build_program()
    res = bass_utils.run_bass_kernel_spmd(nc, in_maps,
                                          core_ids=list(range(NCORES)))
    y = np.concatenate([res.results[c]["y"] for c in range(NCORES)], axis=0)
    return y.reshape(1, SEQ, D).astype(np.float32)


# version 12
# speedup vs baseline: 1.0043x; 1.0043x over previous
import numpy as np
import ml_dtypes
from contextlib import ExitStack

import concourse.bass as bass
import concourse.mybir as mybir
import concourse.bass_utils as bass_utils

F32 = mybir.dt.float32
BF16 = mybir.dt.bfloat16
I32 = mybir.dt.int32
AF = mybir.ActivationFunctionType
ALU = mybir.AluOpType
AX = mybir.AxisListType

D = 4096
NCORES = 8
SEQ = 16384
TOK = SEQ // NCORES
KT = D // 128
NG = 8
NSL = D // NCORES
NMOD = 6 * D
WMK = D // NCORES
ER = 64


class Sem:
    def __init__(self, handle):
        self.h = handle
        self.v = 0

    def inc(self, ins, n=1):
        ins.then_inc(self.h, n)
        self.v += n
        return self.v


def build_program(tok=TOK, nsub=4):
    sub = tok // nsub
    st_n = sub // 128
    nc = bass.Bass("TRN2", target_bir_lowering=False)

    x_d = nc.dram_tensor("x", [tok, D], F32, kind="ExternalInput")
    wT_d = nc.dram_tensor("wt", [D, NSL], F32, kind="ExternalInput")
    mT_d = nc.dram_tensor("maskt", [D, NSL], I32, kind="ExternalInput")
    fT_d = nc.dram_tensor("wbt", [D, NSL], F32, kind="ExternalInput")
    wm_d = nc.dram_tensor("wmod", [WMK, NMOD], F32, kind="ExternalInput")
    cs_d = nc.dram_tensor("csl", [4, 128], F32, kind="ExternalInput")
    gw_d = nc.dram_tensor("gwn", [8, D], F32, kind="ExternalInput")
    A_d = nc.dram_tensor("afl", [ER, D], F32, kind="ExternalInput")
    Bf_d = nc.dram_tensor("bfl", [ER, D], F32, kind="ExternalInput")
    idb_d = nc.dram_tensor("identb", [128, 128], BF16, kind="ExternalInput")
    idf_d = nc.dram_tensor("identf", [128, 128], F32, kind="ExternalInput")
    gsel_d = nc.dram_tensor("grpsel", [128, 4], F32, kind="ExternalInput")
    ex4_d = nc.dram_tensor("expand4", [4, 128], F32, kind="ExternalInput")
    ex8_d = nc.dram_tensor("expand8", [8, 64], F32, kind="ExternalInput")
    onesb_d = nc.dram_tensor("onesb", [1, 128], BF16, kind="ExternalInput")
    onesf_d = nc.dram_tensor("onesf", [1, 128], F32, kind="ExternalInput")
    osel_d = nc.dram_tensor("osel", [8, 1], F32, kind="ExternalInput")
    y_d = nc.dram_tensor("y", [tok, D], F32, kind="ExternalOutput")

    modp_d = nc.dram_tensor("modp", [NMOD], F32)
    modf_d = nc.dram_tensor("modf", [NMOD], F32)
    mproc_d = nc.dram_tensor("mproc", [2 * D], F32)
    wag_i = nc.dram_tensor("wag_i", [128, KT * NSL], BF16)
    wag_o = nc.dram_tensor("wag_o", [8 * 128, KT * NSL], BF16)
    fag_i = nc.dram_tensor("fag_i", [128, KT * NSL], BF16)
    fag_o = nc.dram_tensor("fag_o", [8 * 128, KT * NSL], BF16)
    baga_i = nc.dram_tensor("baga_i", [NSL], F32)
    baga_o = nc.dram_tensor("baga_o", [D], F32)
    bagf_i = nc.dram_tensor("bagf_i", [NSL], F32)
    bagf_o = nc.dram_tensor("bagf_o", [D], F32)

    rg = [list(range(NCORES))]
    M = {}

    with ExitStack() as ctx:
        en = ctx.enter_context

        x_sb = en(nc.sbuf_tensor("x_sb", [128, st_n * D], BF16))
        xnT = en(nc.sbuf_tensor("xnT", [128, st_n * KT * 128], BF16))
        wcol = [en(nc.sbuf_tensor(f"wcol{i}", [128, KT * 512], BF16))
                for i in range(2)]
        xn_st = [en(nc.sbuf_tensor(f"xn{i}", [128, D], BF16)) for i in range(2)]
        idb = en(nc.sbuf_tensor("idb", [128, 128], BF16))
        idf = en(nc.sbuf_tensor("idf", [128, 128], F32))
        onesb = en(nc.sbuf_tensor("onesb_s", [1, 128], BF16))
        onesf = en(nc.sbuf_tensor("onesf_s", [1, 128], F32))
        gsel = en(nc.sbuf_tensor("gsel", [128, 4], F32))
        ex4 = en(nc.sbuf_tensor("ex4", [4, 128], F32))
        ex8 = en(nc.sbuf_tensor("ex8", [8, 64], F32))
        ca1c = en(nc.sbuf_tensor("ca1c", [128, KT], F32))
        cb1c = en(nc.sbuf_tensor("cb1c", [128, KT], F32))
        sapc = en(nc.sbuf_tensor("sapc", [128, KT], BF16))
        sbpc = en(nc.sbuf_tensor("sbpc", [128, KT], BF16))
        ga_b = en(nc.sbuf_tensor("ga_b", [128, 512], F32))
        gb_b = en(nc.sbuf_tensor("gb_b", [128, 512], F32))
        bbuf = [en(nc.sbuf_tensor(f"bbuf{i}", [1, 512], BF16))
                for i in range(2)]
        SgT = en(nc.sbuf_tensor("SgT", [128, KT * 72], BF16))
        bA_c = en(nc.sbuf_tensor("bA_c", [72, 1], F32))
        BfT = en(nc.sbuf_tensor("BfT", [64, D], BF16))
        hterT = en(nc.sbuf_tensor("hterT", [64, sub], BF16))
        logit = en(nc.sbuf_tensor("logit", [8, sub], F32))
        lg_tm = en(nc.sbuf_tensor("lg_tm", [128, st_n * 8], F32))
        wd_tm = en(nc.sbuf_tensor("wd_tm", [128, st_n * 8], F32))
        lg2 = en(nc.sbuf_tensor("lg2", [128, st_n * 8], F32))
        ggs = en(nc.sbuf_tensor("ggs", [128, 4 * st_n], F32))
        wdT = en(nc.sbuf_tensor("wdT", [8, sub], F32))
        hsT = en(nc.sbuf_tensor("hsT", [64, sub], BF16))
        stt = en(nc.sbuf_tensor("stt", [128, 48], F32))
        lsc = en(nc.sbuf_tensor("lsc", [128, 17], F32))
        mvar = en(nc.sbuf_tensor("mvar", [128, 4 * st_n], F32))
        epsc = en(nc.sbuf_tensor("epsc", [128, 1], F32))
        ostg = [en(nc.sbuf_tensor(f"ostg{i}", [128, 512], F32))
                for i in range(2)]

        ps = [en(nc.psum_tensor(f"ps{i}", [128, 512], F32)) for i in range(8)]

        sems = {}
        for name in ["s_init", "s_mw0", "s_mw1", "s_mvec", "s_ar", "s_vec",
                     "s_norm", "s_var", "s_tp", "s_xnt", "s_wc0", "s_wc1",
                     "s_mm", "s_ev", "s_gt", "s_gv", "s_gs", "s_out",
                     "s_y0", "s_y1", "s_x0", "s_x1", "s_x2", "s_x3",
                     "s_mk0", "s_mk1", "s_wt0", "s_wt1", "s_sm",
                     "s_bias", "s_ag", "s_misc", "s_bl0", "s_bl1",
                     "s_std", "s_nb", "s_mp0", "s_mp1", "s_cc", "s_mw2", "s_mw3",
                     "s_sm2", "s_mw4", "s_mw5", "s_mw6", "s_mw7",
                     "s_mp2", "s_mp3", "s_wt2", "s_wt3"]:
            sems[name] = Sem(en(nc.semaphore(name)))
        S = lambda n: sems[n]
        s_x = [S(f"s_x{i}") for i in range(4)]
        s_mw8 = [S(f"s_mw{i}") for i in range(8)]
        s_mk = [S("s_mk0"), S("s_mk1")]
        s_wt = [S("s_wt0"), S("s_wt1"), S("s_wt2"), S("s_wt3")]
        s_wcp = [S("s_wc0"), S("s_wc1")]
        s_bl = [S("s_bl0"), S("s_bl1")]
        s_yp = [S("s_y0"), S("s_y1")]
        s_mp = [S("s_mp0"), S("s_mp1"), S("s_mp2"), S("s_mp3")]
        s_smp = [S("s_sm"), S("s_sm2")]

        block = en(nc.Block())

        csl_s = en(nc.sbuf_tensor("csl_s", [4, 128], F32))
        sT = en(nc.sbuf_tensor("sT", [128, 4], F32))
        modv = en(nc.sbuf_tensor("modv", [32, 6 * 128], F32))
        gbf64 = en(nc.sbuf_tensor("gbf64", [64, D], BF16))
        wtile = en(nc.sbuf_tensor("wtile", [128, 2 * 512], F32))
        mtile = en(nc.sbuf_tensor("mtile", [128, 2 * 512], I32))
        red1 = en(nc.sbuf_tensor("red1", [128, 16], F32))
        osel = en(nc.sbuf_tensor("osel_s", [8, 1], F32))
        xnTf = xnT[:].bitcast(F32)
        smfr = [xnTf[0:64, 0:1024], xnTf[0:64, 1024:2048]]
        brow = ostg[0][0:1, 0:512]
        btmp = ostg[1][0:1, 0:72]
        btmp2 = ostg[1][0:1, 0:512]
        gsl8 = en(nc.sbuf_tensor("gsl8", [8, 2 * 512], F32))
        wmblk = wtile
        bm4 = en(nc.sbuf_tensor("bm4", [4, KT * 16], F32))
        em16 = en(nc.sbuf_tensor("em16", [128, 16], F32))

        @block.gpsimd
        def _(g):
            for dst, src in ((idb, idb_d), (idf, idf_d), (onesb, onesb_d),
                             (onesf, onesf_d), (gsel, gsel_d), (ex4, ex4_d),
                             (ex8, ex8_d), (csl_s, cs_d), (osel, osel_d)):
                S("s_init").inc(g.dma_start(out=dst[:], in_=src.ap()), 16)
            for st_i in range(st_n):
                s_x[st_i].inc(
                    g.dma_start(out=x_sb[:, st_i * D:(st_i + 1) * D],
                                in_=x_d.ap()[st_i * 128:(st_i + 1) * 128, :]), 16)
        M["init"] = S("s_init").v

        @block.scalar
        def _(s):
            s.wait_ge(S("s_init").h, M["init"])
            S("s_misc").inc(s.activation(csl_s[:], csl_s[:], AF.Silu))
        M["silu"] = S("s_misc").v

        @block.tensor
        def _(t):
            t.wait_ge(S("s_misc").h, M["silu"])
            S("s_misc").inc(t.transpose(ps[0][0:128, 0:4], csl_s[:],
                                        idf[0:4, 0:4]))
        M["sT_ps"] = S("s_misc").v

        @block.vector
        def _(v):
            v.memset(epsc[:], 1e-5)
            v.wait_ge(S("s_misc").h, M["sT_ps"])
            S("s_misc").inc(v.tensor_copy(sT[:], ps[0][0:128, 0:4]))
        M["sT"] = S("s_misc").v

        mw_marks = {}
        mvslot = [wcol[0][:].bitcast(F32), wcol[1][:].bitcast(F32)]

        @block.sync
        def _(sy):
            for q in range(12):
                for kt in range(4):
                    if q >= 2:
                        sy.wait_ge(S("s_mvec").h, (q - 2) * 16 + 12 + kt + 1)
                    mw_marks[(q, kt)] = s_mw8[4 * (q % 2) + kt].inc(
                        sy.dma_start(
                            out=mvslot[q % 2][:, kt * 2048:(kt + 1) * 2048],
                            in_=wm_d.ap()[kt * 128:(kt + 1) * 128,
                                          q * 2048:(q + 1) * 2048]), 16)

        @block.tensor
        def _(t):
            t.wait_ge(S("s_misc").h, M["sT"])
            for q in range(12):
                for j in range(4):
                    nt = q * 4 + j
                    pbank = ps[1] if nt % 2 == 0 else ps[2]
                    if nt >= 2:
                        t.wait_ge(S("s_vec").h, S("s_vec").v + nt - 1)
                    for kt in range(4):
                        t.wait_ge(s_mw8[4 * (q % 2) + kt].h, mw_marks[(q, kt)])
                        S("s_mvec").inc(
                            t.matmul(pbank[0:1, :], sT[:, kt:kt + 1],
                                     mvslot[q % 2][:, kt * 2048 + j * 512:
                                                   kt * 2048 + (j + 1) * 512],
                                     start=(kt == 0), stop=(kt == 3)))
        mv_base = S("s_mvec").v - 192

        mp_marks = []
        mps_v0 = S("s_vec").v
        xnTf0 = xnT[:].bitcast(F32)
        mpslot = [xnTf0[0:1, 2048:3072], xnTf0[0:1, 3072:4096]]

        @block.vector
        def _(v):
            for nt in range(48):
                pbank = ps[1] if nt % 2 == 0 else ps[2]
                grp, j = nt // 2, nt % 2
                v.wait_ge(S("s_mvec").h, mv_base + (nt + 1) * 4)
                if grp >= 2 and j == 0:
                    v.wait_ge(s_mp[grp % 2].h, 16 * (grp // 2))
                S("s_vec").inc(v.tensor_copy(
                    mpslot[grp % 2][:, j * 512:(j + 1) * 512], pbank[0:1, :]))
        M["modps"] = S("s_vec").v

        @block.gpsimd
        def _(g):
            for grp in range(24):
                g.wait_ge(S("s_vec").h, mps_v0 + (grp + 1) * 2)
                mp_marks.append(s_mp[grp % 2].inc(
                    g.dma_start(
                        out=modp_d.ap()[bass.ds(grp * 1024, 1024)].unsqueeze(0),
                        in_=mpslot[grp % 2]), 16))
            for i in range(2):
                g.wait_ge(s_mp[i].h, s_mp[i].v)
            S("s_cc").inc(g.collective_compute(
                "AllReduce", ALU.add, replica_groups=rg,
                ins=[modp_d.ap()], outs=[modf_d.ap()]), 1)
            g.wait_ge(S("s_cc").h, S("s_cc").v)
            S("s_ar").inc(
                g.dma_start(out=modv[:].rearrange("j (v p) -> j v p", v=6),
                            in_=modf_d.ap().rearrange("(v j p) -> j v p",
                                                      v=6, p=128)), 16)
        M["modv"] = S("s_ar").v

        def vrow(i):
            return modv[:, i * 128:(i + 1) * 128]

        @block.vector
        def _(v):
            v.wait_ge(S("s_ar").h, M["modv"])
            v.tensor_scalar_add(vrow(1), vrow(1), 1.0)
            v.tensor_scalar_add(vrow(4), vrow(4), 1.0)
            v.drain()
            v.reciprocal(wtile[0:32, 0:128], vrow(1))
            v.drain()
            v.tensor_mul(vrow(0), vrow(0), wtile[0:32, 0:128])
            v.drain()
            v.reciprocal(wtile[0:32, 0:128], vrow(4))
            v.drain()
            v.tensor_mul(vrow(3), vrow(3), wtile[0:32, 0:128])
            v.drain()
            for src in (2, 5):
                gv = vrow(src)
                dd = wtile[0:32, 0:128]
                v.tensor_scalar(dd, gv, 0.0, None, ALU.is_ge)
                v.drain()
                v.tensor_scalar(dd, dd, 2.0, -1.0, ALU.mult, ALU.add)
                v.drain()
                v.tensor_mul(gv, gv, dd)
                v.drain()
                v.tensor_scalar_max(gv, gv, 0.25)
                v.drain()
                v.tensor_mul(gv, gv, dd)
                v.drain()
            S("s_vec").inc(v.drain())
        M["mrows"] = S("s_vec").v

        @block.tensor
        def _(t):
            t.wait_ge(S("s_vec").h, M["mrows"])
            for i, src in enumerate((1, 4, 0, 3)):
                S("s_gt").inc(t.transpose(ps[3 + i][:, 0:32], vrow(src),
                                          idf[0:32, 0:32]))
        M["mcolT"] = S("s_gt").v

        @block.vector
        def _(v):
            v.wait_ge(S("s_gt").h, M["mcolT"])
            S("s_vec").inc(v.tensor_copy(ca1c[:], ps[3][:, 0:32]))
            S("s_vec").inc(v.tensor_copy(cb1c[:], ps[4][:, 0:32]))
            S("s_vec").inc(v.tensor_copy(sapc[:], ps[5][:, 0:32]))
            S("s_vec").inc(v.tensor_copy(sbpc[:], ps[6][:, 0:32]))
        M["mcols"] = S("s_vec").v

        @block.gpsimd
        def _(g):
            g.wait_ge(S("s_vec").h, M["mcols"])
            mp = mproc_d.ap().rearrange("(j p) -> j p", j=64)
            S("s_ar").inc(g.dma_start(out=mp[0:32, :], in_=vrow(2)), 16)
            S("s_ar").inc(g.dma_start(out=mp[32:64, :], in_=vrow(5)), 16)
            g.wait_ge(S("s_ar").h, S("s_ar").v)
            S("s_ar").inc(
                g.dma_start(out=gsl8[:, 0:512],
                            in_=mproc_d.ap().rearrange("(h j) -> h j", h=16)
                            [0:8, :]), 16)
            S("s_ar").inc(
                g.dma_start(out=gsl8[:, 512:1024],
                            in_=mproc_d.ap().rearrange("(h j) -> h j", h=16)
                            [8:16, :]), 16)
            g.wait_ge(S("s_gt").h, S("s_gt").v + 4)
            for i in range(8):
                S("s_ar").inc(
                    g.dma_start(out=btmp2,
                                in_=mproc_d.ap()[bass.ds(D + i * 512, 512)]
                                .unsqueeze(0)), 16)
                g.wait_ge(S("s_gt").h, S("s_gt").v + 5 + i)
        ar_rows0 = S("s_ar").v - 10 * 16

        gt_rows0 = S("s_gt").v
        v_rows0 = S("s_vec").v

        @block.tensor
        def _(t):
            t.wait_ge(S("s_ar").h, ar_rows0 + 32)
            S("s_gt").inc(t.matmul(ps[4][0:1, :], osel[:], gsl8[:, 0:512],
                                   start=True, stop=True))
            S("s_gt").inc(t.matmul(ps[5][0:1, :], osel[:], gsl8[:, 512:1024],
                                   start=True, stop=True))
            t.wait_ge(S("s_vec").h, v_rows0 + 1)
            S("s_gt").inc(t.matmul(ps[0][:], onesf[:], brow,
                                   start=True, stop=True))
            t.wait_ge(S("s_vec").h, v_rows0 + 2)
            S("s_gt").inc(t.matmul(ps[1][:], onesf[:], btmp2,
                                   start=True, stop=True))
            for i in range(8):
                pbank = ps[2] if i % 2 == 0 else ps[3]
                t.wait_ge(S("s_ar").h, ar_rows0 + 32 + (i + 1) * 16)
                if i >= 2:
                    t.wait_ge(S("s_vec").h, v_rows0 + 4 + i - 1)
                S("s_gt").inc(t.matmul(pbank[0:64, :], onesf[0:1, 0:64],
                                       btmp2, start=True, stop=True))

        @block.vector
        def _(v):
            v.drain()
            v.wait_ge(S("s_gt").h, gt_rows0 + 1)
            S("s_vec").inc(v.tensor_copy(brow, ps[4][0:1, :]))
            v.wait_ge(S("s_gt").h, gt_rows0 + 2)
            S("s_vec").inc(v.tensor_copy(btmp2, ps[5][0:1, :]))
            v.wait_ge(S("s_gt").h, gt_rows0 + 3)
            S("s_vec").inc(v.tensor_copy(ga_b[:], ps[0][:]))
            v.wait_ge(S("s_gt").h, gt_rows0 + 4)
            S("s_vec").inc(v.tensor_copy(gb_b[:], ps[1][:]))
            for i in range(8):
                pbank = ps[2] if i % 2 == 0 else ps[3]
                v.wait_ge(S("s_gt").h, gt_rows0 + 5 + i)
                S("s_vec").inc(v.tensor_copy(gbf64[:, i * 512:(i + 1) * 512],
                                             pbank[0:64, :]))
        M["gbf64"] = S("s_vec").v
        M["gbc"] = M["gbf64"]

        mk_marks = []
        mk_v0 = S("s_vec").v

        @block.gpsimd
        def _(g):
            for kt in range(KT):
                buf = (kt % 2) * 512
                if kt >= 2:
                    g.wait_ge(S("s_vec").h, mk_v0 + 2 * (kt - 2) + 1)
                mk_marks.append(s_mk[kt % 2].inc(
                    g.dma_start(out=mtile[:, buf:buf + 512],
                                in_=mT_d.ap()[kt * 128:(kt + 1) * 128, :]),
                    16))
        bm_gt0 = S("s_gt").v

        @block.tensor
        def _(t):
            t.wait_ge(S("s_init").h, M["init"])
            for kt in range(KT):
                t.wait_ge(S("s_vec").h, mk_v0 + 2 * kt + 1)
                pbank = ps[4] if kt % 2 == 0 else ps[5]
                S("s_gt").inc(t.matmul(pbank[0:4, 0:16], gsel[:], red1[:],
                                       start=True, stop=True))

        @block.vector
        def _(v):
            v.drain()
            for kt in range(KT):
                buf = (kt % 2) * 512
                v.wait_ge(s_mk[kt % 2].h, mk_marks[kt])
                S("s_vec").inc(v.tensor_reduce(
                    red1[:],
                    mtile[:, buf:buf + 512].rearrange("p (nb i) -> p nb i", i=32),
                    AX.X, ALU.add))
                pbank = ps[4] if kt % 2 == 0 else ps[5]
                v.wait_ge(S("s_gt").h, bm_gt0 + kt + 1)
                S("s_vec").inc(v.tensor_copy(bm4[:, kt * 16:(kt + 1) * 16],
                                             pbank[0:4, 0:16]))
            v.drain()
            v.tensor_scalar(bm4[:], bm4[:], 0.5, None, ALU.is_ge)
            S("s_vec").inc(v.drain())
        M["bmb"] = S("s_vec").v

        def prep_matrix(mat_idx, src_d, dst_sb, ag_in, c1col, g_bcast, masked):
            gs_base = S("s_gs").v
            wt_marks = []
            wslot = [wtile[:, 0:512], wtile[:, 512:1024],
                     ostg[0][:, 0:512]]

            @block.sync
            def _(sy):
                for kt in range(KT):
                    gidx = gs_base + kt
                    if gidx < 3:
                        sy.wait_ge(S("s_vec").h, M["mrows"])
                    if gidx == 2:
                        sy.wait_ge(S("s_vec").h, M["gbf64"])
                    if gidx == 32:
                        sy.wait_ge(S("s_bias").h, 32)
                    if gidx - 2 > 0:
                        sy.wait_ge(S("s_gs").h, gidx - 2)
                    wt_marks.append(s_wt[gidx % 3].inc(
                        sy.dma_start(out=wslot[gidx % 3],
                                     in_=src_d.ap()[kt * 128:(kt + 1) * 128, :]),
                        16))

            em_gt0 = S("s_gt").v

            if masked:
                @block.tensor
                def _(t):
                    t.wait_ge(S("s_vec").h, M["bmb"])
                    for kt in range(KT):
                        pbank = ps[6] if kt % 2 == 0 else ps[7]
                        if kt >= 2:
                            t.wait_ge(S("s_gs").h, gs_base + kt - 1)
                        S("s_gt").inc(t.matmul(
                            pbank[:, 0:16], ex4[:],
                            bm4[0:4, kt * 16:(kt + 1) * 16],
                            start=True, stop=True))

            @block.vector
            def _(v):
                v.drain()
                v.wait_ge(S("s_vec").h, M["gbf64"])
                for kt in range(KT):
                    gidx = gs_base + kt
                    v.wait_ge(s_wt[gidx % 3].h, wt_marks[kt])
                    dst = dst_sb[:, kt * 512:(kt + 1) * 512]
                    wt = wslot[gidx % 3]
                    if masked:
                        pbank = ps[6] if kt % 2 == 0 else ps[7]
                        v.wait_ge(S("s_gt").h, em_gt0 + kt + 1)
                        v.tensor_copy(em16[:], pbank[:, 0:16])
                        v.drain()
                        v.tensor_tensor(
                            wt.rearrange("p (nb i) -> p nb i", i=32),
                            wt.rearrange("p (nb i) -> p nb i", i=32),
                            em16[:].unsqueeze(2).broadcast_to([128, 16, 32]),
                            ALU.mult)
                        v.drain()
                    v.tensor_scalar(wt, wt, c1col[:, kt:kt + 1], None, ALU.mult)
                    v.drain()
                    S("s_gs").inc(v.tensor_tensor(dst, wt, g_bcast[:], ALU.mult))

            gsv = S("s_gs").v
            sc = sapc if mat_idx == 0 else sbpc

            @block.tensor
            def _(t):
                t.wait_ge(S("s_gs").h, gsv)
                for kt in range(KT):
                    S("s_gt").inc(t.matmul(ps[0][0:1, :], sc[:, kt:kt + 1],
                                           dst_sb[:, kt * 512:(kt + 1) * 512],
                                           start=(kt == 0), stop=(kt == KT - 1)))
            bv = S("s_gt").v

            @block.vector
            def _(v):
                v.wait_ge(S("s_gt").h, bv)
                if mat_idx == 1:
                    v.wait_ge(S("s_bias").h, 32)
                S("s_vec").inc(v.tensor_copy(brow, ps[0][0:1, :]))
            bvv = S("s_vec").v

            bag_in = baga_i if mat_idx == 0 else bagf_i
            bag_out = baga_o if mat_idx == 0 else bagf_o
            ag_out = wag_o if mat_idx == 0 else fag_o

            @block.gpsimd
            def _(g):
                g.wait_ge(S("s_vec").h, bvv)
                S("s_bias").inc(
                    g.dma_start(out=bag_in.ap().unsqueeze(0), in_=brow), 16)
                S("s_bias").inc(g.dma_start(out=ag_in.ap(), in_=dst_sb[:]), 16)
                g.wait_ge(S("s_bias").h, S("s_bias").v)
                S("s_ag").inc(g.collective_compute(
                    "AllGather", ALU.bypass, replica_groups=rg,
                    ins=[ag_in.ap()], outs=[ag_out.ap()]), 1)
                S("s_ag").inc(g.collective_compute(
                    "AllGather", ALU.bypass, replica_groups=rg,
                    ins=[bag_in.ap()], outs=[bag_out.ap()]), 1)
            M[f"ag_{mat_idx}"] = S("s_ag").v

        prep_matrix(0, wT_d, wcol[0], wag_i, ca1c, ga_b, True)
        prep_matrix(1, fT_d, wcol[1], fag_i, cb1c, gb_b, False)
        M["prep_done"] = S("s_bias").v

        M["ag"] = M["ag_1"]

        def emit_phase3():
            mm_after = S("s_mm").v
            ev_after = S("s_ev").v
            a_gt0 = S("s_gt").v
            a_v0 = S("s_vec").v
            a_marks = []

            @block.sync
            def _(sy):
                sy.wait_ge(S("s_mm").h, mm_after)
                for q in range(4):
                    if q >= 2:
                        sy.wait_ge(S("s_gt").h, a_gt0 + (q - 1) * 8)
                    a_marks.append(s_smp[q % 2].inc(
                        sy.dma_start(out=smfr[q % 2],
                                     in_=A_d.ap()[:, q * 1024:(q + 1) * 1024]), 16))

            @block.tensor
            def _(t):
                t.wait_ge(S("s_ev").h, ev_after)
                for q in range(4):
                    t.wait_ge(s_smp[q % 2].h, a_marks[q])
                    for j in range(8):
                        kt = q * 8 + j
                        pbank = ps[1] if kt % 2 == 0 else ps[2]
                        if kt >= 2:
                            t.wait_ge(S("s_vec").h, a_v0 + kt - 1)
                        S("s_gt").inc(t.transpose(
                            pbank[:, 0:64], smfr[q % 2][:, j * 128:(j + 1) * 128],
                            idf[0:64, 0:64]))

            @block.vector
            def _(v):
                v.wait_ge(S("s_vec").h, M["gbf64"])
                for kt in range(KT):
                    pbank = ps[1] if kt % 2 == 0 else ps[2]
                    v.wait_ge(S("s_gt").h, a_gt0 + kt + 1)
                    S("s_vec").inc(v.tensor_scalar(
                        SgT[:, kt * 72:kt * 72 + 64], pbank[:, 0:64],
                        cb1c[:, kt:kt + 1], None, ALU.mult))
            M["A_T"] = S("s_vec").v

            gw_gt0 = S("s_gt").v
            gw_v0 = S("s_vec").v
            gw_marks = []

            @block.sync
            def _(sy):
                for q in range(4):
                    if q < 2:
                        sy.wait_ge(S("s_gt").h, a_gt0 + 32)
                    else:
                        sy.wait_ge(S("s_gt").h, gw_gt0 + (q - 1) * 8)
                    gw_marks.append(s_smp[q % 2].inc(
                        sy.dma_start(out=smfr[q % 2][0:8, :],
                                     in_=gw_d.ap()[:, q * 1024:(q + 1) * 1024]), 16))

            @block.tensor
            def _(t):
                t.wait_ge(S("s_vec").h, gw_v0)
                for q in range(4):
                    t.wait_ge(s_smp[q % 2].h, gw_marks[q])
                    for j in range(8):
                        kt = q * 8 + j
                        pbank = ps[1] if kt % 2 == 0 else ps[2]
                        if kt >= 2:
                            t.wait_ge(S("s_vec").h, gw_v0 + kt - 1)
                        S("s_gt").inc(t.transpose(
                            pbank[:, 0:8], smfr[q % 2][0:8, j * 128:(j + 1) * 128],
                            idf[0:8, 0:8]))

            @block.vector
            def _(v):
                for kt in range(KT):
                    pbank = ps[1] if kt % 2 == 0 else ps[2]
                    v.wait_ge(S("s_gt").h, gw_gt0 + kt + 1)
                    S("s_vec").inc(v.tensor_scalar(
                        SgT[:, kt * 72 + 64:kt * 72 + 72], pbank[:, 0:8],
                        cb1c[:, kt:kt + 1], None, ALU.mult))
            M["Sg"] = S("s_vec").v

            bf_v0 = S("s_vec").v
            bf_marks = []

            @block.sync
            def _(sy):
                for q in range(4):
                    if q < 2:
                        sy.wait_ge(S("s_gt").h, gw_gt0 + 32)
                    else:
                        sy.wait_ge(S("s_vec").h, bf_v0 + q - 1)
                    bf_marks.append(s_smp[q % 2].inc(
                        sy.dma_start(out=smfr[q % 2],
                                     in_=Bf_d.ap()[:, q * 1024:(q + 1) * 1024]), 16))

            @block.vector
            def _(v):
                v.drain()
                for q in range(4):
                    v.wait_ge(s_smp[q % 2].h, bf_marks[q])
                    S("s_vec").inc(v.tensor_tensor(
                        BfT[:, q * 1024:(q + 1) * 1024], smfr[q % 2],
                        gbf64[:, q * 1024:(q + 1) * 1024], ALU.mult))
            M["BfT"] = S("s_vec").v

            bA_gt0 = S("s_gt").v

            @block.tensor
            def _(t):
                t.wait_ge(S("s_vec").h, M["Sg"])
                for kt in range(KT):
                    S("s_gt").inc(t.matmul(ps[3][0:1, 0:72], sbpc[:, kt:kt + 1],
                                           SgT[:, kt * 72:(kt + 1) * 72],
                                           start=(kt == 0), stop=(kt == KT - 1)))
            M["bA_ps"] = S("s_gt").v

            @block.vector
            def _(v):
                v.wait_ge(S("s_gt").h, M["bA_ps"])
                S("s_vec").inc(v.tensor_copy(btmp, ps[3][0:1, 0:72]))
            M["bA_sb"] = S("s_vec").v

            @block.tensor
            def _(t):
                t.wait_ge(S("s_vec").h, M["bA_sb"])
                S("s_gt").inc(t.transpose(ps[4][0:72, 0:1], btmp,
                                          idf[0:1, 0:1]))
            M["bA_T"] = S("s_gt").v

            @block.vector
            def _(v):
                v.wait_ge(S("s_gt").h, M["bA_T"])
                S("s_vec").inc(v.tensor_copy(bA_c[:], ps[4][0:72, 0:1]))
            M["bA"] = S("s_vec").v


        emit_phase3()

        def xsb(st_i):
            return x_sb[:, st_i * D:(st_i + 1) * D]

        def xnT_blk(st_i, kt):
            off = (st_i * KT + kt) * 128
            return xnT[:, off:off + 128]

        oc_total = 2 * NG * st_n * 0 + 0
        oc = [0]

        for b in range(nsub):
            if b > 0:
                ev_need = S("s_out").v

                @block.gpsimd
                def _(g, b=b, ev_need=ev_need):
                    g.wait_ge(S("s_out").h, ev_need)
                    for st_i in range(st_n):
                        r0 = b * sub + st_i * 128
                        s_x[st_i].inc(
                            g.dma_start(out=xsb(st_i),
                                        in_=x_d.ap()[r0:r0 + 128, :]), 16)

            x_marks = [s_x[i].v for i in range(st_n)]

            for layer in range(2):
                var_base = S("s_var").v
                nb_base = S("s_nb").v
                std_base0 = S("s_std").v

                @block.vector
                def _(v, b=b, layer=layer, x_marks=tuple(x_marks)):
                    v.drain()
                    for st_i in range(st_n):
                        if layer == 0:
                            v.wait_ge(s_x[st_i].h, x_marks[st_i])
                        for cch in range(8):
                            v.bn_stats(stt[:, cch * 6:(cch + 1) * 6],
                                       xsb(st_i)[:, cch * 512:(cch + 1) * 512])
                        v.drain()
                        means = stt[:].rearrange("p (c t) -> p c t", t=3)[:, :, 1]
                        cvs = stt[:].rearrange("p (c t) -> p c t", t=3)[:, :, 2]
                        c0 = st_i * 4
                        v.tensor_reduce(mvar[:, c0:c0 + 1], means, AX.X,
                                        ALU.add, negate=True)
                        v.tensor_reduce(mvar[:, c0 + 1:c0 + 2], cvs, AX.X,
                                        ALU.add)
                        v.tensor_tensor(lsc[:, 0:16], means, means, ALU.mult)
                        v.drain()
                        v.tensor_reduce(lsc[:, 16:17], lsc[:, 0:16], AX.X,
                                        ALU.add)
                        v.tensor_scalar(mvar[:, c0:c0 + 1], mvar[:, c0:c0 + 1],
                                        1.0 / 16, None, ALU.mult)
                        v.tensor_scalar(mvar[:, c0 + 1:c0 + 2],
                                        mvar[:, c0 + 1:c0 + 2], 1.0 / D, None,
                                        ALU.mult)
                        v.drain()
                        v.tensor_scalar(lsc[:, 16:17], lsc[:, 16:17],
                                        1.0 / 16, None, ALU.mult)
                        v.tensor_tensor(lsc[:, 0:1], mvar[:, c0:c0 + 1],
                                        mvar[:, c0:c0 + 1], ALU.mult)
                        v.drain()
                        v.tensor_tensor(mvar[:, c0 + 1:c0 + 2],
                                        mvar[:, c0 + 1:c0 + 2],
                                        lsc[:, 16:17], ALU.add)
                        v.drain()
                        S("s_var").inc(v.tensor_tensor(
                            mvar[:, c0 + 1:c0 + 2], mvar[:, c0 + 1:c0 + 2],
                            lsc[:, 0:1], ALU.subtract))
                        v.wait_ge(S("s_std").h, S("s_std").v + st_i + 1)
                        v.reciprocal(mvar[:, c0 + 2:c0 + 3],
                                     mvar[:, c0 + 1:c0 + 2])
                        v.drain()
                        v.tensor_tensor(mvar[:, c0 + 3:c0 + 4],
                                        mvar[:, c0:c0 + 1],
                                        mvar[:, c0 + 2:c0 + 3], ALU.mult)
                        S("s_nb").inc(v.drain())

                norm_base = S("s_norm").v
                tp_base = S("s_tp").v
                xnt_base = S("s_xnt").v
                std_base = std_base0

                @block.scalar
                def _(s, var_base=var_base, std_base=std_base):
                    for st_i in range(st_n):
                        s.wait_ge(S("s_var").h, var_base + st_i + 1)
                        c0 = st_i * 4
                        S("s_std").inc(
                            s.activation(mvar[:, c0 + 1:c0 + 2],
                                         mvar[:, c0 + 1:c0 + 2], AF.Sqrt,
                                         bias=epsc[:]))

                @block.scalar
                def _(s, b=b, layer=layer, nb_base=nb_base,
                      norm_base=norm_base, tp_base=tp_base,
                      xnt_base=xnt_base):
                    def emit_norm(st_i):
                        s.wait_ge(S("s_nb").h, nb_base + st_i + 1)
                        if st_i >= 2:
                            s.wait_ge(S("s_tp").h, tp_base + (st_i - 1) * 32)
                        c0 = st_i * 4
                        ins = s.activation(xn_st[st_i % 2][:], xsb(st_i),
                                           AF.Identity,
                                           bias=mvar[:, c0 + 3:c0 + 4],
                                           scale=mvar[:, c0 + 2:c0 + 3])
                        S("s_norm").inc(ins)

                    emit_norm(0)
                    if st_n > 1:
                        emit_norm(1)
                    for st_i in range(st_n):
                        for gi in range(8):
                            gidx = st_i * 8 + gi
                            pbank = ps[6 + (gi % 2)]
                            s.wait_ge(S("s_tp").h, tp_base + (gidx + 1) * 4)
                            off = (st_i * KT + gi * 4) * 128
                            S("s_xnt").inc(
                                s.copy(xnT[:, off:off + 512],
                                       pbank[:].bitcast(BF16)[:, 0:512]))
                        if st_i + 2 < st_n:
                            emit_norm(st_i + 2)

                @block.tensor
                def _(t, b=b, layer=layer, norm_base=norm_base,
                      tp_base=tp_base, xnt_base=xnt_base):
                    for st_i in range(st_n):
                        t.wait_ge(S("s_norm").h, norm_base + st_i + 1)
                        for gi in range(8):
                            gidx = st_i * 8 + gi
                            pbank = ps[6 + (gi % 2)]
                            if gidx >= 2:
                                t.wait_ge(S("s_xnt").h, xnt_base + gidx - 1)
                            for j in range(4):
                                kt = gi * 4 + j
                                S("s_tp").inc(t.transpose(
                                    pbank[:].bitcast(BF16)[:, j * 128:(j + 1) * 128],
                                    xn_st[st_i % 2][:, kt * 128:(kt + 1) * 128],
                                    idb[:]))
                S("s_xnt").v = xnt_base + st_n * 8
                xnt_done = S("s_xnt").v

                if layer == 1:
                    g_gt0 = S("s_gt").v

                    @block.tensor
                    def _(t, b=b, xnt_done=xnt_done):
                        t.wait_ge(S("s_xnt").h, xnt_done)
                        if b == 0:
                            t.wait_ge(S("s_vec").h, M["bA"])
                        for kt in range(KT):
                            rhs = xnT[:].rearrange("p (s k c) -> p s k c",
                                                   s=st_n, k=KT)[:, :, kt, :]
                            S("s_gt").inc(t.matmul(
                                ps[0][0:72, 0:sub],
                                SgT[:, kt * 72:(kt + 1) * 72],
                                rhs, start=(kt == 0), stop=(kt == KT - 1)))
                    g_ps = S("s_gt").v

                    @block.scalar
                    def _(s, b=b, g_ps=g_ps):
                        s.wait_ge(S("s_gt").h, g_ps)
                        S("s_gv").inc(
                            s.activation(hterT[:], ps[0][0:64, 0:sub],
                                         AF.Identity, bias=bA_c[0:64, :]))
                        S("s_gv").inc(
                            s.activation(logit[:], ps[0][64:72, 0:sub],
                                         AF.Identity, bias=bA_c[64:72, :]))
                    g_ev = S("s_gv").v

                    lgT_gt0 = S("s_gt").v
                    lgT_v0 = S("s_gv").v

                    @block.tensor
                    def _(t, b=b, g_ev=g_ev):
                        t.wait_ge(S("s_gv").h, g_ev)
                        for st_i in range(st_n):
                            pbank = ps[1] if st_i % 2 == 0 else ps[2]
                            if st_i >= 2:
                                t.wait_ge(S("s_gv").h, lgT_v0 + st_i - 1)
                            S("s_gt").inc(t.transpose(
                                pbank[:, 0:8],
                                logit[:, st_i * 128:(st_i + 1) * 128],
                                idf[0:8, 0:8]))

                    @block.vector
                    def _(v, b=b):
                        for st_i in range(st_n):
                            pbank = ps[1] if st_i % 2 == 0 else ps[2]
                            v.wait_ge(S("s_gt").h, lgT_gt0 + st_i + 1)
                            S("s_gv").inc(v.tensor_copy(
                                lg_tm[:, st_i * 8:(st_i + 1) * 8],
                                pbank[:, 0:8]))
                        v.drain()
                        for st_i in range(st_n):
                            l = lg_tm[:, st_i * 8:(st_i + 1) * 8]
                            wdc = wd_tm[:, st_i * 8:(st_i + 1) * 8]
                            l2 = lg2[:, st_i * 8:(st_i + 1) * 8]
                            m1 = ggs[:, st_i * 4:st_i * 4 + 1]
                            nm1 = ggs[:, st_i * 4 + 1:st_i * 4 + 2]
                            m2 = ggs[:, st_i * 4 + 2:st_i * 4 + 3]
                            v.tensor_reduce(m1, l, AX.X, ALU.max)
                            v.drain()
                            v.tensor_scalar(nm1, m1, -1.0, None, ALU.mult)
                            v.tensor_scalar(wdc, l, m1, None, ALU.is_ge)
                            v.drain()
                            v.scalar_tensor_tensor(l2, wdc, -1e9, l,
                                                   ALU.mult, ALU.add)
                            v.drain()
                            v.tensor_reduce(m2, l2, AX.X, ALU.max)
                            v.drain()
                            v.tensor_scalar(wdc, l, m2, None, ALU.is_ge)
                            S("s_gv").inc(v.drain())
                    keep_base = S("s_gv").v - st_n

                    @block.scalar
                    def _(s, b=b, keep_base=keep_base):
                        for st_i in range(st_n):
                            s.wait_ge(S("s_gv").h, keep_base + st_i + 1)
                            l = lg_tm[:, st_i * 8:(st_i + 1) * 8]
                            nm1 = ggs[:, st_i * 4 + 1:st_i * 4 + 2]
                            S("s_gs").inc(
                                s.activation(l, l, AF.Exp, bias=nm1, scale=1.0))
                    exp_base = S("s_gs").v - st_n

                    @block.vector
                    def _(v, b=b, exp_base=exp_base):
                        for st_i in range(st_n):
                            v.wait_ge(S("s_gs").h, exp_base + st_i + 1)
                            l = lg_tm[:, st_i * 8:(st_i + 1) * 8]
                            wdc = wd_tm[:, st_i * 8:(st_i + 1) * 8]
                            den = ggs[:, st_i * 4 + 3:st_i * 4 + 4]
                            v.tensor_tensor(wdc, wdc, l, ALU.mult)
                            v.drain()
                            v.tensor_reduce(den, wdc, AX.X, ALU.add)
                            v.drain()
                            v.reciprocal(den, den)
                            v.drain()
                            v.tensor_scalar(wdc, wdc, den, None, ALU.mult)
                            S("s_gv").inc(v.drain())
                    wd_base = S("s_gv").v - st_n

                    wdT_gt0 = S("s_gt").v
                    wdT_v0 = S("s_gv").v

                    @block.tensor
                    def _(t, b=b, wd_base=wd_base):
                        for st_i in range(st_n):
                            pbank = ps[1] if st_i % 2 == 0 else ps[2]
                            t.wait_ge(S("s_gv").h, wd_base + st_i + 1)
                            if st_i >= 2:
                                t.wait_ge(S("s_gv").h, wdT_v0 + st_i - 1)
                            S("s_gt").inc(t.transpose(
                                pbank[0:8, 0:128],
                                wd_tm[:, st_i * 8:(st_i + 1) * 8],
                                idf[:]))

                    @block.vector
                    def _(v, b=b):
                        for st_i in range(st_n):
                            pbank = ps[1] if st_i % 2 == 0 else ps[2]
                            v.wait_ge(S("s_gt").h, wdT_gt0 + st_i + 1)
                            S("s_gv").inc(v.tensor_copy(
                                wdT[:, st_i * 128:(st_i + 1) * 128],
                                pbank[0:8, 0:128]))
                    wdT_done = S("s_gv").v

                    @block.tensor
                    def _(t, b=b, wdT_done=wdT_done):
                        t.wait_ge(S("s_gv").h, wdT_done)
                        S("s_gt").inc(t.matmul(ps[3][0:64, 0:sub], ex8[:],
                                               wdT[:], start=True, stop=True))
                    ex_ps = S("s_gt").v

                    @block.vector
                    def _(v, b=b, ex_ps=ex_ps):
                        v.wait_ge(S("s_gt").h, ex_ps)
                        S("s_gv").inc(v.tensor_tensor(hsT[:], hterT[:],
                                                      ps[3][0:64, 0:sub],
                                                      ALU.mult))
                    M[f"hsT_{b}"] = S("s_gv").v

                ag_src = wag_o if layer == 0 else fag_o
                mm_base = S("s_mm").v
                ev_base = S("s_ev").v
                phase_idx = b * 2 + layer

                bl_marks = []
                out_base_g = S("s_out").v

                @block.gpsimd
                def _(g, b=b, layer=layer, phase_idx=phase_idx,
                      out_base_g=out_base_g):
                    if phase_idx <= 1:
                        g.wait_ge(S("s_ag").h, M[f"ag_{layer}"])
                    bo = (baga_o if layer == 0 else bagf_o).ap()\
                        .rearrange("(g j) -> g j", g=8)

                    def emit_outs(gh_o):
                        for st_i in range(st_n):
                            i = gh_o * st_n + st_i
                            g.wait_ge(S("s_out").h, out_base_g + i + 1)
                            r0 = b * sub + st_i * 128
                            gi_o = oc[0] + i
                            ins = g.dma_start(
                                out=y_d.ap()[r0:r0 + 128,
                                             gh_o * 512:(gh_o + 1) * 512],
                                in_=ostg[gi_o % 2][:])
                            s_yp[gi_o % 2].inc(ins, 16)

                    for gh in range(NG):
                        g_gh = phase_idx * NG + gh
                        if g_gh >= 2:
                            g.wait_ge(S("s_mm").h, (g_gh - 1) * st_n)
                        bl_marks.append(s_bl[gh % 2].inc(
                            g.dma_start(out=bbuf[gh % 2][:],
                                        in_=bo[gh:gh + 1, :]), 16))
                        if layer == 1 and gh >= 1:
                            emit_outs(gh - 1)
                    if layer == 1:
                        emit_outs(NG - 1)

                wc_marks = []

                @block.sync
                def _(sy, b=b, layer=layer, mm_base=mm_base,
                      phase_idx=phase_idx):
                    if b == 0:
                        sy.wait_ge(S("s_ag").h, M[f"ag_{layer}"])
                    for gh in range(NG):
                        buf = gh % 2
                        if phase_idx == 0 and gh == 1:
                            sy.wait_ge(S("s_bias").h, 64)
                        g_gh = phase_idx * NG + gh
                        if g_gh >= 2:
                            sy.wait_ge(S("s_mm").h, (g_gh - 1) * st_n)
                        wc_marks.append(s_wcp[buf].inc(
                            sy.dma_start(out=wcol[buf][:],
                                         in_=ag_src.ap()[gh * 128:(gh + 1) * 128,
                                                         :]), 16))

                @block.tensor
                def _(t, b=b, layer=layer, ev_base=ev_base,
                      out_base0=S("s_out").v, xnt_done=xnt_done,
                      phase_idx=phase_idx, bl_marks=bl_marks,
                      wc_marks=wc_marks):
                    t.wait_ge(S("s_xnt").h, xnt_done)
                    if layer == 1:
                        t.wait_ge(S("s_gv").h, M[f"hsT_{b}"])
                        if b == 0:
                            t.wait_ge(S("s_vec").h, M["BfT"])
                    if b == 0:
                        t.wait_ge(S("s_ag").h, M[f"ag_{layer}"])
                    for gh in range(NG):
                        buf = gh % 2
                        t.wait_ge(s_wcp[buf].h, wc_marks[gh])
                        if gh >= 2:
                            if layer == 0:
                                t.wait_ge(S("s_ev").h,
                                          ev_base + (gh - 1) * st_n)
                            else:
                                t.wait_ge(S("s_out").h,
                                          out_base0 + (gh - 1) * st_n)
                        bset = 0 if gh % 2 == 0 else 4
                        for kk in range(KT):
                            for st_i in range(st_n):
                                t.matmul(ps[bset + st_i][:],
                                         xnT_blk(st_i, kk),
                                         wcol[buf][:, kk * 512:(kk + 1) * 512],
                                         start=(kk == 0), stop=False)
                        t.wait_ge(s_bl[gh % 2].h, bl_marks[gh])
                        for st_i in range(st_n):
                            if layer == 1:
                                t.matmul(ps[bset + st_i][:],
                                         hsT[:, st_i * 128:(st_i + 1) * 128],
                                         BfT[:, gh * 512:(gh + 1) * 512],
                                         start=False, stop=False)
                            S("s_mm").inc(t.matmul(ps[bset + st_i][:],
                                                   onesb[:], bbuf[gh % 2][:],
                                                   start=False, stop=True))

                @block.vector
                def _(v, b=b, layer=layer, mm_base=mm_base):
                    v.drain()
                    for gh in range(NG):
                        bset = 0 if gh % 2 == 0 else 4
                        for st_i in range(st_n):
                            v.wait_ge(S("s_mm").h,
                                      mm_base + gh * st_n + st_i + 1)
                            pps = ps[bset + st_i]
                            dst = xsb(st_i)[:, gh * 512:(gh + 1) * 512]
                            if layer == 0:
                                S("s_ev").inc(
                                    v.tensor_tensor(dst, pps[:], dst, ALU.add))
                            else:
                                i = oc[0] + gh * st_n + st_i
                                if i >= 2:
                                    v.wait_ge(s_yp[i % 2].h, 16 * (i // 2))
                                ob = ostg[i % 2]
                                ins = v.tensor_tensor(ob[:], pps[:], dst,
                                                      ALU.add)
                                S("s_out").inc(ins)

                if layer == 1:
                    oc[0] += NG * st_n

        @block.gpsimd
        def _(g):
            g.wait_ge(s_yp[0].h, s_yp[0].v)
            g.wait_ge(s_yp[1].h, s_yp[1].v)

    return nc


def _host_inputs(L0_attn_data, L0_attn_weight, L0_attn_mask, L0_ffn_W_base,
                 L0_ffn_gate_W, L0_ffn_A, L0_ffn_B, L0_c, L0_wmod,
                 tok=TOK):
    x = np.ascontiguousarray(np.asarray(L0_attn_data, dtype=np.float32)
                             .reshape(-1, D))
    assert x.shape[0] == NCORES * tok
    wT = np.ascontiguousarray(np.asarray(L0_attn_weight, np.float32).T)
    mT = np.ascontiguousarray(np.asarray(L0_attn_mask, np.int32).T)
    fT = np.ascontiguousarray(np.asarray(L0_ffn_W_base, np.float32).T)
    wmod = np.asarray(L0_wmod, np.float32)
    cvec = np.asarray(L0_c, np.float32).reshape(D)
    gw = np.ascontiguousarray(np.asarray(L0_ffn_gate_W, np.float32))
    Afl = np.ascontiguousarray(np.asarray(L0_ffn_A, np.float32).reshape(ER, D))
    Bf = np.ascontiguousarray(np.asarray(L0_ffn_B, np.float32)
                              .transpose(0, 2, 1).reshape(ER, D))
    identb = np.eye(128, dtype=np.float32).astype(ml_dtypes.bfloat16)
    identf = np.eye(128, dtype=np.float32)
    grpsel = np.zeros((128, 4), np.float32)
    grpsel[np.arange(128), np.arange(128) // 32] = 1.0
    expand4 = np.ascontiguousarray(grpsel.T)
    expand8 = np.zeros((8, 64), np.float32)
    expand8[np.arange(64) // 8, np.arange(64)] = 2.0
    onesb = np.ones((1, 128), np.float32).astype(ml_dtypes.bfloat16)
    onesf = np.ones((1, 128), np.float32)

    in_maps = []
    for c in range(NCORES):
        n0 = c * NSL
        in_maps.append({
            "x": x[c * tok:(c + 1) * tok],
            "wt": np.ascontiguousarray(wT[:, n0:n0 + NSL]),
            "maskt": np.ascontiguousarray(mT[:, n0:n0 + NSL]),
            "wbt": np.ascontiguousarray(fT[:, n0:n0 + NSL]),
            "wmod": np.ascontiguousarray(wmod[c * WMK:(c + 1) * WMK]),
            "csl": np.ascontiguousarray(
                cvec[c * WMK:(c + 1) * WMK].reshape(4, 128)),
            "gwn": gw, "afl": Afl, "bfl": Bf,
            "identb": identb, "identf": identf, "grpsel": grpsel,
            "expand4": expand4, "expand8": expand8, "onesb": onesb,
            "onesf": onesf,
            "osel": np.eye(8, dtype=np.float32)[:, c:c + 1].copy(),
        })
    return in_maps


def kernel(L0_attn_data, L0_attn_weight, L0_attn_mask, L0_ffn_W_base,
           L0_ffn_gate_W, L0_ffn_A, L0_ffn_B, L0_c, L0_wmod):
    in_maps = _host_inputs(L0_attn_data, L0_attn_weight, L0_attn_mask,
                           L0_ffn_W_base, L0_ffn_gate_W, L0_ffn_A, L0_ffn_B,
                           L0_c, L0_wmod)
    nc = build_program()
    res = bass_utils.run_bass_kernel_spmd(nc, in_maps,
                                          core_ids=list(range(NCORES)))
    y = np.concatenate([res.results[c]["y"] for c in range(NCORES)], axis=0)
    return y.reshape(1, SEQ, D).astype(np.float32)


# version 13
# speedup vs baseline: 1.0148x; 1.0104x over previous
import numpy as np
import ml_dtypes
from contextlib import ExitStack

import concourse.bass as bass
import concourse.mybir as mybir
import concourse.bass_utils as bass_utils

F32 = mybir.dt.float32
BF16 = mybir.dt.bfloat16
I32 = mybir.dt.int32
AF = mybir.ActivationFunctionType
ALU = mybir.AluOpType
AX = mybir.AxisListType

D = 4096
NCORES = 8
SEQ = 16384
TOK = SEQ // NCORES
KT = D // 128
NG = 8
NSL = D // NCORES
NMOD = 6 * D
WMK = D // NCORES
ER = 64


class Sem:
    def __init__(self, handle):
        self.h = handle
        self.v = 0

    def inc(self, ins, n=1):
        ins.then_inc(self.h, n)
        self.v += n
        return self.v


def build_program(tok=TOK, nsub=4):
    sub = tok // nsub
    st_n = sub // 128
    nc = bass.Bass("TRN2", target_bir_lowering=False)

    x_d = nc.dram_tensor("x", [tok, D], F32, kind="ExternalInput")
    wT_d = nc.dram_tensor("wt", [D, NSL], F32, kind="ExternalInput")
    mT_d = nc.dram_tensor("maskt", [D, NSL], I32, kind="ExternalInput")
    fT_d = nc.dram_tensor("wbt", [D, NSL], F32, kind="ExternalInput")
    wm_d = nc.dram_tensor("wmod", [WMK, NMOD], F32, kind="ExternalInput")
    cs_d = nc.dram_tensor("csl", [4, 128], F32, kind="ExternalInput")
    gw_d = nc.dram_tensor("gwn", [8, D], F32, kind="ExternalInput")
    A_d = nc.dram_tensor("afl", [ER, D], F32, kind="ExternalInput")
    Bf_d = nc.dram_tensor("bfl", [ER, D], F32, kind="ExternalInput")
    idb_d = nc.dram_tensor("identb", [128, 128], BF16, kind="ExternalInput")
    idf_d = nc.dram_tensor("identf", [128, 128], F32, kind="ExternalInput")
    gsel_d = nc.dram_tensor("grpsel", [128, 4], F32, kind="ExternalInput")
    ex4_d = nc.dram_tensor("expand4", [4, 128], F32, kind="ExternalInput")
    ex8_d = nc.dram_tensor("expand8", [8, 64], F32, kind="ExternalInput")
    onesb_d = nc.dram_tensor("onesb", [1, 128], BF16, kind="ExternalInput")
    onesf_d = nc.dram_tensor("onesf", [1, 128], F32, kind="ExternalInput")
    osel_d = nc.dram_tensor("osel", [8, 1], F32, kind="ExternalInput")
    y_d = nc.dram_tensor("y", [tok, D], F32, kind="ExternalOutput")

    modp_d = nc.dram_tensor("modp", [NMOD], F32)
    modf_d = nc.dram_tensor("modf", [NMOD], F32)
    mproc_d = nc.dram_tensor("mproc", [2 * D], F32)
    QW = KT * NSL // 4
    wagq_i = [nc.dram_tensor(f"wagq_i{q}", [128, QW], BF16) for q in range(4)]
    wagq_o = [nc.dram_tensor(f"wagq_o{q}", [8 * 128, QW], BF16)
              for q in range(4)]
    fagq_i = [nc.dram_tensor(f"fagq_i{q}", [128, QW], BF16) for q in range(4)]
    fagq_o = [nc.dram_tensor(f"fagq_o{q}", [8 * 128, QW], BF16)
              for q in range(4)]
    baga_i = nc.dram_tensor("baga_i", [NSL], F32)
    baga_o = nc.dram_tensor("baga_o", [D], F32)
    bagf_i = nc.dram_tensor("bagf_i", [NSL], F32)
    bagf_o = nc.dram_tensor("bagf_o", [D], F32)

    rg = [list(range(NCORES))]
    M = {}

    with ExitStack() as ctx:
        en = ctx.enter_context

        x_sb = en(nc.sbuf_tensor("x_sb", [128, st_n * D], BF16))
        xnT = en(nc.sbuf_tensor("xnT", [128, st_n * KT * 128], BF16))
        wcol = [en(nc.sbuf_tensor(f"wcol{i}", [128, KT * 512], BF16))
                for i in range(2)]
        xn_st = [en(nc.sbuf_tensor(f"xn{i}", [128, D], BF16)) for i in range(2)]
        idb = en(nc.sbuf_tensor("idb", [128, 128], BF16))
        idf = en(nc.sbuf_tensor("idf", [128, 128], F32))
        onesb = en(nc.sbuf_tensor("onesb_s", [1, 128], BF16))
        onesf = en(nc.sbuf_tensor("onesf_s", [1, 128], F32))
        gsel = en(nc.sbuf_tensor("gsel", [128, 4], F32))
        ex4 = en(nc.sbuf_tensor("ex4", [4, 128], F32))
        ex8 = en(nc.sbuf_tensor("ex8", [8, 64], F32))
        ca1c = en(nc.sbuf_tensor("ca1c", [128, KT], F32))
        cb1c = en(nc.sbuf_tensor("cb1c", [128, KT], F32))
        sapc = en(nc.sbuf_tensor("sapc", [128, KT], BF16))
        sbpc = en(nc.sbuf_tensor("sbpc", [128, KT], BF16))
        ga_b = en(nc.sbuf_tensor("ga_b", [128, 512], F32))
        gb_b = en(nc.sbuf_tensor("gb_b", [128, 512], F32))
        bbuf = [en(nc.sbuf_tensor(f"bbuf{i}", [1, 512], BF16))
                for i in range(2)]
        SgT = en(nc.sbuf_tensor("SgT", [128, KT * 72], BF16))
        bA_c = en(nc.sbuf_tensor("bA_c", [72, 1], F32))
        BfT = en(nc.sbuf_tensor("BfT", [64, D], BF16))
        hterT = en(nc.sbuf_tensor("hterT", [64, sub], BF16))
        logit = en(nc.sbuf_tensor("logit", [8, sub], F32))
        lg_tm = en(nc.sbuf_tensor("lg_tm", [128, st_n * 8], F32))
        wd_tm = en(nc.sbuf_tensor("wd_tm", [128, st_n * 8], F32))
        lg2 = en(nc.sbuf_tensor("lg2", [128, st_n * 8], F32))
        ggs = en(nc.sbuf_tensor("ggs", [128, 4 * st_n], F32))
        wdT = en(nc.sbuf_tensor("wdT", [8, sub], F32))
        hsT = en(nc.sbuf_tensor("hsT", [64, sub], BF16))
        stt = en(nc.sbuf_tensor("stt", [128, 48], F32))
        stt2 = en(nc.sbuf_tensor("stt2", [128, st_n * 48], F32))
        lsc = en(nc.sbuf_tensor("lsc", [128, 17], F32))
        mvar = en(nc.sbuf_tensor("mvar", [128, 4 * st_n], F32))
        epsc = en(nc.sbuf_tensor("epsc", [128, 1], F32))
        ostg = [en(nc.sbuf_tensor(f"ostg{i}", [128, 512], F32))
                for i in range(2)]

        ps = [en(nc.psum_tensor(f"ps{i}", [128, 512], F32)) for i in range(8)]

        sems = {}
        for name in ["s_init", "s_mw0", "s_mw1", "s_mvec", "s_ar", "s_vec",
                     "s_norm", "s_var", "s_tp", "s_xnt", "s_wc0", "s_wc1",
                     "s_mm", "s_ev", "s_gt", "s_gv", "s_gs", "s_out",
                     "s_y0", "s_y1", "s_x0", "s_x1", "s_x2", "s_x3",
                     "s_mk0", "s_mk1", "s_wt0", "s_wt1", "s_sm",
                     "s_bias", "s_ag", "s_misc", "s_bl0", "s_bl1",
                     "s_std", "s_nb", "s_mp0", "s_mp1", "s_cc", "s_mw2", "s_mw3",
                     "s_sm2", "s_mw4", "s_mw5", "s_mw6", "s_mw7",
                     "s_mp2", "s_mp3", "s_wt2", "s_wt3"]:
            sems[name] = Sem(en(nc.semaphore(name)))
        S = lambda n: sems[n]
        s_x = [S(f"s_x{i}") for i in range(4)]
        s_mw8 = [S(f"s_mw{i}") for i in range(8)]
        s_mk = [S("s_mk0"), S("s_mk1")]
        s_wt = [S("s_wt0"), S("s_wt1"), S("s_wt2"), S("s_wt3")]
        s_wcp = [S("s_wc0"), S("s_wc1")]
        s_bl = [S("s_bl0"), S("s_bl1")]
        s_yp = [S("s_y0"), S("s_y1")]
        s_mp = [S("s_mp0"), S("s_mp1"), S("s_mp2"), S("s_mp3")]
        s_smp = [S("s_sm"), S("s_sm2")]

        block = en(nc.Block())

        csl_s = en(nc.sbuf_tensor("csl_s", [4, 128], F32))
        sT = en(nc.sbuf_tensor("sT", [128, 4], F32))
        modv = en(nc.sbuf_tensor("modv", [32, 6 * 128], F32))
        gbf64 = en(nc.sbuf_tensor("gbf64", [64, D], BF16))
        wtile = en(nc.sbuf_tensor("wtile", [128, 2 * 512], F32))
        mtile = en(nc.sbuf_tensor("mtile", [128, 2 * 512], I32))
        red1 = en(nc.sbuf_tensor("red1", [128, 16], F32))
        osel = en(nc.sbuf_tensor("osel_s", [8, 1], F32))
        xnTf = xnT[:].bitcast(F32)
        smfr = [xnTf[0:64, 0:1024], xnTf[0:64, 1024:2048]]
        brow = ostg[0][0:1, 0:512]
        btmp = ostg[1][0:1, 0:72]
        btmp2 = ostg[1][0:1, 0:512]
        gsl8 = en(nc.sbuf_tensor("gsl8", [8, 2 * 512], F32))
        wmblk = wtile
        bm4 = en(nc.sbuf_tensor("bm4", [4, KT * 16], F32))
        em16 = en(nc.sbuf_tensor("em16", [128, 16], F32))

        @block.gpsimd
        def _(g):
            for dst, src in ((idb, idb_d), (idf, idf_d), (onesb, onesb_d),
                             (onesf, onesf_d), (gsel, gsel_d), (ex4, ex4_d),
                             (ex8, ex8_d), (csl_s, cs_d), (osel, osel_d)):
                S("s_init").inc(g.dma_start(out=dst[:], in_=src.ap()), 16)
            for st_i in range(st_n):
                s_x[st_i].inc(
                    g.dma_start(out=x_sb[:, st_i * D:(st_i + 1) * D],
                                in_=x_d.ap()[st_i * 128:(st_i + 1) * 128, :]), 16)
        M["init"] = S("s_init").v

        @block.scalar
        def _(s):
            s.wait_ge(S("s_init").h, M["init"])
            S("s_misc").inc(s.activation(csl_s[:], csl_s[:], AF.Silu))
        M["silu"] = S("s_misc").v

        @block.tensor
        def _(t):
            t.wait_ge(S("s_misc").h, M["silu"])
            S("s_misc").inc(t.transpose(ps[0][0:128, 0:4], csl_s[:],
                                        idf[0:4, 0:4]))
        M["sT_ps"] = S("s_misc").v

        @block.vector
        def _(v):
            v.memset(epsc[:], 1e-5)
            v.wait_ge(S("s_misc").h, M["sT_ps"])
            S("s_misc").inc(v.tensor_copy(sT[:], ps[0][0:128, 0:4]))
        M["sT"] = S("s_misc").v

        mw_marks = {}
        mvslot = [wcol[0][:].bitcast(F32), wcol[1][:].bitcast(F32)]

        @block.sync
        def _(sy):
            for q in range(12):
                for kt in range(4):
                    if q >= 2:
                        sy.wait_ge(S("s_mvec").h, (q - 2) * 16 + 12 + kt + 1)
                    mw_marks[(q, kt)] = s_mw8[4 * (q % 2) + kt].inc(
                        sy.dma_start(
                            out=mvslot[q % 2][:, kt * 2048:(kt + 1) * 2048],
                            in_=wm_d.ap()[kt * 128:(kt + 1) * 128,
                                          q * 2048:(q + 1) * 2048]), 16)

        @block.tensor
        def _(t):
            t.wait_ge(S("s_misc").h, M["sT"])
            for q in range(12):
                for j in range(4):
                    nt = q * 4 + j
                    pbank = ps[1] if nt % 2 == 0 else ps[2]
                    if nt >= 2:
                        t.wait_ge(S("s_vec").h, S("s_vec").v + nt - 1)
                    for kt in range(4):
                        t.wait_ge(s_mw8[4 * (q % 2) + kt].h, mw_marks[(q, kt)])
                        S("s_mvec").inc(
                            t.matmul(pbank[0:1, :], sT[:, kt:kt + 1],
                                     mvslot[q % 2][:, kt * 2048 + j * 512:
                                                   kt * 2048 + (j + 1) * 512],
                                     start=(kt == 0), stop=(kt == 3)))
        mv_base = S("s_mvec").v - 192

        mp_marks = []
        mps_v0 = S("s_vec").v
        xnTf0 = xnT[:].bitcast(F32)
        mpslot = [xnTf0[0:1, 2048:3072], xnTf0[0:1, 3072:4096]]

        @block.vector
        def _(v):
            for nt in range(48):
                pbank = ps[1] if nt % 2 == 0 else ps[2]
                grp, j = nt // 2, nt % 2
                v.wait_ge(S("s_mvec").h, mv_base + (nt + 1) * 4)
                if grp >= 2 and j == 0:
                    v.wait_ge(s_mp[grp % 2].h, 16 * (grp // 2))
                S("s_vec").inc(v.tensor_copy(
                    mpslot[grp % 2][:, j * 512:(j + 1) * 512], pbank[0:1, :]))
        M["modps"] = S("s_vec").v

        @block.gpsimd
        def _(g):
            for grp in range(24):
                g.wait_ge(S("s_vec").h, mps_v0 + (grp + 1) * 2)
                mp_marks.append(s_mp[grp % 2].inc(
                    g.dma_start(
                        out=modp_d.ap()[bass.ds(grp * 1024, 1024)].unsqueeze(0),
                        in_=mpslot[grp % 2]), 16))
            for i in range(2):
                g.wait_ge(s_mp[i].h, s_mp[i].v)
            S("s_cc").inc(g.collective_compute(
                "AllReduce", ALU.add, replica_groups=rg,
                ins=[modp_d.ap()], outs=[modf_d.ap()]), 1)
            g.wait_ge(S("s_cc").h, S("s_cc").v)
            S("s_ar").inc(
                g.dma_start(out=modv[:].rearrange("j (v p) -> j v p", v=6),
                            in_=modf_d.ap().rearrange("(v j p) -> j v p",
                                                      v=6, p=128)), 16)
        M["modv"] = S("s_ar").v

        def vrow(i):
            return modv[:, i * 128:(i + 1) * 128]

        @block.vector
        def _(v):
            v.wait_ge(S("s_ar").h, M["modv"])
            v.tensor_scalar_add(vrow(1), vrow(1), 1.0)
            v.tensor_scalar_add(vrow(4), vrow(4), 1.0)
            v.drain()
            v.reciprocal(wtile[0:32, 0:128], vrow(1))
            v.drain()
            v.tensor_mul(vrow(0), vrow(0), wtile[0:32, 0:128])
            v.drain()
            v.reciprocal(wtile[0:32, 0:128], vrow(4))
            v.drain()
            v.tensor_mul(vrow(3), vrow(3), wtile[0:32, 0:128])
            v.drain()
            for src in (2, 5):
                gv = vrow(src)
                dd = wtile[0:32, 0:128]
                v.tensor_scalar(dd, gv, 0.0, None, ALU.is_ge)
                v.drain()
                v.tensor_scalar(dd, dd, 2.0, -1.0, ALU.mult, ALU.add)
                v.drain()
                v.tensor_mul(gv, gv, dd)
                v.drain()
                v.tensor_scalar_max(gv, gv, 0.25)
                v.drain()
                v.tensor_mul(gv, gv, dd)
                v.drain()
            S("s_vec").inc(v.drain())
        M["mrows"] = S("s_vec").v

        @block.tensor
        def _(t):
            t.wait_ge(S("s_vec").h, M["mrows"])
            for i, src in enumerate((1, 4, 0, 3)):
                S("s_gt").inc(t.transpose(ps[3 + i][:, 0:32], vrow(src),
                                          idf[0:32, 0:32]))
        M["mcolT"] = S("s_gt").v

        @block.vector
        def _(v):
            v.wait_ge(S("s_gt").h, M["mcolT"])
            S("s_vec").inc(v.tensor_copy(ca1c[:], ps[3][:, 0:32]))
            S("s_vec").inc(v.tensor_copy(cb1c[:], ps[4][:, 0:32]))
            S("s_vec").inc(v.tensor_copy(sapc[:], ps[5][:, 0:32]))
            S("s_vec").inc(v.tensor_copy(sbpc[:], ps[6][:, 0:32]))
        M["mcols"] = S("s_vec").v

        @block.gpsimd
        def _(g):
            g.wait_ge(S("s_vec").h, M["mcols"])
            mp = mproc_d.ap().rearrange("(j p) -> j p", j=64)
            S("s_ar").inc(g.dma_start(out=mp[0:32, :], in_=vrow(2)), 16)
            S("s_ar").inc(g.dma_start(out=mp[32:64, :], in_=vrow(5)), 16)
            g.wait_ge(S("s_ar").h, S("s_ar").v)
            S("s_ar").inc(
                g.dma_start(out=gsl8[:, 0:512],
                            in_=mproc_d.ap().rearrange("(h j) -> h j", h=16)
                            [0:8, :]), 16)
            S("s_ar").inc(
                g.dma_start(out=gsl8[:, 512:1024],
                            in_=mproc_d.ap().rearrange("(h j) -> h j", h=16)
                            [8:16, :]), 16)
            g.wait_ge(S("s_gt").h, S("s_gt").v + 4)
            for i in range(8):
                S("s_ar").inc(
                    g.dma_start(out=btmp2,
                                in_=mproc_d.ap()[bass.ds(D + i * 512, 512)]
                                .unsqueeze(0)), 16)
                g.wait_ge(S("s_gt").h, S("s_gt").v + 5 + i)
        ar_rows0 = S("s_ar").v - 10 * 16

        gt_rows0 = S("s_gt").v
        v_rows0 = S("s_vec").v

        @block.tensor
        def _(t):
            t.wait_ge(S("s_ar").h, ar_rows0 + 32)
            S("s_gt").inc(t.matmul(ps[4][0:1, :], osel[:], gsl8[:, 0:512],
                                   start=True, stop=True))
            S("s_gt").inc(t.matmul(ps[5][0:1, :], osel[:], gsl8[:, 512:1024],
                                   start=True, stop=True))
            t.wait_ge(S("s_vec").h, v_rows0 + 1)
            S("s_gt").inc(t.matmul(ps[0][:], onesf[:], brow,
                                   start=True, stop=True))
            t.wait_ge(S("s_vec").h, v_rows0 + 2)
            S("s_gt").inc(t.matmul(ps[1][:], onesf[:], btmp2,
                                   start=True, stop=True))
            for i in range(8):
                pbank = ps[2] if i % 2 == 0 else ps[3]
                t.wait_ge(S("s_ar").h, ar_rows0 + 32 + (i + 1) * 16)
                if i >= 2:
                    t.wait_ge(S("s_vec").h, v_rows0 + 4 + i - 1)
                S("s_gt").inc(t.matmul(pbank[0:64, :], onesf[0:1, 0:64],
                                       btmp2, start=True, stop=True))

        @block.vector
        def _(v):
            v.drain()
            v.wait_ge(S("s_gt").h, gt_rows0 + 1)
            S("s_vec").inc(v.tensor_copy(brow, ps[4][0:1, :]))
            v.wait_ge(S("s_gt").h, gt_rows0 + 2)
            S("s_vec").inc(v.tensor_copy(btmp2, ps[5][0:1, :]))
            v.wait_ge(S("s_gt").h, gt_rows0 + 3)
            S("s_vec").inc(v.tensor_copy(ga_b[:], ps[0][:]))
            v.wait_ge(S("s_gt").h, gt_rows0 + 4)
            S("s_vec").inc(v.tensor_copy(gb_b[:], ps[1][:]))
            for i in range(8):
                pbank = ps[2] if i % 2 == 0 else ps[3]
                v.wait_ge(S("s_gt").h, gt_rows0 + 5 + i)
                S("s_vec").inc(v.tensor_copy(gbf64[:, i * 512:(i + 1) * 512],
                                             pbank[0:64, :]))
        M["gbf64"] = S("s_vec").v
        M["gbc"] = M["gbf64"]

        mk_marks = []
        mk_v0 = S("s_vec").v

        @block.gpsimd
        def _(g):
            for kt in range(KT):
                buf = (kt % 2) * 512
                if kt >= 2:
                    g.wait_ge(S("s_vec").h, mk_v0 + 2 * (kt - 2) + 1)
                mk_marks.append(s_mk[kt % 2].inc(
                    g.dma_start(out=mtile[:, buf:buf + 512],
                                in_=mT_d.ap()[kt * 128:(kt + 1) * 128, :]),
                    16))
        bm_gt0 = S("s_gt").v

        @block.tensor
        def _(t):
            t.wait_ge(S("s_init").h, M["init"])
            for kt in range(KT):
                t.wait_ge(S("s_vec").h, mk_v0 + 2 * kt + 1)
                pbank = ps[4] if kt % 2 == 0 else ps[5]
                S("s_gt").inc(t.matmul(pbank[0:4, 0:16], gsel[:], red1[:],
                                       start=True, stop=True))

        @block.vector
        def _(v):
            v.drain()
            for kt in range(KT):
                buf = (kt % 2) * 512
                v.wait_ge(s_mk[kt % 2].h, mk_marks[kt])
                S("s_vec").inc(v.tensor_reduce(
                    red1[:],
                    mtile[:, buf:buf + 512].rearrange("p (nb i) -> p nb i", i=32),
                    AX.X, ALU.add))
                pbank = ps[4] if kt % 2 == 0 else ps[5]
                v.wait_ge(S("s_gt").h, bm_gt0 + kt + 1)
                S("s_vec").inc(v.tensor_copy(bm4[:, kt * 16:(kt + 1) * 16],
                                             pbank[0:4, 0:16]))
            v.drain()
            v.tensor_scalar(bm4[:], bm4[:], 0.5, None, ALU.is_ge)
            S("s_vec").inc(v.drain())
        M["bmb"] = S("s_vec").v

        def prep_matrix(mat_idx, src_d, dst_sb, agq_in, c1col, g_bcast, masked):
            gs_base = S("s_gs").v
            wt_marks = []
            wslot = [wtile[:, 0:512], wtile[:, 512:1024],
                     ostg[0][:, 0:512]]

            @block.sync
            def _(sy):
                for kt in range(KT):
                    gidx = gs_base + kt
                    if gidx < 3:
                        sy.wait_ge(S("s_vec").h, M["mrows"])
                    if gidx == 2:
                        sy.wait_ge(S("s_vec").h, M["gbf64"])
                    if gidx == 32:
                        sy.wait_ge(S("s_bias").h, 80)
                    if gidx - 2 > 0:
                        sy.wait_ge(S("s_gs").h, gidx - 2)
                    wt_marks.append(s_wt[gidx % 3].inc(
                        sy.dma_start(out=wslot[gidx % 3],
                                     in_=src_d.ap()[kt * 128:(kt + 1) * 128, :]),
                        16))

            em_gt0 = S("s_gt").v

            if masked:
                @block.tensor
                def _(t):
                    t.wait_ge(S("s_vec").h, M["bmb"])
                    for kt in range(KT):
                        pbank = ps[6] if kt % 2 == 0 else ps[7]
                        if kt >= 2:
                            t.wait_ge(S("s_gs").h, gs_base + kt - 1)
                        S("s_gt").inc(t.matmul(
                            pbank[:, 0:16], ex4[:],
                            bm4[0:4, kt * 16:(kt + 1) * 16],
                            start=True, stop=True))

            @block.vector
            def _(v):
                v.drain()
                v.wait_ge(S("s_vec").h, M["gbf64"])
                for kt in range(KT):
                    gidx = gs_base + kt
                    v.wait_ge(s_wt[gidx % 3].h, wt_marks[kt])
                    dst = dst_sb[:, kt * 512:(kt + 1) * 512]
                    wt = wslot[gidx % 3]
                    if masked:
                        pbank = ps[6] if kt % 2 == 0 else ps[7]
                        v.wait_ge(S("s_gt").h, em_gt0 + kt + 1)
                        v.tensor_copy(em16[:], pbank[:, 0:16])
                        v.drain()
                        v.tensor_tensor(
                            wt.rearrange("p (nb i) -> p nb i", i=32),
                            wt.rearrange("p (nb i) -> p nb i", i=32),
                            em16[:].unsqueeze(2).broadcast_to([128, 16, 32]),
                            ALU.mult)
                        v.drain()
                    v.tensor_scalar(wt, wt, c1col[:, kt:kt + 1], None, ALU.mult)
                    v.drain()
                    S("s_gs").inc(v.tensor_tensor(dst, wt, g_bcast[:], ALU.mult))

            gsv = S("s_gs").v
            sc = sapc if mat_idx == 0 else sbpc

            @block.tensor
            def _(t):
                t.wait_ge(S("s_gs").h, gsv)
                for kt in range(KT):
                    S("s_gt").inc(t.matmul(ps[0][0:1, :], sc[:, kt:kt + 1],
                                           dst_sb[:, kt * 512:(kt + 1) * 512],
                                           start=(kt == 0), stop=(kt == KT - 1)))
            bv = S("s_gt").v

            @block.vector
            def _(v):
                v.wait_ge(S("s_gt").h, bv)
                if mat_idx == 1:
                    v.wait_ge(S("s_bias").h, 80)
                S("s_vec").inc(v.tensor_copy(brow, ps[0][0:1, :]))
            bvv = S("s_vec").v
            M[f"brow_{mat_idx}"] = bvv

            bag_in = baga_i if mat_idx == 0 else bagf_i
            bag_out = baga_o if mat_idx == 0 else bagf_o
            agq_out = wagq_o if mat_idx == 0 else fagq_o

            @block.gpsimd
            def _(g):
                for q in range(4):
                    g.wait_ge(S("s_gs").h, gs_base + 8 * (q + 1))
                    mk = S("s_bias").inc(
                        g.dma_start(out=agq_in[q].ap(),
                                    in_=dst_sb[:, q * QW:(q + 1) * QW]), 16)
                    g.wait_ge(S("s_bias").h, mk)
                    S("s_ag").inc(g.collective_compute(
                        "AllGather", ALU.bypass, replica_groups=rg,
                        ins=[agq_in[q].ap()], outs=[agq_out[q].ap()]), 1)
                    M[f"agq_{mat_idx}_{q}"] = S("s_ag").v
                g.wait_ge(S("s_vec").h, bvv)
                mk = S("s_bias").inc(
                    g.dma_start(out=bag_in.ap().unsqueeze(0), in_=brow), 16)
                g.wait_ge(S("s_bias").h, mk)
                S("s_ag").inc(g.collective_compute(
                    "AllGather", ALU.bypass, replica_groups=rg,
                    ins=[bag_in.ap()], outs=[bag_out.ap()]), 1)
            M[f"ag_{mat_idx}"] = S("s_ag").v

        prep_matrix(0, wT_d, wcol[0], wagq_i, ca1c, ga_b, True)
        prep_matrix(1, fT_d, wcol[1], fagq_i, cb1c, gb_b, False)
        M["prep_done"] = S("s_bias").v

        M["ag"] = M["ag_1"]

        def emit_phase3():
            mm_after = S("s_mm").v
            ev_after = S("s_ev").v
            a_gt0 = S("s_gt").v
            a_v0 = S("s_vec").v
            a_marks = []

            @block.sync
            def _(sy):
                sy.wait_ge(S("s_mm").h, mm_after)
                for q in range(4):
                    if q >= 2:
                        sy.wait_ge(S("s_gt").h, a_gt0 + (q - 1) * 8)
                    a_marks.append(s_smp[q % 2].inc(
                        sy.dma_start(out=smfr[q % 2],
                                     in_=A_d.ap()[:, q * 1024:(q + 1) * 1024]), 16))

            @block.tensor
            def _(t):
                t.wait_ge(S("s_ev").h, ev_after)
                for q in range(4):
                    t.wait_ge(s_smp[q % 2].h, a_marks[q])
                    for j in range(8):
                        kt = q * 8 + j
                        pbank = ps[1] if kt % 2 == 0 else ps[2]
                        if kt >= 2:
                            t.wait_ge(S("s_vec").h, a_v0 + kt - 1)
                        S("s_gt").inc(t.transpose(
                            pbank[:, 0:64], smfr[q % 2][:, j * 128:(j + 1) * 128],
                            idf[0:64, 0:64]))

            @block.vector
            def _(v):
                v.wait_ge(S("s_vec").h, M["gbf64"])
                for kt in range(KT):
                    pbank = ps[1] if kt % 2 == 0 else ps[2]
                    v.wait_ge(S("s_gt").h, a_gt0 + kt + 1)
                    S("s_vec").inc(v.tensor_scalar(
                        SgT[:, kt * 72:kt * 72 + 64], pbank[:, 0:64],
                        cb1c[:, kt:kt + 1], None, ALU.mult))
            M["A_T"] = S("s_vec").v

            gw_gt0 = S("s_gt").v
            gw_v0 = S("s_vec").v
            gw_marks = []

            @block.sync
            def _(sy):
                for q in range(4):
                    if q < 2:
                        sy.wait_ge(S("s_gt").h, a_gt0 + 32)
                    else:
                        sy.wait_ge(S("s_gt").h, gw_gt0 + (q - 1) * 8)
                    gw_marks.append(s_smp[q % 2].inc(
                        sy.dma_start(out=smfr[q % 2][0:8, :],
                                     in_=gw_d.ap()[:, q * 1024:(q + 1) * 1024]), 16))

            @block.tensor
            def _(t):
                t.wait_ge(S("s_vec").h, gw_v0)
                for q in range(4):
                    t.wait_ge(s_smp[q % 2].h, gw_marks[q])
                    for j in range(8):
                        kt = q * 8 + j
                        pbank = ps[1] if kt % 2 == 0 else ps[2]
                        if kt >= 2:
                            t.wait_ge(S("s_vec").h, gw_v0 + kt - 1)
                        S("s_gt").inc(t.transpose(
                            pbank[:, 0:8], smfr[q % 2][0:8, j * 128:(j + 1) * 128],
                            idf[0:8, 0:8]))

            @block.vector
            def _(v):
                for kt in range(KT):
                    pbank = ps[1] if kt % 2 == 0 else ps[2]
                    v.wait_ge(S("s_gt").h, gw_gt0 + kt + 1)
                    S("s_vec").inc(v.tensor_scalar(
                        SgT[:, kt * 72 + 64:kt * 72 + 72], pbank[:, 0:8],
                        cb1c[:, kt:kt + 1], None, ALU.mult))
            M["Sg"] = S("s_vec").v

            bf_v0 = S("s_vec").v
            bf_marks = []

            @block.sync
            def _(sy):
                for q in range(4):
                    if q < 2:
                        sy.wait_ge(S("s_gt").h, gw_gt0 + 32)
                    else:
                        sy.wait_ge(S("s_vec").h, bf_v0 + q - 1)
                    bf_marks.append(s_smp[q % 2].inc(
                        sy.dma_start(out=smfr[q % 2],
                                     in_=Bf_d.ap()[:, q * 1024:(q + 1) * 1024]), 16))

            @block.vector
            def _(v):
                v.drain()
                for q in range(4):
                    v.wait_ge(s_smp[q % 2].h, bf_marks[q])
                    S("s_vec").inc(v.tensor_tensor(
                        BfT[:, q * 1024:(q + 1) * 1024], smfr[q % 2],
                        gbf64[:, q * 1024:(q + 1) * 1024], ALU.mult))
            M["BfT"] = S("s_vec").v

            bA_gt0 = S("s_gt").v

            @block.tensor
            def _(t):
                t.wait_ge(S("s_vec").h, M["Sg"])
                for kt in range(KT):
                    S("s_gt").inc(t.matmul(ps[3][0:1, 0:72], sbpc[:, kt:kt + 1],
                                           SgT[:, kt * 72:(kt + 1) * 72],
                                           start=(kt == 0), stop=(kt == KT - 1)))
            M["bA_ps"] = S("s_gt").v

            @block.vector
            def _(v):
                v.wait_ge(S("s_gt").h, M["bA_ps"])
                S("s_vec").inc(v.tensor_copy(btmp, ps[3][0:1, 0:72]))
            M["bA_sb"] = S("s_vec").v

            @block.tensor
            def _(t):
                t.wait_ge(S("s_vec").h, M["bA_sb"])
                S("s_gt").inc(t.transpose(ps[4][0:72, 0:1], btmp,
                                          idf[0:1, 0:1]))
            M["bA_T"] = S("s_gt").v

            @block.vector
            def _(v):
                v.wait_ge(S("s_gt").h, M["bA_T"])
                S("s_vec").inc(v.tensor_copy(bA_c[:], ps[4][0:72, 0:1]))
            M["bA"] = S("s_vec").v


        emit_phase3()

        def xsb(st_i):
            return x_sb[:, st_i * D:(st_i + 1) * D]

        def xnT_blk(st_i, kt):
            off = (st_i * KT + kt) * 128
            return xnT[:, off:off + 128]

        oc_total = 2 * NG * st_n * 0 + 0
        oc = [0]

        for b in range(nsub):
            if b > 0:
                ev_need = S("s_out").v

                @block.gpsimd
                def _(g, b=b, ev_need=ev_need):
                    g.wait_ge(S("s_out").h, ev_need)
                    for st_i in range(st_n):
                        r0 = b * sub + st_i * 128
                        s_x[st_i].inc(
                            g.dma_start(out=xsb(st_i),
                                        in_=x_d.ap()[r0:r0 + 128, :]), 16)

            x_marks = [s_x[i].v for i in range(st_n)]

            for layer in range(2):
                var_base = S("s_var").v
                nb_base = S("s_nb").v
                std_base0 = S("s_std").v

                @block.vector
                def _(v, b=b, layer=layer, x_marks=tuple(x_marks)):
                    v.drain()
                    for st_i in range(st_n):
                        if layer == 0:
                            v.wait_ge(s_x[st_i].h, x_marks[st_i])
                            for cch in range(8):
                                v.bn_stats(
                                    stt[:, cch * 6:(cch + 1) * 6],
                                    xsb(st_i)[:, cch * 512:(cch + 1) * 512])
                            v.drain()
                            src = stt[:]
                        else:
                            src = stt2[:, st_i * 48:(st_i + 1) * 48]
                        means = src.rearrange("p (c t) -> p c t", t=3)[:, :, 1]
                        cvs = src.rearrange("p (c t) -> p c t", t=3)[:, :, 2]
                        c0 = st_i * 4
                        v.tensor_reduce(mvar[:, c0:c0 + 1], means, AX.X,
                                        ALU.add, negate=True)
                        v.tensor_reduce(mvar[:, c0 + 1:c0 + 2], cvs, AX.X,
                                        ALU.add)
                        v.tensor_tensor(lsc[:, 0:16], means, means, ALU.mult)
                        v.drain()
                        v.tensor_reduce(lsc[:, 16:17], lsc[:, 0:16], AX.X,
                                        ALU.add)
                        v.tensor_scalar(mvar[:, c0:c0 + 1], mvar[:, c0:c0 + 1],
                                        1.0 / 16, None, ALU.mult)
                        v.tensor_scalar(mvar[:, c0 + 1:c0 + 2],
                                        mvar[:, c0 + 1:c0 + 2], 1.0 / D, None,
                                        ALU.mult)
                        v.drain()
                        v.tensor_scalar(lsc[:, 16:17], lsc[:, 16:17],
                                        1.0 / 16, None, ALU.mult)
                        v.tensor_tensor(lsc[:, 0:1], mvar[:, c0:c0 + 1],
                                        mvar[:, c0:c0 + 1], ALU.mult)
                        v.drain()
                        v.tensor_tensor(mvar[:, c0 + 1:c0 + 2],
                                        mvar[:, c0 + 1:c0 + 2],
                                        lsc[:, 16:17], ALU.add)
                        v.drain()
                        S("s_var").inc(v.tensor_tensor(
                            mvar[:, c0 + 1:c0 + 2], mvar[:, c0 + 1:c0 + 2],
                            lsc[:, 0:1], ALU.subtract))
                        v.wait_ge(S("s_std").h, S("s_std").v + st_i + 1)
                        v.reciprocal(mvar[:, c0 + 2:c0 + 3],
                                     mvar[:, c0 + 1:c0 + 2])
                        v.drain()
                        v.tensor_tensor(mvar[:, c0 + 3:c0 + 4],
                                        mvar[:, c0:c0 + 1],
                                        mvar[:, c0 + 2:c0 + 3], ALU.mult)
                        S("s_nb").inc(v.drain())

                norm_base = S("s_norm").v
                tp_base = S("s_tp").v
                xnt_base = S("s_xnt").v
                std_base = std_base0

                @block.scalar
                def _(s, var_base=var_base, std_base=std_base):
                    for st_i in range(st_n):
                        s.wait_ge(S("s_var").h, var_base + st_i + 1)
                        c0 = st_i * 4
                        S("s_std").inc(
                            s.activation(mvar[:, c0 + 1:c0 + 2],
                                         mvar[:, c0 + 1:c0 + 2], AF.Sqrt,
                                         bias=epsc[:]))

                @block.scalar
                def _(s, b=b, layer=layer, nb_base=nb_base,
                      norm_base=norm_base, tp_base=tp_base,
                      xnt_base=xnt_base):
                    def emit_norm(st_i):
                        s.wait_ge(S("s_nb").h, nb_base + st_i + 1)
                        if st_i >= 2:
                            s.wait_ge(S("s_tp").h, tp_base + (st_i - 1) * 32)
                        c0 = st_i * 4
                        ins = s.activation(xn_st[st_i % 2][:], xsb(st_i),
                                           AF.Identity,
                                           bias=mvar[:, c0 + 3:c0 + 4],
                                           scale=mvar[:, c0 + 2:c0 + 3])
                        S("s_norm").inc(ins)

                    emit_norm(0)
                    if st_n > 1:
                        emit_norm(1)
                    for st_i in range(st_n):
                        for gi in range(8):
                            gidx = st_i * 8 + gi
                            pbank = ps[6 + (gi % 2)]
                            s.wait_ge(S("s_tp").h, tp_base + (gidx + 1) * 4)
                            off = (st_i * KT + gi * 4) * 128
                            S("s_xnt").inc(
                                s.copy(xnT[:, off:off + 512],
                                       pbank[:].bitcast(BF16)[:, 0:512]))
                        if st_i + 2 < st_n:
                            emit_norm(st_i + 2)

                @block.tensor
                def _(t, b=b, layer=layer, norm_base=norm_base,
                      tp_base=tp_base, xnt_base=xnt_base):
                    for st_i in range(st_n):
                        t.wait_ge(S("s_norm").h, norm_base + st_i + 1)
                        for gi in range(8):
                            gidx = st_i * 8 + gi
                            pbank = ps[6 + (gi % 2)]
                            if gidx >= 2:
                                t.wait_ge(S("s_xnt").h, xnt_base + gidx - 1)
                            for j in range(4):
                                kt = gi * 4 + j
                                S("s_tp").inc(t.transpose(
                                    pbank[:].bitcast(BF16)[:, j * 128:(j + 1) * 128],
                                    xn_st[st_i % 2][:, kt * 128:(kt + 1) * 128],
                                    idb[:]))
                S("s_xnt").v = xnt_base + st_n * 8
                xnt_done = S("s_xnt").v

                if layer == 1:
                    g_gt0 = S("s_gt").v

                    @block.tensor
                    def _(t, b=b, xnt_done=xnt_done):
                        t.wait_ge(S("s_xnt").h, xnt_done)
                        if b == 0:
                            t.wait_ge(S("s_vec").h, M["bA"])
                        for kt in range(KT):
                            rhs = xnT[:].rearrange("p (s k c) -> p s k c",
                                                   s=st_n, k=KT)[:, :, kt, :]
                            S("s_gt").inc(t.matmul(
                                ps[0][0:72, 0:sub],
                                SgT[:, kt * 72:(kt + 1) * 72],
                                rhs, start=(kt == 0), stop=(kt == KT - 1)))
                    g_ps = S("s_gt").v

                    @block.scalar
                    def _(s, b=b, g_ps=g_ps):
                        s.wait_ge(S("s_gt").h, g_ps)
                        S("s_gv").inc(
                            s.activation(hterT[:], ps[0][0:64, 0:sub],
                                         AF.Identity, bias=bA_c[0:64, :]))
                        S("s_gv").inc(
                            s.activation(logit[:], ps[0][64:72, 0:sub],
                                         AF.Identity, bias=bA_c[64:72, :]))
                    g_ev = S("s_gv").v

                    lgT_gt0 = S("s_gt").v
                    lgT_v0 = S("s_gv").v

                    @block.tensor
                    def _(t, b=b, g_ev=g_ev):
                        t.wait_ge(S("s_gv").h, g_ev)
                        for st_i in range(st_n):
                            pbank = ps[1] if st_i % 2 == 0 else ps[2]
                            if st_i >= 2:
                                t.wait_ge(S("s_gv").h, lgT_v0 + st_i - 1)
                            S("s_gt").inc(t.transpose(
                                pbank[:, 0:8],
                                logit[:, st_i * 128:(st_i + 1) * 128],
                                idf[0:8, 0:8]))

                    @block.vector
                    def _(v, b=b):
                        for st_i in range(st_n):
                            pbank = ps[1] if st_i % 2 == 0 else ps[2]
                            v.wait_ge(S("s_gt").h, lgT_gt0 + st_i + 1)
                            S("s_gv").inc(v.tensor_copy(
                                lg_tm[:, st_i * 8:(st_i + 1) * 8],
                                pbank[:, 0:8]))
                        v.drain()
                        for st_i in range(st_n):
                            l = lg_tm[:, st_i * 8:(st_i + 1) * 8]
                            wdc = wd_tm[:, st_i * 8:(st_i + 1) * 8]
                            l2 = lg2[:, st_i * 8:(st_i + 1) * 8]
                            m1 = ggs[:, st_i * 4:st_i * 4 + 1]
                            nm1 = ggs[:, st_i * 4 + 1:st_i * 4 + 2]
                            m2 = ggs[:, st_i * 4 + 2:st_i * 4 + 3]
                            v.tensor_reduce(m1, l, AX.X, ALU.max)
                            v.drain()
                            v.tensor_scalar(nm1, m1, -1.0, None, ALU.mult)
                            v.tensor_scalar(wdc, l, m1, None, ALU.is_ge)
                            v.drain()
                            v.scalar_tensor_tensor(l2, wdc, -1e9, l,
                                                   ALU.mult, ALU.add)
                            v.drain()
                            v.tensor_reduce(m2, l2, AX.X, ALU.max)
                            v.drain()
                            v.tensor_scalar(wdc, l, m2, None, ALU.is_ge)
                            S("s_gv").inc(v.drain())
                    keep_base = S("s_gv").v - st_n

                    @block.scalar
                    def _(s, b=b, keep_base=keep_base):
                        for st_i in range(st_n):
                            s.wait_ge(S("s_gv").h, keep_base + st_i + 1)
                            l = lg_tm[:, st_i * 8:(st_i + 1) * 8]
                            nm1 = ggs[:, st_i * 4 + 1:st_i * 4 + 2]
                            S("s_gs").inc(
                                s.activation(l, l, AF.Exp, bias=nm1, scale=1.0))
                    exp_base = S("s_gs").v - st_n

                    @block.vector
                    def _(v, b=b, exp_base=exp_base):
                        for st_i in range(st_n):
                            v.wait_ge(S("s_gs").h, exp_base + st_i + 1)
                            l = lg_tm[:, st_i * 8:(st_i + 1) * 8]
                            wdc = wd_tm[:, st_i * 8:(st_i + 1) * 8]
                            den = ggs[:, st_i * 4 + 3:st_i * 4 + 4]
                            v.tensor_tensor(wdc, wdc, l, ALU.mult)
                            v.drain()
                            v.tensor_reduce(den, wdc, AX.X, ALU.add)
                            v.drain()
                            v.reciprocal(den, den)
                            v.drain()
                            v.tensor_scalar(wdc, wdc, den, None, ALU.mult)
                            S("s_gv").inc(v.drain())
                    wd_base = S("s_gv").v - st_n

                    wdT_gt0 = S("s_gt").v
                    wdT_v0 = S("s_gv").v

                    @block.tensor
                    def _(t, b=b, wd_base=wd_base):
                        for st_i in range(st_n):
                            pbank = ps[1] if st_i % 2 == 0 else ps[2]
                            t.wait_ge(S("s_gv").h, wd_base + st_i + 1)
                            if st_i >= 2:
                                t.wait_ge(S("s_gv").h, wdT_v0 + st_i - 1)
                            S("s_gt").inc(t.transpose(
                                pbank[0:8, 0:128],
                                wd_tm[:, st_i * 8:(st_i + 1) * 8],
                                idf[:]))

                    @block.vector
                    def _(v, b=b):
                        for st_i in range(st_n):
                            pbank = ps[1] if st_i % 2 == 0 else ps[2]
                            v.wait_ge(S("s_gt").h, wdT_gt0 + st_i + 1)
                            S("s_gv").inc(v.tensor_copy(
                                wdT[:, st_i * 128:(st_i + 1) * 128],
                                pbank[0:8, 0:128]))
                    wdT_done = S("s_gv").v

                    @block.tensor
                    def _(t, b=b, wdT_done=wdT_done):
                        t.wait_ge(S("s_gv").h, wdT_done)
                        S("s_gt").inc(t.matmul(ps[3][0:64, 0:sub], ex8[:],
                                               wdT[:], start=True, stop=True))
                    ex_ps = S("s_gt").v

                    @block.vector
                    def _(v, b=b, ex_ps=ex_ps):
                        v.wait_ge(S("s_gt").h, ex_ps)
                        S("s_gv").inc(v.tensor_tensor(hsT[:], hterT[:],
                                                      ps[3][0:64, 0:sub],
                                                      ALU.mult))
                    M[f"hsT_{b}"] = S("s_gv").v

                ag_src = wagq_o if layer == 0 else fagq_o
                mm_base = S("s_mm").v
                ev_base = S("s_ev").v
                phase_idx = b * 2 + layer

                bl_marks = []
                out_base_g = S("s_out").v

                @block.gpsimd
                def _(g, b=b, layer=layer, phase_idx=phase_idx,
                      out_base_g=out_base_g):
                    if phase_idx <= 1:
                        g.wait_ge(S("s_ag").h, M[f"ag_{layer}"])
                    bo = (baga_o if layer == 0 else bagf_o).ap()\
                        .rearrange("(g j) -> g j", g=8)

                    def emit_outs(gh_o):
                        for st_i in range(st_n):
                            i = gh_o * st_n + st_i
                            g.wait_ge(S("s_out").h, out_base_g + i + 1)
                            r0 = b * sub + st_i * 128
                            gi_o = oc[0] + i
                            ins = g.dma_start(
                                out=y_d.ap()[r0:r0 + 128,
                                             gh_o * 512:(gh_o + 1) * 512],
                                in_=ostg[gi_o % 2][:])
                            s_yp[gi_o % 2].inc(ins, 16)

                    for gh in range(NG):
                        g_gh = phase_idx * NG + gh
                        if g_gh >= 2:
                            g.wait_ge(S("s_mm").h, (g_gh - 1) * st_n)
                        bl_marks.append(s_bl[gh % 2].inc(
                            g.dma_start(out=bbuf[gh % 2][:],
                                        in_=bo[gh:gh + 1, :]), 16))
                        if layer == 1 and gh >= 1:
                            emit_outs(gh - 1)
                    if layer == 1:
                        emit_outs(NG - 1)

                wc_marks = []

                @block.sync
                def _(sy, b=b, layer=layer, mm_base=mm_base,
                      phase_idx=phase_idx):
                    for gh in range(NG):
                        buf = gh % 2
                        if phase_idx == 0 and gh == 0:
                            sy.wait_ge(S("s_bias").h, 80)
                        if phase_idx == 0 and gh == 1:
                            sy.wait_ge(S("s_bias").h, 160)
                        g_gh = phase_idx * NG + gh
                        if g_gh >= 2:
                            sy.wait_ge(S("s_mm").h, (g_gh - 1) * st_n)
                        qm = []
                        for q in range(4):
                            if b == 0:
                                sy.wait_ge(S("s_ag").h,
                                           M[f"agq_{layer}_{q}"])
                            qm.append(s_wcp[buf].inc(
                                sy.dma_start(
                                    out=wcol[buf][:, q * QW:(q + 1) * QW],
                                    in_=ag_src[q].ap()[gh * 128:(gh + 1) * 128,
                                                       :]), 16))
                        wc_marks.append(qm)

                @block.tensor
                def _(t, b=b, layer=layer, ev_base=ev_base,
                      out_base0=S("s_out").v, xnt_done=xnt_done,
                      phase_idx=phase_idx, bl_marks=bl_marks,
                      wc_marks=wc_marks):
                    t.wait_ge(S("s_xnt").h, xnt_done)
                    if layer == 1:
                        t.wait_ge(S("s_gv").h, M[f"hsT_{b}"])
                        if b == 0:
                            t.wait_ge(S("s_vec").h, M["BfT"])
                    for gh in range(NG):
                        buf = gh % 2
                        if gh >= 2:
                            if layer == 0:
                                t.wait_ge(S("s_ev").h,
                                          ev_base + (gh - 1) * st_n)
                            else:
                                t.wait_ge(S("s_out").h,
                                          out_base0 + (gh - 1) * st_n)
                        bset = 0 if gh % 2 == 0 else 4
                        for kk in range(KT):
                            if kk % 8 == 0:
                                t.wait_ge(s_wcp[buf].h,
                                          wc_marks[gh][kk // 8])
                            for st_i in range(st_n):
                                t.matmul(ps[bset + st_i][:],
                                         xnT_blk(st_i, kk),
                                         wcol[buf][:, kk * 512:(kk + 1) * 512],
                                         start=(kk == 0), stop=False)
                        t.wait_ge(s_bl[gh % 2].h, bl_marks[gh])
                        for st_i in range(st_n):
                            if layer == 1:
                                t.matmul(ps[bset + st_i][:],
                                         hsT[:, st_i * 128:(st_i + 1) * 128],
                                         BfT[:, gh * 512:(gh + 1) * 512],
                                         start=False, stop=False)
                            S("s_mm").inc(t.matmul(ps[bset + st_i][:],
                                                   onesb[:], bbuf[gh % 2][:],
                                                   start=False, stop=True))

                @block.vector
                def _(v, b=b, layer=layer, mm_base=mm_base):
                    v.drain()
                    for gh in range(NG):
                        bset = 0 if gh % 2 == 0 else 4
                        for st_i in range(st_n):
                            v.wait_ge(S("s_mm").h,
                                      mm_base + gh * st_n + st_i + 1)
                            pps = ps[bset + st_i]
                            dst = xsb(st_i)[:, gh * 512:(gh + 1) * 512]
                            if layer == 0:
                                S("s_ev").inc(
                                    v.tensor_tensor(dst, pps[:], dst, ALU.add))
                            else:
                                i = oc[0] + gh * st_n + st_i
                                if i >= 2:
                                    v.wait_ge(s_yp[i % 2].h, 16 * (i // 2))
                                ob = ostg[i % 2]
                                ins = v.tensor_tensor(ob[:], pps[:], dst,
                                                      ALU.add)
                                S("s_out").inc(ins)
                        if layer == 0:
                            v.drain()
                            for st_i in range(st_n):
                                c6 = (st_i * 8 + gh) * 6
                                v.bn_stats(
                                    stt2[:, c6:c6 + 6],
                                    xsb(st_i)[:, gh * 512:(gh + 1) * 512])

                if layer == 1:
                    oc[0] += NG * st_n

        @block.gpsimd
        def _(g):
            g.wait_ge(s_yp[0].h, s_yp[0].v)
            g.wait_ge(s_yp[1].h, s_yp[1].v)

    return nc


def _host_inputs(L0_attn_data, L0_attn_weight, L0_attn_mask, L0_ffn_W_base,
                 L0_ffn_gate_W, L0_ffn_A, L0_ffn_B, L0_c, L0_wmod,
                 tok=TOK):
    x = np.ascontiguousarray(np.asarray(L0_attn_data, dtype=np.float32)
                             .reshape(-1, D))
    assert x.shape[0] == NCORES * tok
    wT = np.ascontiguousarray(np.asarray(L0_attn_weight, np.float32).T)
    mT = np.ascontiguousarray(np.asarray(L0_attn_mask, np.int32).T)
    fT = np.ascontiguousarray(np.asarray(L0_ffn_W_base, np.float32).T)
    wmod = np.asarray(L0_wmod, np.float32)
    cvec = np.asarray(L0_c, np.float32).reshape(D)
    gw = np.ascontiguousarray(np.asarray(L0_ffn_gate_W, np.float32))
    Afl = np.ascontiguousarray(np.asarray(L0_ffn_A, np.float32).reshape(ER, D))
    Bf = np.ascontiguousarray(np.asarray(L0_ffn_B, np.float32)
                              .transpose(0, 2, 1).reshape(ER, D))
    identb = np.eye(128, dtype=np.float32).astype(ml_dtypes.bfloat16)
    identf = np.eye(128, dtype=np.float32)
    grpsel = np.zeros((128, 4), np.float32)
    grpsel[np.arange(128), np.arange(128) // 32] = 1.0
    expand4 = np.ascontiguousarray(grpsel.T)
    expand8 = np.zeros((8, 64), np.float32)
    expand8[np.arange(64) // 8, np.arange(64)] = 2.0
    onesb = np.ones((1, 128), np.float32).astype(ml_dtypes.bfloat16)
    onesf = np.ones((1, 128), np.float32)

    in_maps = []
    for c in range(NCORES):
        n0 = c * NSL
        in_maps.append({
            "x": x[c * tok:(c + 1) * tok],
            "wt": np.ascontiguousarray(wT[:, n0:n0 + NSL]),
            "maskt": np.ascontiguousarray(mT[:, n0:n0 + NSL]),
            "wbt": np.ascontiguousarray(fT[:, n0:n0 + NSL]),
            "wmod": np.ascontiguousarray(wmod[c * WMK:(c + 1) * WMK]),
            "csl": np.ascontiguousarray(
                cvec[c * WMK:(c + 1) * WMK].reshape(4, 128)),
            "gwn": gw, "afl": Afl, "bfl": Bf,
            "identb": identb, "identf": identf, "grpsel": grpsel,
            "expand4": expand4, "expand8": expand8, "onesb": onesb,
            "onesf": onesf,
            "osel": np.eye(8, dtype=np.float32)[:, c:c + 1].copy(),
        })
    return in_maps


def kernel(L0_attn_data, L0_attn_weight, L0_attn_mask, L0_ffn_W_base,
           L0_ffn_gate_W, L0_ffn_A, L0_ffn_B, L0_c, L0_wmod):
    in_maps = _host_inputs(L0_attn_data, L0_attn_weight, L0_attn_mask,
                           L0_ffn_W_base, L0_ffn_gate_W, L0_ffn_A, L0_ffn_B,
                           L0_c, L0_wmod)
    nc = build_program()
    res = bass_utils.run_bass_kernel_spmd(nc, in_maps,
                                          core_ids=list(range(NCORES)))
    y = np.concatenate([res.results[c]["y"] for c in range(NCORES)], axis=0)
    return y.reshape(1, SEQ, D).astype(np.float32)


# version 14
# speedup vs baseline: 1.0436x; 1.0284x over previous
import numpy as np
import ml_dtypes
from contextlib import ExitStack

import concourse.bass as bass
import concourse.mybir as mybir
import concourse.bass_utils as bass_utils

F32 = mybir.dt.float32
BF16 = mybir.dt.bfloat16
I32 = mybir.dt.int32
AF = mybir.ActivationFunctionType
ALU = mybir.AluOpType
AX = mybir.AxisListType

D = 4096
NCORES = 8
SEQ = 16384
TOK = SEQ // NCORES
KT = D // 128
NG = 8
NSL = D // NCORES
NMOD = 6 * D
WMK = D // NCORES
ER = 64


class Sem:
    def __init__(self, handle):
        self.h = handle
        self.v = 0

    def inc(self, ins, n=1):
        ins.then_inc(self.h, n)
        self.v += n
        return self.v


def build_program(tok=TOK, nsub=4):
    sub = tok // nsub
    st_n = sub // 128
    nc = bass.Bass("TRN2", target_bir_lowering=False)

    x_d = nc.dram_tensor("x", [tok, D], F32, kind="ExternalInput")
    wT_d = nc.dram_tensor("wt", [D, NSL], F32, kind="ExternalInput")
    mT_d = nc.dram_tensor("maskt", [D, NSL], I32, kind="ExternalInput")
    fT_d = nc.dram_tensor("wbt", [D, NSL], F32, kind="ExternalInput")
    wm_d = nc.dram_tensor("wmod", [WMK, NMOD], F32, kind="ExternalInput")
    cs_d = nc.dram_tensor("csl", [4, 128], F32, kind="ExternalInput")
    gw_d = nc.dram_tensor("gwn", [8, D], F32, kind="ExternalInput")
    A_d = nc.dram_tensor("afl", [ER, D], F32, kind="ExternalInput")
    Bf_d = nc.dram_tensor("bfl", [ER, D], F32, kind="ExternalInput")
    idb_d = nc.dram_tensor("identb", [128, 128], BF16, kind="ExternalInput")
    idf_d = nc.dram_tensor("identf", [128, 128], F32, kind="ExternalInput")
    gsel_d = nc.dram_tensor("grpsel", [128, 4], F32, kind="ExternalInput")
    ex4_d = nc.dram_tensor("expand4", [4, 128], F32, kind="ExternalInput")
    ex8_d = nc.dram_tensor("expand8", [8, 64], F32, kind="ExternalInput")
    onesb_d = nc.dram_tensor("onesb", [1, 128], BF16, kind="ExternalInput")
    onesf_d = nc.dram_tensor("onesf", [1, 128], F32, kind="ExternalInput")
    osel_d = nc.dram_tensor("osel", [8, 1], F32, kind="ExternalInput")
    y_d = nc.dram_tensor("y", [tok, D], F32, kind="ExternalOutput")

    modp_d = nc.dram_tensor("modp", [NMOD], F32)
    modf_d = nc.dram_tensor("modf", [NMOD], F32)
    mproc_d = nc.dram_tensor("mproc", [2 * D], F32)
    QW = KT * NSL // 4
    wagq_i = [nc.dram_tensor(f"wagq_i{q}", [128, QW], BF16) for q in range(4)]
    wagq_o = [nc.dram_tensor(f"wagq_o{q}", [8 * 128, QW], BF16)
              for q in range(4)]
    fagq_i = [nc.dram_tensor(f"fagq_i{q}", [128, QW], BF16) for q in range(4)]
    fagq_o = [nc.dram_tensor(f"fagq_o{q}", [8 * 128, QW], BF16)
              for q in range(4)]
    baga_i = nc.dram_tensor("baga_i", [NSL], F32)
    baga_o = nc.dram_tensor("baga_o", [D], F32)
    bagf_i = nc.dram_tensor("bagf_i", [NSL], F32)
    bagf_o = nc.dram_tensor("bagf_o", [D], F32)

    rg = [list(range(NCORES))]
    M = {}

    with ExitStack() as ctx:
        en = ctx.enter_context

        x_sb = en(nc.sbuf_tensor("x_sb", [128, st_n * D], BF16))
        xnT = en(nc.sbuf_tensor("xnT", [128, st_n * KT * 128], BF16))
        wcol = [en(nc.sbuf_tensor(f"wcol{i}", [128, KT * 512], BF16))
                for i in range(2)]
        xn_st = [en(nc.sbuf_tensor(f"xn{i}", [128, D], BF16)) for i in range(2)]
        idb = en(nc.sbuf_tensor("idb", [128, 128], BF16))
        idf = en(nc.sbuf_tensor("idf", [128, 128], F32))
        onesb = en(nc.sbuf_tensor("onesb_s", [1, 128], BF16))
        onesf = en(nc.sbuf_tensor("onesf_s", [1, 128], F32))
        gsel = en(nc.sbuf_tensor("gsel", [128, 4], F32))
        ex4 = en(nc.sbuf_tensor("ex4", [4, 128], F32))
        ex8 = en(nc.sbuf_tensor("ex8", [8, 64], F32))
        ca1c = en(nc.sbuf_tensor("ca1c", [128, KT], F32))
        cb1c = en(nc.sbuf_tensor("cb1c", [128, KT], F32))
        sapc = en(nc.sbuf_tensor("sapc", [128, KT], BF16))
        sbpc = en(nc.sbuf_tensor("sbpc", [128, KT], BF16))
        ga_b = en(nc.sbuf_tensor("ga_b", [128, 512], F32))
        gb_b = en(nc.sbuf_tensor("gb_b", [128, 512], F32))
        bbuf = [en(nc.sbuf_tensor(f"bbuf{i}", [1, 512], BF16))
                for i in range(2)]
        SgT = en(nc.sbuf_tensor("SgT", [128, KT * 72], BF16))
        bA_c = en(nc.sbuf_tensor("bA_c", [72, 1], F32))
        BfT = en(nc.sbuf_tensor("BfT", [64, D], BF16))
        hterT = en(nc.sbuf_tensor("hterT", [64, sub], BF16))
        logit = en(nc.sbuf_tensor("logit", [8, sub], F32))
        lg_tm = en(nc.sbuf_tensor("lg_tm", [128, st_n * 8], F32))
        wd_tm = en(nc.sbuf_tensor("wd_tm", [128, st_n * 8], F32))
        lg2 = en(nc.sbuf_tensor("lg2", [128, st_n * 8], F32))
        ggs = en(nc.sbuf_tensor("ggs", [128, 4 * st_n], F32))
        wdT = en(nc.sbuf_tensor("wdT", [8, sub], F32))
        hsT = en(nc.sbuf_tensor("hsT", [64, sub], BF16))
        stt = en(nc.sbuf_tensor("stt", [128, 48], F32))
        stt2 = en(nc.sbuf_tensor("stt2", [128, st_n * 48], F32))
        lsc = en(nc.sbuf_tensor("lsc", [128, 17], F32))
        mvar = en(nc.sbuf_tensor("mvar", [128, 4 * st_n], F32))
        epsc = en(nc.sbuf_tensor("epsc", [128, 1], F32))
        ostg = [en(nc.sbuf_tensor(f"ostg{i}", [128, 512], F32))
                for i in range(2)]

        ps = [en(nc.psum_tensor(f"ps{i}", [128, 512], F32)) for i in range(8)]

        sems = {}
        for name in ["s_init", "s_mw0", "s_mw1", "s_mvec", "s_ar", "s_vec",
                     "s_norm", "s_var", "s_tp", "s_xnt", "s_wc0", "s_wc1",
                     "s_mm", "s_ev", "s_gt", "s_gv", "s_gs", "s_out",
                     "s_y0", "s_y1", "s_x0", "s_x1", "s_x2", "s_x3",
                     "s_mk0", "s_mk1", "s_wt0", "s_wt1", "s_sm",
                     "s_bias", "s_ag", "s_misc", "s_bl0", "s_bl1",
                     "s_std", "s_nb", "s_mp0", "s_mp1", "s_cc", "s_mw2", "s_mw3",
                     "s_sm2", "s_mw4", "s_mw5", "s_mw6", "s_mw7",
                     "s_mp2", "s_mp3", "s_wt2", "s_wt3"]:
            sems[name] = Sem(en(nc.semaphore(name)))
        S = lambda n: sems[n]
        s_x = [S(f"s_x{i}") for i in range(4)]
        s_mw8 = [S(f"s_mw{i}") for i in range(8)]
        s_mk = [S("s_mk0"), S("s_mk1")]
        s_wt = [S("s_wt0"), S("s_wt1"), S("s_wt2"), S("s_wt3")]
        s_wcp = [S("s_wc0"), S("s_wc1")]
        s_bl = [S("s_bl0"), S("s_bl1")]
        s_yp = [S("s_y0"), S("s_y1")]
        s_mp = [S("s_mp0"), S("s_mp1"), S("s_mp2"), S("s_mp3")]
        s_smp = [S("s_sm"), S("s_sm2")]

        block = en(nc.Block())

        csl_s = en(nc.sbuf_tensor("csl_s", [4, 128], F32))
        sT = en(nc.sbuf_tensor("sT", [128, 4], F32))
        modv = en(nc.sbuf_tensor("modv", [32, 6 * 128], F32))
        gbf64 = en(nc.sbuf_tensor("gbf64", [64, D], BF16))
        wtile = en(nc.sbuf_tensor("wtile", [128, 2 * 512], F32))
        mtile = en(nc.sbuf_tensor("mtile", [128, 2 * 512], I32))
        red1 = en(nc.sbuf_tensor("red1", [128, 16], F32))
        osel = en(nc.sbuf_tensor("osel_s", [8, 1], F32))
        xnTf = xnT[:].bitcast(F32)
        smfr = [xnTf[0:64, 0:1024], xnTf[0:64, 1024:2048]]
        brow = ostg[0][0:1, 0:512]
        btmp = ostg[1][0:1, 0:72]
        btmp2 = ostg[1][0:1, 0:512]
        gsl8 = en(nc.sbuf_tensor("gsl8", [8, 2 * 512], F32))
        wmblk = wtile
        bm4 = en(nc.sbuf_tensor("bm4", [4, KT * 16], F32))
        em16 = en(nc.sbuf_tensor("em16", [128, 16], F32))

        @block.gpsimd
        def _(g):
            for dst, src in ((idb, idb_d), (idf, idf_d), (onesb, onesb_d),
                             (onesf, onesf_d), (gsel, gsel_d), (ex4, ex4_d),
                             (ex8, ex8_d), (csl_s, cs_d), (osel, osel_d)):
                S("s_init").inc(g.dma_start(out=dst[:], in_=src.ap()), 16)
            for st_i in range(st_n):
                s_x[st_i].inc(
                    g.dma_start(out=x_sb[:, st_i * D:(st_i + 1) * D],
                                in_=x_d.ap()[st_i * 128:(st_i + 1) * 128, :]), 16)
        M["init"] = S("s_init").v

        @block.scalar
        def _(s):
            s.wait_ge(S("s_init").h, M["init"])
            S("s_misc").inc(s.activation(csl_s[:], csl_s[:], AF.Silu))
        M["silu"] = S("s_misc").v

        @block.tensor
        def _(t):
            t.wait_ge(S("s_misc").h, M["silu"])
            S("s_misc").inc(t.transpose(ps[0][0:128, 0:4], csl_s[:],
                                        idf[0:4, 0:4]))
        M["sT_ps"] = S("s_misc").v

        @block.vector
        def _(v):
            v.memset(epsc[:], 1e-5)
            v.wait_ge(S("s_misc").h, M["sT_ps"])
            S("s_misc").inc(v.tensor_copy(sT[:], ps[0][0:128, 0:4]))
        M["sT"] = S("s_misc").v

        mw_marks = {}
        mvslot = [wcol[0][:].bitcast(F32), wcol[1][:].bitcast(F32)]

        @block.sync
        def _(sy):
            for q in range(12):
                for kt in (0, 1):
                    if q >= 2:
                        sy.wait_ge(S("s_mvec").h, (q - 2) * 16 + 12 + kt + 1)
                    mw_marks[(q, kt)] = s_mw8[4 * (q % 2) + kt].inc(
                        sy.dma_start(
                            out=mvslot[q % 2][:, kt * 2048:(kt + 1) * 2048],
                            in_=wm_d.ap()[kt * 128:(kt + 1) * 128,
                                          q * 2048:(q + 1) * 2048]), 16)

        @block.scalar
        def _(sc):
            for q in range(12):
                for kt in (2, 3):
                    if q >= 2:
                        sc.wait_ge(S("s_mvec").h, (q - 2) * 16 + 12 + kt + 1)
                    mw_marks[(q, kt)] = s_mw8[4 * (q % 2) + kt].inc(
                        sc.dma_start(
                            out=mvslot[q % 2][:, kt * 2048:(kt + 1) * 2048],
                            in_=wm_d.ap()[kt * 128:(kt + 1) * 128,
                                          q * 2048:(q + 1) * 2048]), 16)

        @block.tensor
        def _(t):
            t.wait_ge(S("s_misc").h, M["sT"])
            for q in range(12):
                for j in range(4):
                    nt = q * 4 + j
                    pbank = ps[1] if nt % 2 == 0 else ps[2]
                    if nt >= 2:
                        t.wait_ge(S("s_vec").h, S("s_vec").v + nt - 1)
                    for kt in range(4):
                        t.wait_ge(s_mw8[4 * (q % 2) + kt].h, mw_marks[(q, kt)])
                        S("s_mvec").inc(
                            t.matmul(pbank[0:1, :], sT[:, kt:kt + 1],
                                     mvslot[q % 2][:, kt * 2048 + j * 512:
                                                   kt * 2048 + (j + 1) * 512],
                                     start=(kt == 0), stop=(kt == 3)))
        mv_base = S("s_mvec").v - 192

        mp_marks = []
        mps_v0 = S("s_vec").v
        xnTf0 = xnT[:].bitcast(F32)
        mpslot = [xnTf0[0:1, 2048:3072], xnTf0[0:1, 3072:4096]]

        @block.vector
        def _(v):
            for nt in range(48):
                pbank = ps[1] if nt % 2 == 0 else ps[2]
                grp, j = nt // 2, nt % 2
                v.wait_ge(S("s_mvec").h, mv_base + (nt + 1) * 4)
                if grp >= 2 and j == 0:
                    v.wait_ge(s_mp[grp % 2].h, 16 * (grp // 2))
                S("s_vec").inc(v.tensor_copy(
                    mpslot[grp % 2][:, j * 512:(j + 1) * 512], pbank[0:1, :]))
        M["modps"] = S("s_vec").v

        @block.gpsimd
        def _(g):
            for grp in range(24):
                g.wait_ge(S("s_vec").h, mps_v0 + (grp + 1) * 2)
                mp_marks.append(s_mp[grp % 2].inc(
                    g.dma_start(
                        out=modp_d.ap()[bass.ds(grp * 1024, 1024)].unsqueeze(0),
                        in_=mpslot[grp % 2]), 16))
            for i in range(2):
                g.wait_ge(s_mp[i].h, s_mp[i].v)
            S("s_cc").inc(g.collective_compute(
                "AllReduce", ALU.add, replica_groups=rg,
                ins=[modp_d.ap()], outs=[modf_d.ap()]), 1)
            g.wait_ge(S("s_cc").h, S("s_cc").v)
            S("s_ar").inc(
                g.dma_start(out=modv[:].rearrange("j (v p) -> j v p", v=6),
                            in_=modf_d.ap().rearrange("(v j p) -> j v p",
                                                      v=6, p=128)), 16)
        M["modv"] = S("s_ar").v

        def vrow(i):
            return modv[:, i * 128:(i + 1) * 128]

        @block.vector
        def _(v):
            v.wait_ge(S("s_ar").h, M["modv"])
            v.tensor_scalar_add(vrow(1), vrow(1), 1.0)
            v.tensor_scalar_add(vrow(4), vrow(4), 1.0)
            v.drain()
            v.reciprocal(wtile[0:32, 0:128], vrow(1))
            v.drain()
            v.tensor_mul(vrow(0), vrow(0), wtile[0:32, 0:128])
            v.drain()
            v.reciprocal(wtile[0:32, 0:128], vrow(4))
            v.drain()
            v.tensor_mul(vrow(3), vrow(3), wtile[0:32, 0:128])
            v.drain()
            for src in (2, 5):
                gv = vrow(src)
                dd = wtile[0:32, 0:128]
                v.tensor_scalar(dd, gv, 0.0, None, ALU.is_ge)
                v.drain()
                v.tensor_scalar(dd, dd, 2.0, -1.0, ALU.mult, ALU.add)
                v.drain()
                v.tensor_mul(gv, gv, dd)
                v.drain()
                v.tensor_scalar_max(gv, gv, 0.25)
                v.drain()
                v.tensor_mul(gv, gv, dd)
                v.drain()
            S("s_vec").inc(v.drain())
        M["mrows"] = S("s_vec").v

        @block.tensor
        def _(t):
            t.wait_ge(S("s_vec").h, M["mrows"])
            for i, src in enumerate((1, 4, 0, 3)):
                S("s_gt").inc(t.transpose(ps[3 + i][:, 0:32], vrow(src),
                                          idf[0:32, 0:32]))
        M["mcolT"] = S("s_gt").v

        @block.vector
        def _(v):
            v.wait_ge(S("s_gt").h, M["mcolT"])
            S("s_vec").inc(v.tensor_copy(ca1c[:], ps[3][:, 0:32]))
            S("s_vec").inc(v.tensor_copy(cb1c[:], ps[4][:, 0:32]))
            S("s_vec").inc(v.tensor_copy(sapc[:], ps[5][:, 0:32]))
            S("s_vec").inc(v.tensor_copy(sbpc[:], ps[6][:, 0:32]))
        M["mcols"] = S("s_vec").v

        @block.gpsimd
        def _(g):
            g.wait_ge(S("s_vec").h, M["mcols"])
            mp = mproc_d.ap().rearrange("(j p) -> j p", j=64)
            S("s_ar").inc(g.dma_start(out=mp[0:32, :], in_=vrow(2)), 16)
            S("s_ar").inc(g.dma_start(out=mp[32:64, :], in_=vrow(5)), 16)
            g.wait_ge(S("s_ar").h, S("s_ar").v)
            S("s_ar").inc(
                g.dma_start(out=gsl8[:, 0:512],
                            in_=mproc_d.ap().rearrange("(h j) -> h j", h=16)
                            [0:8, :]), 16)
            S("s_ar").inc(
                g.dma_start(out=gsl8[:, 512:1024],
                            in_=mproc_d.ap().rearrange("(h j) -> h j", h=16)
                            [8:16, :]), 16)
            g.wait_ge(S("s_gt").h, S("s_gt").v + 4)
            for i in range(8):
                S("s_ar").inc(
                    g.dma_start(out=btmp2,
                                in_=mproc_d.ap()[bass.ds(D + i * 512, 512)]
                                .unsqueeze(0)), 16)
                g.wait_ge(S("s_gt").h, S("s_gt").v + 5 + i)
        ar_rows0 = S("s_ar").v - 10 * 16

        gt_rows0 = S("s_gt").v
        v_rows0 = S("s_vec").v

        @block.tensor
        def _(t):
            t.wait_ge(S("s_ar").h, ar_rows0 + 32)
            S("s_gt").inc(t.matmul(ps[4][0:1, :], osel[:], gsl8[:, 0:512],
                                   start=True, stop=True))
            S("s_gt").inc(t.matmul(ps[5][0:1, :], osel[:], gsl8[:, 512:1024],
                                   start=True, stop=True))
            t.wait_ge(S("s_vec").h, v_rows0 + 1)
            S("s_gt").inc(t.matmul(ps[0][:], onesf[:], brow,
                                   start=True, stop=True))
            t.wait_ge(S("s_vec").h, v_rows0 + 2)
            S("s_gt").inc(t.matmul(ps[1][:], onesf[:], btmp2,
                                   start=True, stop=True))
            for i in range(8):
                pbank = ps[2] if i % 2 == 0 else ps[3]
                t.wait_ge(S("s_ar").h, ar_rows0 + 32 + (i + 1) * 16)
                if i >= 2:
                    t.wait_ge(S("s_vec").h, v_rows0 + 4 + i - 1)
                S("s_gt").inc(t.matmul(pbank[0:64, :], onesf[0:1, 0:64],
                                       btmp2, start=True, stop=True))

        @block.vector
        def _(v):
            v.drain()
            v.wait_ge(S("s_gt").h, gt_rows0 + 1)
            S("s_vec").inc(v.tensor_copy(brow, ps[4][0:1, :]))
            v.wait_ge(S("s_gt").h, gt_rows0 + 2)
            S("s_vec").inc(v.tensor_copy(btmp2, ps[5][0:1, :]))
            v.wait_ge(S("s_gt").h, gt_rows0 + 3)
            S("s_vec").inc(v.tensor_copy(ga_b[:], ps[0][:]))
            v.wait_ge(S("s_gt").h, gt_rows0 + 4)
            S("s_vec").inc(v.tensor_copy(gb_b[:], ps[1][:]))
            for i in range(8):
                pbank = ps[2] if i % 2 == 0 else ps[3]
                v.wait_ge(S("s_gt").h, gt_rows0 + 5 + i)
                S("s_vec").inc(v.tensor_copy(gbf64[:, i * 512:(i + 1) * 512],
                                             pbank[0:64, :]))
        M["gbf64"] = S("s_vec").v
        M["gbc"] = M["gbf64"]

        mk_marks = []
        mk_v0 = S("s_vec").v

        @block.gpsimd
        def _(g):
            for kt in range(KT):
                buf = (kt % 2) * 512
                if kt >= 2:
                    g.wait_ge(S("s_vec").h, mk_v0 + 2 * (kt - 2) + 1)
                mk_marks.append(s_mk[kt % 2].inc(
                    g.dma_start(out=mtile[:, buf:buf + 512],
                                in_=mT_d.ap()[kt * 128:(kt + 1) * 128, :]),
                    16))
        bm_gt0 = S("s_gt").v

        @block.tensor
        def _(t):
            t.wait_ge(S("s_init").h, M["init"])
            for kt in range(KT):
                t.wait_ge(S("s_vec").h, mk_v0 + 2 * kt + 1)
                pbank = ps[4] if kt % 2 == 0 else ps[5]
                S("s_gt").inc(t.matmul(pbank[0:4, 0:16], gsel[:], red1[:],
                                       start=True, stop=True))

        @block.vector
        def _(v):
            v.drain()
            for kt in range(KT):
                buf = (kt % 2) * 512
                v.wait_ge(s_mk[kt % 2].h, mk_marks[kt])
                S("s_vec").inc(v.tensor_reduce(
                    red1[:],
                    mtile[:, buf:buf + 512].rearrange("p (nb i) -> p nb i", i=32),
                    AX.X, ALU.add))
                pbank = ps[4] if kt % 2 == 0 else ps[5]
                v.wait_ge(S("s_gt").h, bm_gt0 + kt + 1)
                S("s_vec").inc(v.tensor_copy(bm4[:, kt * 16:(kt + 1) * 16],
                                             pbank[0:4, 0:16]))
            v.drain()
            v.tensor_scalar(bm4[:], bm4[:], 0.5, None, ALU.is_ge)
            S("s_vec").inc(v.drain())
        M["bmb"] = S("s_vec").v

        def prep_matrix(mat_idx, src_d, dst_sb, agq_in, c1col, g_bcast, masked):
            gs_base = S("s_gs").v
            wt_marks = []
            wslot = [wtile[:, 0:512], wtile[:, 512:1024],
                     ostg[0][:, 0:512]]

            @block.sync
            def _(sy):
                for kt in range(KT):
                    gidx = gs_base + kt
                    if gidx < 3:
                        sy.wait_ge(S("s_vec").h, M["mrows"])
                    if gidx == 2:
                        sy.wait_ge(S("s_vec").h, M["gbf64"])
                    if gidx == 32:
                        sy.wait_ge(S("s_bias").h, 80)
                    if gidx - 2 > 0:
                        sy.wait_ge(S("s_gs").h, gidx - 2)
                    wt_marks.append(s_wt[gidx % 3].inc(
                        sy.dma_start(out=wslot[gidx % 3],
                                     in_=src_d.ap()[kt * 128:(kt + 1) * 128, :]),
                        16))

            em_gt0 = S("s_gt").v

            if masked:
                @block.tensor
                def _(t):
                    t.wait_ge(S("s_vec").h, M["bmb"])
                    for kt in range(KT):
                        pbank = ps[6] if kt % 2 == 0 else ps[7]
                        if kt >= 2:
                            t.wait_ge(S("s_gs").h, gs_base + kt - 1)
                        S("s_gt").inc(t.matmul(
                            pbank[:, 0:16], ex4[:],
                            bm4[0:4, kt * 16:(kt + 1) * 16],
                            start=True, stop=True))

            @block.vector
            def _(v):
                v.drain()
                v.wait_ge(S("s_vec").h, M["gbf64"])
                for kt in range(KT):
                    gidx = gs_base + kt
                    v.wait_ge(s_wt[gidx % 3].h, wt_marks[kt])
                    dst = dst_sb[:, kt * 512:(kt + 1) * 512]
                    wt = wslot[gidx % 3]
                    if masked:
                        pbank = ps[6] if kt % 2 == 0 else ps[7]
                        v.wait_ge(S("s_gt").h, em_gt0 + kt + 1)
                        v.tensor_copy(em16[:], pbank[:, 0:16])
                        v.drain()
                        v.tensor_tensor(
                            wt.rearrange("p (nb i) -> p nb i", i=32),
                            wt.rearrange("p (nb i) -> p nb i", i=32),
                            em16[:].unsqueeze(2).broadcast_to([128, 16, 32]),
                            ALU.mult)
                        v.drain()
                    v.tensor_scalar(wt, wt, c1col[:, kt:kt + 1], None, ALU.mult)
                    v.drain()
                    S("s_gs").inc(v.tensor_tensor(dst, wt, g_bcast[:], ALU.mult))

            gsv = S("s_gs").v
            sc = sapc if mat_idx == 0 else sbpc

            @block.tensor
            def _(t):
                t.wait_ge(S("s_gs").h, gsv)
                for kt in range(KT):
                    S("s_gt").inc(t.matmul(ps[0][0:1, :], sc[:, kt:kt + 1],
                                           dst_sb[:, kt * 512:(kt + 1) * 512],
                                           start=(kt == 0), stop=(kt == KT - 1)))
            bv = S("s_gt").v

            @block.vector
            def _(v):
                v.wait_ge(S("s_gt").h, bv)
                if mat_idx == 1:
                    v.wait_ge(S("s_bias").h, 80)
                S("s_vec").inc(v.tensor_copy(brow, ps[0][0:1, :]))
            bvv = S("s_vec").v
            M[f"brow_{mat_idx}"] = bvv

            bag_in = baga_i if mat_idx == 0 else bagf_i
            bag_out = baga_o if mat_idx == 0 else bagf_o
            agq_out = wagq_o if mat_idx == 0 else fagq_o

            @block.gpsimd
            def _(g):
                for q in range(4):
                    g.wait_ge(S("s_gs").h, gs_base + 8 * (q + 1))
                    mk = S("s_bias").inc(
                        g.dma_start(out=agq_in[q].ap(),
                                    in_=dst_sb[:, q * QW:(q + 1) * QW]), 16)
                    g.wait_ge(S("s_bias").h, mk)
                    S("s_ag").inc(g.collective_compute(
                        "AllGather", ALU.bypass, replica_groups=rg,
                        ins=[agq_in[q].ap()], outs=[agq_out[q].ap()]), 1)
                    M[f"agq_{mat_idx}_{q}"] = S("s_ag").v
                g.wait_ge(S("s_vec").h, bvv)
                mk = S("s_bias").inc(
                    g.dma_start(out=bag_in.ap().unsqueeze(0), in_=brow), 16)
                g.wait_ge(S("s_bias").h, mk)
                S("s_ag").inc(g.collective_compute(
                    "AllGather", ALU.bypass, replica_groups=rg,
                    ins=[bag_in.ap()], outs=[bag_out.ap()]), 1)
            M[f"ag_{mat_idx}"] = S("s_ag").v

        prep_matrix(0, wT_d, wcol[0], wagq_i, ca1c, ga_b, True)
        prep_matrix(1, fT_d, wcol[1], fagq_i, cb1c, gb_b, False)
        M["prep_done"] = S("s_bias").v

        M["ag"] = M["ag_1"]

        def emit_phase3():
            mm_after = S("s_mm").v
            ev_after = S("s_ev").v
            a_gt0 = S("s_gt").v
            a_v0 = S("s_vec").v
            a_marks = []

            @block.scalar
            def _(sy):
                sy.wait_ge(S("s_mm").h, mm_after)
                for q in range(4):
                    if q >= 2:
                        sy.wait_ge(S("s_gt").h, a_gt0 + (q - 1) * 8)
                    a_marks.append(s_smp[q % 2].inc(
                        sy.dma_start(out=smfr[q % 2],
                                     in_=A_d.ap()[:, q * 1024:(q + 1) * 1024]), 16))

            @block.tensor
            def _(t):
                t.wait_ge(S("s_ev").h, ev_after)
                for q in range(4):
                    t.wait_ge(s_smp[q % 2].h, a_marks[q])
                    for j in range(8):
                        kt = q * 8 + j
                        pbank = ps[1] if kt % 2 == 0 else ps[2]
                        if kt >= 2:
                            t.wait_ge(S("s_vec").h, a_v0 + kt - 1)
                        S("s_gt").inc(t.transpose(
                            pbank[:, 0:64], smfr[q % 2][:, j * 128:(j + 1) * 128],
                            idf[0:64, 0:64]))

            @block.vector
            def _(v):
                v.wait_ge(S("s_vec").h, M["gbf64"])
                for kt in range(KT):
                    pbank = ps[1] if kt % 2 == 0 else ps[2]
                    v.wait_ge(S("s_gt").h, a_gt0 + kt + 1)
                    S("s_vec").inc(v.tensor_scalar(
                        SgT[:, kt * 72:kt * 72 + 64], pbank[:, 0:64],
                        cb1c[:, kt:kt + 1], None, ALU.mult))
            M["A_T"] = S("s_vec").v

            gw_gt0 = S("s_gt").v
            gw_v0 = S("s_vec").v
            gw_marks = []

            @block.scalar
            def _(sy):
                for q in range(4):
                    if q < 2:
                        sy.wait_ge(S("s_gt").h, a_gt0 + 32)
                    else:
                        sy.wait_ge(S("s_gt").h, gw_gt0 + (q - 1) * 8)
                    gw_marks.append(s_smp[q % 2].inc(
                        sy.dma_start(out=smfr[q % 2][0:8, :],
                                     in_=gw_d.ap()[:, q * 1024:(q + 1) * 1024]), 16))

            @block.tensor
            def _(t):
                t.wait_ge(S("s_vec").h, gw_v0)
                for q in range(4):
                    t.wait_ge(s_smp[q % 2].h, gw_marks[q])
                    for j in range(8):
                        kt = q * 8 + j
                        pbank = ps[1] if kt % 2 == 0 else ps[2]
                        if kt >= 2:
                            t.wait_ge(S("s_vec").h, gw_v0 + kt - 1)
                        S("s_gt").inc(t.transpose(
                            pbank[:, 0:8], smfr[q % 2][0:8, j * 128:(j + 1) * 128],
                            idf[0:8, 0:8]))

            @block.vector
            def _(v):
                for kt in range(KT):
                    pbank = ps[1] if kt % 2 == 0 else ps[2]
                    v.wait_ge(S("s_gt").h, gw_gt0 + kt + 1)
                    S("s_vec").inc(v.tensor_scalar(
                        SgT[:, kt * 72 + 64:kt * 72 + 72], pbank[:, 0:8],
                        cb1c[:, kt:kt + 1], None, ALU.mult))
            M["Sg"] = S("s_vec").v

            bf_v0 = S("s_vec").v
            bf_marks = []

            @block.scalar
            def _(sy):
                for q in range(4):
                    if q < 2:
                        sy.wait_ge(S("s_gt").h, gw_gt0 + 32)
                    else:
                        sy.wait_ge(S("s_vec").h, bf_v0 + q - 1)
                    bf_marks.append(s_smp[q % 2].inc(
                        sy.dma_start(out=smfr[q % 2],
                                     in_=Bf_d.ap()[:, q * 1024:(q + 1) * 1024]), 16))

            @block.vector
            def _(v):
                v.drain()
                for q in range(4):
                    v.wait_ge(s_smp[q % 2].h, bf_marks[q])
                    S("s_vec").inc(v.tensor_tensor(
                        BfT[:, q * 1024:(q + 1) * 1024], smfr[q % 2],
                        gbf64[:, q * 1024:(q + 1) * 1024], ALU.mult))
            M["BfT"] = S("s_vec").v

            bA_gt0 = S("s_gt").v

            @block.tensor
            def _(t):
                t.wait_ge(S("s_vec").h, M["Sg"])
                for kt in range(KT):
                    S("s_gt").inc(t.matmul(ps[3][0:1, 0:72], sbpc[:, kt:kt + 1],
                                           SgT[:, kt * 72:(kt + 1) * 72],
                                           start=(kt == 0), stop=(kt == KT - 1)))
            M["bA_ps"] = S("s_gt").v

            @block.vector
            def _(v):
                v.wait_ge(S("s_gt").h, M["bA_ps"])
                S("s_vec").inc(v.tensor_copy(btmp, ps[3][0:1, 0:72]))
            M["bA_sb"] = S("s_vec").v

            @block.tensor
            def _(t):
                t.wait_ge(S("s_vec").h, M["bA_sb"])
                S("s_gt").inc(t.transpose(ps[4][0:72, 0:1], btmp,
                                          idf[0:1, 0:1]))
            M["bA_T"] = S("s_gt").v

            @block.vector
            def _(v):
                v.wait_ge(S("s_gt").h, M["bA_T"])
                S("s_vec").inc(v.tensor_copy(bA_c[:], ps[4][0:72, 0:1]))
            M["bA"] = S("s_vec").v


        emit_phase3()

        def xsb(st_i):
            return x_sb[:, st_i * D:(st_i + 1) * D]

        def xnT_blk(st_i, kt):
            off = (st_i * KT + kt) * 128
            return xnT[:, off:off + 128]

        oc_total = 2 * NG * st_n * 0 + 0
        oc = [0]

        for b in range(nsub):
            if b > 0:
                ev_need = S("s_out").v

                @block.gpsimd
                def _(g, b=b, ev_need=ev_need):
                    g.wait_ge(S("s_out").h, ev_need)
                    for st_i in range(st_n):
                        r0 = b * sub + st_i * 128
                        s_x[st_i].inc(
                            g.dma_start(out=xsb(st_i),
                                        in_=x_d.ap()[r0:r0 + 128, :]), 16)

            x_marks = [s_x[i].v for i in range(st_n)]

            for layer in range(2):
                var_base = S("s_var").v
                nb_base = S("s_nb").v
                std_base0 = S("s_std").v

                @block.vector
                def _(v, b=b, layer=layer, x_marks=tuple(x_marks)):
                    v.drain()
                    for st_i in range(st_n):
                        if layer == 0:
                            v.wait_ge(s_x[st_i].h, x_marks[st_i])
                            for cch in range(8):
                                v.bn_stats(
                                    stt[:, cch * 6:(cch + 1) * 6],
                                    xsb(st_i)[:, cch * 512:(cch + 1) * 512])
                            v.drain()
                            src = stt[:]
                        else:
                            src = stt2[:, st_i * 48:(st_i + 1) * 48]
                        means = src.rearrange("p (c t) -> p c t", t=3)[:, :, 1]
                        cvs = src.rearrange("p (c t) -> p c t", t=3)[:, :, 2]
                        c0 = st_i * 4
                        v.tensor_reduce(mvar[:, c0:c0 + 1], means, AX.X,
                                        ALU.add, negate=True)
                        v.tensor_reduce(mvar[:, c0 + 1:c0 + 2], cvs, AX.X,
                                        ALU.add)
                        v.tensor_tensor(lsc[:, 0:16], means, means, ALU.mult)
                        v.drain()
                        v.tensor_reduce(lsc[:, 16:17], lsc[:, 0:16], AX.X,
                                        ALU.add)
                        v.tensor_scalar(mvar[:, c0:c0 + 1], mvar[:, c0:c0 + 1],
                                        1.0 / 16, None, ALU.mult)
                        v.tensor_scalar(mvar[:, c0 + 1:c0 + 2],
                                        mvar[:, c0 + 1:c0 + 2], 1.0 / D, None,
                                        ALU.mult)
                        v.drain()
                        v.tensor_scalar(lsc[:, 16:17], lsc[:, 16:17],
                                        1.0 / 16, None, ALU.mult)
                        v.tensor_tensor(lsc[:, 0:1], mvar[:, c0:c0 + 1],
                                        mvar[:, c0:c0 + 1], ALU.mult)
                        v.drain()
                        v.tensor_tensor(mvar[:, c0 + 1:c0 + 2],
                                        mvar[:, c0 + 1:c0 + 2],
                                        lsc[:, 16:17], ALU.add)
                        v.drain()
                        S("s_var").inc(v.tensor_tensor(
                            mvar[:, c0 + 1:c0 + 2], mvar[:, c0 + 1:c0 + 2],
                            lsc[:, 0:1], ALU.subtract))
                        v.wait_ge(S("s_std").h, S("s_std").v + st_i + 1)
                        v.reciprocal(mvar[:, c0 + 2:c0 + 3],
                                     mvar[:, c0 + 1:c0 + 2])
                        v.drain()
                        v.tensor_tensor(mvar[:, c0 + 3:c0 + 4],
                                        mvar[:, c0:c0 + 1],
                                        mvar[:, c0 + 2:c0 + 3], ALU.mult)
                        S("s_nb").inc(v.drain())

                norm_base = S("s_norm").v
                tp_base = S("s_tp").v
                xnt_base = S("s_xnt").v
                std_base = std_base0

                @block.scalar
                def _(s, var_base=var_base, std_base=std_base):
                    for st_i in range(st_n):
                        s.wait_ge(S("s_var").h, var_base + st_i + 1)
                        c0 = st_i * 4
                        S("s_std").inc(
                            s.activation(mvar[:, c0 + 1:c0 + 2],
                                         mvar[:, c0 + 1:c0 + 2], AF.Sqrt,
                                         bias=epsc[:]))

                @block.scalar
                def _(s, b=b, layer=layer, nb_base=nb_base,
                      norm_base=norm_base, tp_base=tp_base,
                      xnt_base=xnt_base):
                    def emit_norm(st_i):
                        s.wait_ge(S("s_nb").h, nb_base + st_i + 1)
                        if st_i >= 2:
                            s.wait_ge(S("s_tp").h, tp_base + (st_i - 1) * 32)
                        c0 = st_i * 4
                        ins = s.activation(xn_st[st_i % 2][:], xsb(st_i),
                                           AF.Identity,
                                           bias=mvar[:, c0 + 3:c0 + 4],
                                           scale=mvar[:, c0 + 2:c0 + 3])
                        S("s_norm").inc(ins)

                    emit_norm(0)
                    if st_n > 1:
                        emit_norm(1)
                    for st_i in range(st_n):
                        for gi in range(8):
                            gidx = st_i * 8 + gi
                            pbank = ps[6 + (gi % 2)]
                            s.wait_ge(S("s_tp").h, tp_base + (gidx + 1) * 4)
                            off = (st_i * KT + gi * 4) * 128
                            S("s_xnt").inc(
                                s.copy(xnT[:, off:off + 512],
                                       pbank[:].bitcast(BF16)[:, 0:512]))
                        if st_i + 2 < st_n:
                            emit_norm(st_i + 2)

                @block.tensor
                def _(t, b=b, layer=layer, norm_base=norm_base,
                      tp_base=tp_base, xnt_base=xnt_base):
                    for st_i in range(st_n):
                        t.wait_ge(S("s_norm").h, norm_base + st_i + 1)
                        for gi in range(8):
                            gidx = st_i * 8 + gi
                            pbank = ps[6 + (gi % 2)]
                            if gidx >= 2:
                                t.wait_ge(S("s_xnt").h, xnt_base + gidx - 1)
                            for j in range(4):
                                kt = gi * 4 + j
                                S("s_tp").inc(t.transpose(
                                    pbank[:].bitcast(BF16)[:, j * 128:(j + 1) * 128],
                                    xn_st[st_i % 2][:, kt * 128:(kt + 1) * 128],
                                    idb[:]))
                S("s_xnt").v = xnt_base + st_n * 8
                xnt_done = S("s_xnt").v

                if layer == 1:
                    g_gt0 = S("s_gt").v

                    @block.tensor
                    def _(t, b=b, xnt_done=xnt_done):
                        t.wait_ge(S("s_xnt").h, xnt_done)
                        if b == 0:
                            t.wait_ge(S("s_vec").h, M["bA"])
                        for kt in range(KT):
                            rhs = xnT[:].rearrange("p (s k c) -> p s k c",
                                                   s=st_n, k=KT)[:, :, kt, :]
                            S("s_gt").inc(t.matmul(
                                ps[0][0:72, 0:sub],
                                SgT[:, kt * 72:(kt + 1) * 72],
                                rhs, start=(kt == 0), stop=(kt == KT - 1)))
                    g_ps = S("s_gt").v

                    @block.scalar
                    def _(s, b=b, g_ps=g_ps):
                        s.wait_ge(S("s_gt").h, g_ps)
                        S("s_gv").inc(
                            s.activation(hterT[:], ps[0][0:64, 0:sub],
                                         AF.Identity, bias=bA_c[0:64, :]))
                        S("s_gv").inc(
                            s.activation(logit[:], ps[0][64:72, 0:sub],
                                         AF.Identity, bias=bA_c[64:72, :]))
                    g_ev = S("s_gv").v

                    lgT_gt0 = S("s_gt").v
                    lgT_v0 = S("s_gv").v

                    @block.tensor
                    def _(t, b=b, g_ev=g_ev):
                        t.wait_ge(S("s_gv").h, g_ev)
                        for st_i in range(st_n):
                            pbank = ps[1] if st_i % 2 == 0 else ps[2]
                            if st_i >= 2:
                                t.wait_ge(S("s_gv").h, lgT_v0 + st_i - 1)
                            S("s_gt").inc(t.transpose(
                                pbank[:, 0:8],
                                logit[:, st_i * 128:(st_i + 1) * 128],
                                idf[0:8, 0:8]))

                    @block.vector
                    def _(v, b=b):
                        for st_i in range(st_n):
                            pbank = ps[1] if st_i % 2 == 0 else ps[2]
                            v.wait_ge(S("s_gt").h, lgT_gt0 + st_i + 1)
                            S("s_gv").inc(v.tensor_copy(
                                lg_tm[:, st_i * 8:(st_i + 1) * 8],
                                pbank[:, 0:8]))
                        v.drain()
                        for st_i in range(st_n):
                            l = lg_tm[:, st_i * 8:(st_i + 1) * 8]
                            wdc = wd_tm[:, st_i * 8:(st_i + 1) * 8]
                            l2 = lg2[:, st_i * 8:(st_i + 1) * 8]
                            m1 = ggs[:, st_i * 4:st_i * 4 + 1]
                            nm1 = ggs[:, st_i * 4 + 1:st_i * 4 + 2]
                            m2 = ggs[:, st_i * 4 + 2:st_i * 4 + 3]
                            v.tensor_reduce(m1, l, AX.X, ALU.max)
                            v.drain()
                            v.tensor_scalar(nm1, m1, -1.0, None, ALU.mult)
                            v.tensor_scalar(wdc, l, m1, None, ALU.is_ge)
                            v.drain()
                            v.scalar_tensor_tensor(l2, wdc, -1e9, l,
                                                   ALU.mult, ALU.add)
                            v.drain()
                            v.tensor_reduce(m2, l2, AX.X, ALU.max)
                            v.drain()
                            v.tensor_scalar(wdc, l, m2, None, ALU.is_ge)
                            S("s_gv").inc(v.drain())
                    keep_base = S("s_gv").v - st_n

                    @block.scalar
                    def _(s, b=b, keep_base=keep_base):
                        for st_i in range(st_n):
                            s.wait_ge(S("s_gv").h, keep_base + st_i + 1)
                            l = lg_tm[:, st_i * 8:(st_i + 1) * 8]
                            nm1 = ggs[:, st_i * 4 + 1:st_i * 4 + 2]
                            S("s_gs").inc(
                                s.activation(l, l, AF.Exp, bias=nm1, scale=1.0))
                    exp_base = S("s_gs").v - st_n

                    @block.vector
                    def _(v, b=b, exp_base=exp_base):
                        for st_i in range(st_n):
                            v.wait_ge(S("s_gs").h, exp_base + st_i + 1)
                            l = lg_tm[:, st_i * 8:(st_i + 1) * 8]
                            wdc = wd_tm[:, st_i * 8:(st_i + 1) * 8]
                            den = ggs[:, st_i * 4 + 3:st_i * 4 + 4]
                            v.tensor_tensor(wdc, wdc, l, ALU.mult)
                            v.drain()
                            v.tensor_reduce(den, wdc, AX.X, ALU.add)
                            v.drain()
                            v.reciprocal(den, den)
                            v.drain()
                            v.tensor_scalar(wdc, wdc, den, None, ALU.mult)
                            S("s_gv").inc(v.drain())
                    wd_base = S("s_gv").v - st_n

                    wdT_gt0 = S("s_gt").v
                    wdT_v0 = S("s_gv").v

                    @block.tensor
                    def _(t, b=b, wd_base=wd_base):
                        for st_i in range(st_n):
                            pbank = ps[1] if st_i % 2 == 0 else ps[2]
                            t.wait_ge(S("s_gv").h, wd_base + st_i + 1)
                            if st_i >= 2:
                                t.wait_ge(S("s_gv").h, wdT_v0 + st_i - 1)
                            S("s_gt").inc(t.transpose(
                                pbank[0:8, 0:128],
                                wd_tm[:, st_i * 8:(st_i + 1) * 8],
                                idf[:]))

                    @block.vector
                    def _(v, b=b):
                        for st_i in range(st_n):
                            pbank = ps[1] if st_i % 2 == 0 else ps[2]
                            v.wait_ge(S("s_gt").h, wdT_gt0 + st_i + 1)
                            S("s_gv").inc(v.tensor_copy(
                                wdT[:, st_i * 128:(st_i + 1) * 128],
                                pbank[0:8, 0:128]))
                    wdT_done = S("s_gv").v

                    @block.tensor
                    def _(t, b=b, wdT_done=wdT_done):
                        t.wait_ge(S("s_gv").h, wdT_done)
                        S("s_gt").inc(t.matmul(ps[3][0:64, 0:sub], ex8[:],
                                               wdT[:], start=True, stop=True))
                    ex_ps = S("s_gt").v

                    @block.vector
                    def _(v, b=b, ex_ps=ex_ps):
                        v.wait_ge(S("s_gt").h, ex_ps)
                        S("s_gv").inc(v.tensor_tensor(hsT[:], hterT[:],
                                                      ps[3][0:64, 0:sub],
                                                      ALU.mult))
                    M[f"hsT_{b}"] = S("s_gv").v

                ag_src = wagq_o if layer == 0 else fagq_o
                mm_base = S("s_mm").v
                ev_base = S("s_ev").v
                phase_idx = b * 2 + layer

                bl_marks = []
                out_base_g = S("s_out").v

                @block.gpsimd
                def _(g, b=b, layer=layer, phase_idx=phase_idx,
                      out_base_g=out_base_g):
                    if phase_idx <= 1:
                        g.wait_ge(S("s_ag").h, M[f"ag_{layer}"])
                    bo = (baga_o if layer == 0 else bagf_o).ap()\
                        .rearrange("(g j) -> g j", g=8)

                    def emit_outs(gh_o):
                        for st_i in range(st_n):
                            i = gh_o * st_n + st_i
                            g.wait_ge(S("s_out").h, out_base_g + i + 1)
                            r0 = b * sub + st_i * 128
                            gi_o = oc[0] + i
                            ins = g.dma_start(
                                out=y_d.ap()[r0:r0 + 128,
                                             gh_o * 512:(gh_o + 1) * 512],
                                in_=ostg[gi_o % 2][:])
                            s_yp[gi_o % 2].inc(ins, 16)

                    for gh in range(NG):
                        g_gh = phase_idx * NG + gh
                        if g_gh >= 2:
                            g.wait_ge(S("s_mm").h, (g_gh - 1) * st_n)
                        bl_marks.append(s_bl[gh % 2].inc(
                            g.dma_start(out=bbuf[gh % 2][:],
                                        in_=bo[gh:gh + 1, :]), 16))
                        if layer == 1 and gh >= 1:
                            emit_outs(gh - 1)
                    if layer == 1:
                        emit_outs(NG - 1)

                wc_marks = []

                @block.sync
                def _(sy, b=b, layer=layer, mm_base=mm_base,
                      phase_idx=phase_idx):
                    for gh in range(NG):
                        buf = gh % 2
                        if phase_idx == 0 and gh == 0:
                            sy.wait_ge(S("s_bias").h, 80)
                        if phase_idx == 0 and gh == 1:
                            sy.wait_ge(S("s_bias").h, 160)
                        g_gh = phase_idx * NG + gh
                        if g_gh >= 2:
                            sy.wait_ge(S("s_mm").h, (g_gh - 1) * st_n)
                        qm = []
                        for q in range(4):
                            if b == 0:
                                sy.wait_ge(S("s_ag").h,
                                           M[f"agq_{layer}_{q}"])
                            qm.append(s_wcp[buf].inc(
                                sy.dma_start(
                                    out=wcol[buf][:, q * QW:(q + 1) * QW],
                                    in_=ag_src[q].ap()[gh * 128:(gh + 1) * 128,
                                                       :]), 16))
                        wc_marks.append(qm)

                @block.tensor
                def _(t, b=b, layer=layer, ev_base=ev_base,
                      out_base0=S("s_out").v, xnt_done=xnt_done,
                      phase_idx=phase_idx, bl_marks=bl_marks,
                      wc_marks=wc_marks):
                    t.wait_ge(S("s_xnt").h, xnt_done)
                    if layer == 1:
                        t.wait_ge(S("s_gv").h, M[f"hsT_{b}"])
                        if b == 0:
                            t.wait_ge(S("s_vec").h, M["BfT"])
                    for gh in range(NG):
                        buf = gh % 2
                        if gh >= 2:
                            if layer == 0:
                                t.wait_ge(S("s_ev").h,
                                          ev_base + (gh - 1) * st_n)
                            else:
                                t.wait_ge(S("s_out").h,
                                          out_base0 + (gh - 1) * st_n)
                        bset = 0 if gh % 2 == 0 else 4
                        for kk in range(KT):
                            if kk % 8 == 0:
                                t.wait_ge(s_wcp[buf].h,
                                          wc_marks[gh][kk // 8])
                            for st_i in range(st_n):
                                t.matmul(ps[bset + st_i][:],
                                         xnT_blk(st_i, kk),
                                         wcol[buf][:, kk * 512:(kk + 1) * 512],
                                         start=(kk == 0), stop=False)
                        t.wait_ge(s_bl[gh % 2].h, bl_marks[gh])
                        for st_i in range(st_n):
                            if layer == 1:
                                t.matmul(ps[bset + st_i][:],
                                         hsT[:, st_i * 128:(st_i + 1) * 128],
                                         BfT[:, gh * 512:(gh + 1) * 512],
                                         start=False, stop=False)
                            S("s_mm").inc(t.matmul(ps[bset + st_i][:],
                                                   onesb[:], bbuf[gh % 2][:],
                                                   start=False, stop=True))

                @block.vector
                def _(v, b=b, layer=layer, mm_base=mm_base):
                    v.drain()
                    for gh in range(NG):
                        bset = 0 if gh % 2 == 0 else 4
                        for st_i in range(st_n):
                            v.wait_ge(S("s_mm").h,
                                      mm_base + gh * st_n + st_i + 1)
                            pps = ps[bset + st_i]
                            dst = xsb(st_i)[:, gh * 512:(gh + 1) * 512]
                            if layer == 0:
                                S("s_ev").inc(
                                    v.tensor_tensor(dst, pps[:], dst, ALU.add))
                            else:
                                i = oc[0] + gh * st_n + st_i
                                if i >= 2:
                                    v.wait_ge(s_yp[i % 2].h, 16 * (i // 2))
                                ob = ostg[i % 2]
                                ins = v.tensor_tensor(ob[:], pps[:], dst,
                                                      ALU.add)
                                S("s_out").inc(ins)
                        if layer == 0:
                            v.drain()
                            for st_i in range(st_n):
                                c6 = (st_i * 8 + gh) * 6
                                v.bn_stats(
                                    stt2[:, c6:c6 + 6],
                                    xsb(st_i)[:, gh * 512:(gh + 1) * 512])

                if layer == 1:
                    oc[0] += NG * st_n

        @block.gpsimd
        def _(g):
            g.wait_ge(s_yp[0].h, s_yp[0].v)
            g.wait_ge(s_yp[1].h, s_yp[1].v)

    return nc


def _host_inputs(L0_attn_data, L0_attn_weight, L0_attn_mask, L0_ffn_W_base,
                 L0_ffn_gate_W, L0_ffn_A, L0_ffn_B, L0_c, L0_wmod,
                 tok=TOK):
    x = np.ascontiguousarray(np.asarray(L0_attn_data, dtype=np.float32)
                             .reshape(-1, D))
    assert x.shape[0] == NCORES * tok
    wT = np.ascontiguousarray(np.asarray(L0_attn_weight, np.float32).T)
    mT = np.ascontiguousarray(np.asarray(L0_attn_mask, np.int32).T)
    fT = np.ascontiguousarray(np.asarray(L0_ffn_W_base, np.float32).T)
    wmod = np.asarray(L0_wmod, np.float32)
    cvec = np.asarray(L0_c, np.float32).reshape(D)
    gw = np.ascontiguousarray(np.asarray(L0_ffn_gate_W, np.float32))
    Afl = np.ascontiguousarray(np.asarray(L0_ffn_A, np.float32).reshape(ER, D))
    Bf = np.ascontiguousarray(np.asarray(L0_ffn_B, np.float32)
                              .transpose(0, 2, 1).reshape(ER, D))
    identb = np.eye(128, dtype=np.float32).astype(ml_dtypes.bfloat16)
    identf = np.eye(128, dtype=np.float32)
    grpsel = np.zeros((128, 4), np.float32)
    grpsel[np.arange(128), np.arange(128) // 32] = 1.0
    expand4 = np.ascontiguousarray(grpsel.T)
    expand8 = np.zeros((8, 64), np.float32)
    expand8[np.arange(64) // 8, np.arange(64)] = 2.0
    onesb = np.ones((1, 128), np.float32).astype(ml_dtypes.bfloat16)
    onesf = np.ones((1, 128), np.float32)

    in_maps = []
    for c in range(NCORES):
        n0 = c * NSL
        in_maps.append({
            "x": x[c * tok:(c + 1) * tok],
            "wt": np.ascontiguousarray(wT[:, n0:n0 + NSL]),
            "maskt": np.ascontiguousarray(mT[:, n0:n0 + NSL]),
            "wbt": np.ascontiguousarray(fT[:, n0:n0 + NSL]),
            "wmod": np.ascontiguousarray(wmod[c * WMK:(c + 1) * WMK]),
            "csl": np.ascontiguousarray(
                cvec[c * WMK:(c + 1) * WMK].reshape(4, 128)),
            "gwn": gw, "afl": Afl, "bfl": Bf,
            "identb": identb, "identf": identf, "grpsel": grpsel,
            "expand4": expand4, "expand8": expand8, "onesb": onesb,
            "onesf": onesf,
            "osel": np.eye(8, dtype=np.float32)[:, c:c + 1].copy(),
        })
    return in_maps


def kernel(L0_attn_data, L0_attn_weight, L0_attn_mask, L0_ffn_W_base,
           L0_ffn_gate_W, L0_ffn_A, L0_ffn_B, L0_c, L0_wmod):
    in_maps = _host_inputs(L0_attn_data, L0_attn_weight, L0_attn_mask,
                           L0_ffn_W_base, L0_ffn_gate_W, L0_ffn_A, L0_ffn_B,
                           L0_c, L0_wmod)
    nc = build_program()
    res = bass_utils.run_bass_kernel_spmd(nc, in_maps,
                                          core_ids=list(range(NCORES)))
    y = np.concatenate([res.results[c]["y"] for c in range(NCORES)], axis=0)
    return y.reshape(1, SEQ, D).astype(np.float32)


# version 15
# speedup vs baseline: 1.0576x; 1.0134x over previous
import numpy as np
import ml_dtypes
from contextlib import ExitStack

import concourse.bass as bass
import concourse.mybir as mybir
import concourse.bass_utils as bass_utils

F32 = mybir.dt.float32
BF16 = mybir.dt.bfloat16
I32 = mybir.dt.int32
AF = mybir.ActivationFunctionType
ALU = mybir.AluOpType
AX = mybir.AxisListType

D = 4096
NCORES = 8
SEQ = 16384
TOK = SEQ // NCORES
KT = D // 128
NG = 8
NSL = D // NCORES
NMOD = 6 * D
WMK = D // NCORES
ER = 64


class Sem:
    def __init__(self, handle):
        self.h = handle
        self.v = 0

    def inc(self, ins, n=1):
        ins.then_inc(self.h, n)
        self.v += n
        return self.v


def build_program(tok=TOK, nsub=4):
    sub = tok // nsub
    st_n = sub // 128
    nc = bass.Bass("TRN2", target_bir_lowering=False)

    x_d = nc.dram_tensor("x", [tok, D], F32, kind="ExternalInput")
    wT_d = nc.dram_tensor("wt", [D, NSL], F32, kind="ExternalInput")
    mT_d = nc.dram_tensor("maskt", [D, NSL], I32, kind="ExternalInput")
    fT_d = nc.dram_tensor("wbt", [D, NSL], F32, kind="ExternalInput")
    wm_d = nc.dram_tensor("wmod", [WMK, NMOD], F32, kind="ExternalInput")
    cs_d = nc.dram_tensor("csl", [4, 128], F32, kind="ExternalInput")
    gw_d = nc.dram_tensor("gwn", [8, D], F32, kind="ExternalInput")
    A_d = nc.dram_tensor("afl", [ER, D], F32, kind="ExternalInput")
    Bf_d = nc.dram_tensor("bfl", [ER, D], F32, kind="ExternalInput")
    idb_d = nc.dram_tensor("identb", [128, 128], BF16, kind="ExternalInput")
    idf_d = nc.dram_tensor("identf", [128, 128], F32, kind="ExternalInput")
    gsel_d = nc.dram_tensor("grpsel", [128, 4], F32, kind="ExternalInput")
    ex4_d = nc.dram_tensor("expand4", [4, 128], F32, kind="ExternalInput")
    ex8_d = nc.dram_tensor("expand8", [8, 64], F32, kind="ExternalInput")
    onesb_d = nc.dram_tensor("onesb", [1, 128], BF16, kind="ExternalInput")
    onesf_d = nc.dram_tensor("onesf", [1, 128], F32, kind="ExternalInput")
    osel_d = nc.dram_tensor("osel", [8, 1], F32, kind="ExternalInput")
    y_d = nc.dram_tensor("y", [tok, D], F32, kind="ExternalOutput")

    modp_d = nc.dram_tensor("modp", [NMOD], F32)
    modf_d = nc.dram_tensor("modf", [NMOD], F32)
    mproc_d = nc.dram_tensor("mproc", [2 * D], F32)
    QW = KT * NSL // 4
    wagq_i = [nc.dram_tensor(f"wagq_i{q}", [128, QW], BF16) for q in range(4)]
    wagq_o = [nc.dram_tensor(f"wagq_o{q}", [8 * 128, QW], BF16)
              for q in range(4)]
    fagq_i = [nc.dram_tensor(f"fagq_i{q}", [128, QW], BF16) for q in range(4)]
    fagq_o = [nc.dram_tensor(f"fagq_o{q}", [8 * 128, QW], BF16)
              for q in range(4)]
    baga_i = nc.dram_tensor("baga_i", [NSL], F32)
    baga_o = nc.dram_tensor("baga_o", [D], F32)
    bagf_i = nc.dram_tensor("bagf_i", [NSL], F32)
    bagf_o = nc.dram_tensor("bagf_o", [D], F32)
    dum_i = nc.dram_tensor("dum_i", [8], F32)
    dum_o = nc.dram_tensor("dum_o", [8], F32)

    rg = [list(range(NCORES))]
    M = {}

    with ExitStack() as ctx:
        en = ctx.enter_context

        x_sb = en(nc.sbuf_tensor("x_sb", [128, st_n * D], BF16))
        xnT = en(nc.sbuf_tensor("xnT", [128, st_n * KT * 128], BF16))
        wcol = [en(nc.sbuf_tensor(f"wcol{i}", [128, KT * 512], BF16))
                for i in range(2)]
        xn_st = [en(nc.sbuf_tensor(f"xn{i}", [128, D], BF16)) for i in range(2)]
        idb = en(nc.sbuf_tensor("idb", [128, 128], BF16))
        idf = en(nc.sbuf_tensor("idf", [128, 128], F32))
        onesb = en(nc.sbuf_tensor("onesb_s", [1, 128], BF16))
        onesf = en(nc.sbuf_tensor("onesf_s", [1, 128], F32))
        gsel = en(nc.sbuf_tensor("gsel", [128, 4], F32))
        ex4 = en(nc.sbuf_tensor("ex4", [4, 128], F32))
        ex8 = en(nc.sbuf_tensor("ex8", [8, 64], F32))
        ca1c = en(nc.sbuf_tensor("ca1c", [128, KT], F32))
        cb1c = en(nc.sbuf_tensor("cb1c", [128, KT], F32))
        sapc = en(nc.sbuf_tensor("sapc", [128, KT], BF16))
        sbpc = en(nc.sbuf_tensor("sbpc", [128, KT], BF16))
        ga_b = en(nc.sbuf_tensor("ga_b", [128, 512], F32))
        gb_b = en(nc.sbuf_tensor("gb_b", [128, 512], F32))
        bbuf = [en(nc.sbuf_tensor(f"bbuf{i}", [1, 512], BF16))
                for i in range(2)]
        SgT = en(nc.sbuf_tensor("SgT", [128, KT * 72], BF16))
        bA_c = en(nc.sbuf_tensor("bA_c", [72, 1], F32))
        BfT = en(nc.sbuf_tensor("BfT", [64, D], BF16))
        hterT = en(nc.sbuf_tensor("hterT", [64, sub], BF16))
        logit = en(nc.sbuf_tensor("logit", [8, sub], F32))
        lg_tm = en(nc.sbuf_tensor("lg_tm", [128, st_n * 8], F32))
        wd_tm = en(nc.sbuf_tensor("wd_tm", [128, st_n * 8], F32))
        lg2 = en(nc.sbuf_tensor("lg2", [128, st_n * 8], F32))
        ggs = en(nc.sbuf_tensor("ggs", [128, 4 * st_n], F32))
        wdT = en(nc.sbuf_tensor("wdT", [8, sub], F32))
        hsT = en(nc.sbuf_tensor("hsT", [64, sub], BF16))
        stt = en(nc.sbuf_tensor("stt", [128, 48], F32))
        stt2 = en(nc.sbuf_tensor("stt2", [128, st_n * 48], F32))
        stt3 = en(nc.sbuf_tensor("stt3", [128, st_n * 48], F32))
        lsc = en(nc.sbuf_tensor("lsc", [128, 17], F32))
        mvar = en(nc.sbuf_tensor("mvar", [128, 4 * st_n], F32))
        epsc = en(nc.sbuf_tensor("epsc", [128, 1], F32))
        ostg = [en(nc.sbuf_tensor(f"ostg{i}", [128, 512], F32))
                for i in range(2)]

        ps = [en(nc.psum_tensor(f"ps{i}", [128, 512], F32)) for i in range(8)]

        sems = {}
        for name in ["s_init", "s_mw0", "s_mw1", "s_mvec", "s_ar", "s_vec",
                     "s_norm", "s_var", "s_tp", "s_xnt", "s_wc0", "s_wc1",
                     "s_mm", "s_ev", "s_gt", "s_gv", "s_gs", "s_out",
                     "s_y0", "s_y1", "s_x0", "s_x1", "s_x2", "s_x3",
                     "s_mk0", "s_mk1", "s_wt0", "s_wt1", "s_sm",
                     "s_bias", "s_ag", "s_misc", "s_bl0", "s_bl1",
                     "s_std", "s_nb", "s_mp0", "s_mp1", "s_cc", "s_mw2", "s_mw3",
                     "s_sm2", "s_mw4", "s_mw5", "s_mw6", "s_mw7",
                     "s_mp2", "s_mp3", "s_wt2", "s_wt3"]:
            sems[name] = Sem(en(nc.semaphore(name)))
        S = lambda n: sems[n]
        s_x = [S(f"s_x{i}") for i in range(4)]
        s_mw8 = [S(f"s_mw{i}") for i in range(8)]
        s_mk = [S("s_mk0"), S("s_mk1")]
        s_wt = [S("s_wt0"), S("s_wt1"), S("s_wt2"), S("s_wt3")]
        s_wcp = [S("s_wc0"), S("s_wc1")]
        s_bl = [S("s_bl0"), S("s_bl1")]
        s_yp = [S("s_y0"), S("s_y1")]
        s_mp = [S("s_mp0"), S("s_mp1"), S("s_mp2"), S("s_mp3")]
        s_smp = [S("s_sm"), S("s_sm2")]

        block = en(nc.Block())

        csl_s = en(nc.sbuf_tensor("csl_s", [4, 128], F32))
        sT = en(nc.sbuf_tensor("sT", [128, 4], F32))
        modv = en(nc.sbuf_tensor("modv", [32, 6 * 128], F32))
        gbf64 = en(nc.sbuf_tensor("gbf64", [64, D], BF16))
        wtile = en(nc.sbuf_tensor("wtile", [128, 2 * 512], F32))
        mtile = en(nc.sbuf_tensor("mtile", [128, 2 * 512], I32))
        red1 = en(nc.sbuf_tensor("red1", [128, 16], F32))
        osel = en(nc.sbuf_tensor("osel_s", [8, 1], F32))
        xnTf = xnT[:].bitcast(F32)
        smfr = [xnTf[0:64, 0:1024], xnTf[0:64, 1024:2048]]
        brow = ostg[0][0:1, 0:512]
        btmp = ostg[1][0:1, 0:72]
        btmp2 = ostg[1][0:1, 0:512]
        gsl8 = en(nc.sbuf_tensor("gsl8", [8, 2 * 512], F32))
        wmblk = wtile
        bm4 = en(nc.sbuf_tensor("bm4", [4, KT * 16], F32))
        em16 = en(nc.sbuf_tensor("em16", [128, 16], F32))

        @block.gpsimd
        def _(g):
            S("s_cc").inc(g.collective_compute(
                "AllReduce", ALU.add, replica_groups=rg,
                ins=[dum_i.ap()], outs=[dum_o.ap()]), 1)
            for dst, src in ((idb, idb_d), (idf, idf_d), (onesb, onesb_d),
                             (onesf, onesf_d), (gsel, gsel_d), (ex4, ex4_d),
                             (ex8, ex8_d), (csl_s, cs_d), (osel, osel_d)):
                S("s_init").inc(g.dma_start(out=dst[:], in_=src.ap()), 16)
            for st_i in range(st_n):
                s_x[st_i].inc(
                    g.dma_start(out=x_sb[:, st_i * D:(st_i + 1) * D],
                                in_=x_d.ap()[st_i * 128:(st_i + 1) * 128, :]), 16)
        M["init"] = S("s_init").v

        @block.scalar
        def _(s):
            s.wait_ge(S("s_init").h, M["init"])
            S("s_misc").inc(s.activation(csl_s[:], csl_s[:], AF.Silu))
        M["silu"] = S("s_misc").v

        @block.tensor
        def _(t):
            t.wait_ge(S("s_misc").h, M["silu"])
            S("s_misc").inc(t.transpose(ps[0][0:128, 0:4], csl_s[:],
                                        idf[0:4, 0:4]))
        M["sT_ps"] = S("s_misc").v

        @block.vector
        def _(v):
            v.memset(epsc[:], 1e-5)
            v.wait_ge(S("s_misc").h, M["sT_ps"])
            S("s_misc").inc(v.tensor_copy(sT[:], ps[0][0:128, 0:4]))
        M["sT"] = S("s_misc").v

        mw_marks = {}
        mvslot = [wcol[0][:].bitcast(F32), wcol[1][:].bitcast(F32)]

        @block.sync
        def _(sy):
            for q in range(12):
                for kt in (0, 1):
                    if q >= 2:
                        sy.wait_ge(S("s_mvec").h, (q - 2) * 16 + 12 + kt + 1)
                    mw_marks[(q, kt)] = s_mw8[4 * (q % 2) + kt].inc(
                        sy.dma_start(
                            out=mvslot[q % 2][:, kt * 2048:(kt + 1) * 2048],
                            in_=wm_d.ap()[kt * 128:(kt + 1) * 128,
                                          q * 2048:(q + 1) * 2048]), 16)

        @block.scalar
        def _(sc):
            for q in range(12):
                for kt in (2, 3):
                    if q >= 2:
                        sc.wait_ge(S("s_mvec").h, (q - 2) * 16 + 12 + kt + 1)
                    mw_marks[(q, kt)] = s_mw8[4 * (q % 2) + kt].inc(
                        sc.dma_start(
                            out=mvslot[q % 2][:, kt * 2048:(kt + 1) * 2048],
                            in_=wm_d.ap()[kt * 128:(kt + 1) * 128,
                                          q * 2048:(q + 1) * 2048]), 16)

        @block.tensor
        def _(t):
            t.wait_ge(S("s_misc").h, M["sT"])
            for q in range(12):
                for j in range(4):
                    nt = q * 4 + j
                    pbank = ps[1] if nt % 2 == 0 else ps[2]
                    if nt >= 2:
                        t.wait_ge(S("s_vec").h, S("s_vec").v + nt - 1)
                    for kt in range(4):
                        t.wait_ge(s_mw8[4 * (q % 2) + kt].h, mw_marks[(q, kt)])
                        S("s_mvec").inc(
                            t.matmul(pbank[0:1, :], sT[:, kt:kt + 1],
                                     mvslot[q % 2][:, kt * 2048 + j * 512:
                                                   kt * 2048 + (j + 1) * 512],
                                     start=(kt == 0), stop=(kt == 3)))
        mv_base = S("s_mvec").v - 192

        mp_marks = []
        mps_v0 = S("s_vec").v
        xnTf0 = xnT[:].bitcast(F32)
        mpslot = [xnTf0[0:1, 2048:3072], xnTf0[0:1, 3072:4096]]

        @block.vector
        def _(v):
            for nt in range(48):
                pbank = ps[1] if nt % 2 == 0 else ps[2]
                grp, j = nt // 2, nt % 2
                v.wait_ge(S("s_mvec").h, mv_base + (nt + 1) * 4)
                if grp >= 2 and j == 0:
                    v.wait_ge(s_mp[grp % 2].h, 16 * (grp // 2))
                S("s_vec").inc(v.tensor_copy(
                    mpslot[grp % 2][:, j * 512:(j + 1) * 512], pbank[0:1, :]))
        M["modps"] = S("s_vec").v

        @block.gpsimd
        def _(g):
            for grp in range(24):
                g.wait_ge(S("s_vec").h, mps_v0 + (grp + 1) * 2)
                mp_marks.append(s_mp[grp % 2].inc(
                    g.dma_start(
                        out=modp_d.ap()[bass.ds(grp * 1024, 1024)].unsqueeze(0),
                        in_=mpslot[grp % 2]), 16))
            for i in range(2):
                g.wait_ge(s_mp[i].h, s_mp[i].v)
            S("s_cc").inc(g.collective_compute(
                "AllReduce", ALU.add, replica_groups=rg,
                ins=[modp_d.ap()], outs=[modf_d.ap()]), 1)
            g.wait_ge(S("s_cc").h, S("s_cc").v)
            S("s_ar").inc(
                g.dma_start(out=modv[:].rearrange("j (v p) -> j v p", v=6),
                            in_=modf_d.ap().rearrange("(v j p) -> j v p",
                                                      v=6, p=128)), 16)
        M["modv"] = S("s_ar").v

        def vrow(i):
            return modv[:, i * 128:(i + 1) * 128]

        @block.vector
        def _(v):
            v.wait_ge(S("s_ar").h, M["modv"])
            v.tensor_scalar_add(vrow(1), vrow(1), 1.0)
            v.tensor_scalar_add(vrow(4), vrow(4), 1.0)
            v.drain()
            v.reciprocal(wtile[0:32, 0:128], vrow(1))
            v.drain()
            v.tensor_mul(vrow(0), vrow(0), wtile[0:32, 0:128])
            v.drain()
            v.reciprocal(wtile[0:32, 0:128], vrow(4))
            v.drain()
            v.tensor_mul(vrow(3), vrow(3), wtile[0:32, 0:128])
            v.drain()
            for src in (2, 5):
                gv = vrow(src)
                dd = wtile[0:32, 0:128]
                v.tensor_scalar(dd, gv, 0.0, None, ALU.is_ge)
                v.drain()
                v.tensor_scalar(dd, dd, 2.0, -1.0, ALU.mult, ALU.add)
                v.drain()
                v.tensor_mul(gv, gv, dd)
                v.drain()
                v.tensor_scalar_max(gv, gv, 0.25)
                v.drain()
                v.tensor_mul(gv, gv, dd)
                v.drain()
            S("s_vec").inc(v.drain())
        M["mrows"] = S("s_vec").v

        @block.tensor
        def _(t):
            t.wait_ge(S("s_vec").h, M["mrows"])
            for i, src in enumerate((1, 4, 0, 3)):
                S("s_gt").inc(t.transpose(ps[3 + i][:, 0:32], vrow(src),
                                          idf[0:32, 0:32]))
        M["mcolT"] = S("s_gt").v

        @block.vector
        def _(v):
            v.wait_ge(S("s_gt").h, M["mcolT"])
            S("s_vec").inc(v.tensor_copy(ca1c[:], ps[3][:, 0:32]))
            S("s_vec").inc(v.tensor_copy(cb1c[:], ps[4][:, 0:32]))
            S("s_vec").inc(v.tensor_copy(sapc[:], ps[5][:, 0:32]))
            S("s_vec").inc(v.tensor_copy(sbpc[:], ps[6][:, 0:32]))
        M["mcols"] = S("s_vec").v

        @block.gpsimd
        def _(g):
            g.wait_ge(S("s_vec").h, M["mcols"])
            mp = mproc_d.ap().rearrange("(j p) -> j p", j=64)
            S("s_ar").inc(g.dma_start(out=mp[0:32, :], in_=vrow(2)), 16)
            S("s_ar").inc(g.dma_start(out=mp[32:64, :], in_=vrow(5)), 16)
            g.wait_ge(S("s_ar").h, S("s_ar").v)
            S("s_ar").inc(
                g.dma_start(out=gsl8[:, 0:512],
                            in_=mproc_d.ap().rearrange("(h j) -> h j", h=16)
                            [0:8, :]), 16)
            S("s_ar").inc(
                g.dma_start(out=gsl8[:, 512:1024],
                            in_=mproc_d.ap().rearrange("(h j) -> h j", h=16)
                            [8:16, :]), 16)
            g.wait_ge(S("s_gt").h, S("s_gt").v + 4)
            for i in range(8):
                S("s_ar").inc(
                    g.dma_start(out=btmp2,
                                in_=mproc_d.ap()[bass.ds(D + i * 512, 512)]
                                .unsqueeze(0)), 16)
                g.wait_ge(S("s_gt").h, S("s_gt").v + 5 + i)
        ar_rows0 = S("s_ar").v - 10 * 16

        gt_rows0 = S("s_gt").v
        v_rows0 = S("s_vec").v

        @block.tensor
        def _(t):
            t.wait_ge(S("s_ar").h, ar_rows0 + 32)
            S("s_gt").inc(t.matmul(ps[4][0:1, :], osel[:], gsl8[:, 0:512],
                                   start=True, stop=True))
            S("s_gt").inc(t.matmul(ps[5][0:1, :], osel[:], gsl8[:, 512:1024],
                                   start=True, stop=True))
            t.wait_ge(S("s_vec").h, v_rows0 + 1)
            S("s_gt").inc(t.matmul(ps[0][:], onesf[:], brow,
                                   start=True, stop=True))
            t.wait_ge(S("s_vec").h, v_rows0 + 2)
            S("s_gt").inc(t.matmul(ps[1][:], onesf[:], btmp2,
                                   start=True, stop=True))
            for i in range(8):
                pbank = ps[2] if i % 2 == 0 else ps[3]
                t.wait_ge(S("s_ar").h, ar_rows0 + 32 + (i + 1) * 16)
                if i >= 2:
                    t.wait_ge(S("s_vec").h, v_rows0 + 4 + i - 1)
                S("s_gt").inc(t.matmul(pbank[0:64, :], onesf[0:1, 0:64],
                                       btmp2, start=True, stop=True))

        @block.vector
        def _(v):
            v.drain()
            v.wait_ge(S("s_gt").h, gt_rows0 + 1)
            S("s_vec").inc(v.tensor_copy(brow, ps[4][0:1, :]))
            v.wait_ge(S("s_gt").h, gt_rows0 + 2)
            S("s_vec").inc(v.tensor_copy(btmp2, ps[5][0:1, :]))
            v.wait_ge(S("s_gt").h, gt_rows0 + 3)
            S("s_vec").inc(v.tensor_copy(ga_b[:], ps[0][:]))
            v.wait_ge(S("s_gt").h, gt_rows0 + 4)
            S("s_vec").inc(v.tensor_copy(gb_b[:], ps[1][:]))
            for i in range(8):
                pbank = ps[2] if i % 2 == 0 else ps[3]
                v.wait_ge(S("s_gt").h, gt_rows0 + 5 + i)
                S("s_vec").inc(v.tensor_copy(gbf64[:, i * 512:(i + 1) * 512],
                                             pbank[0:64, :]))
        M["gbf64"] = S("s_vec").v
        M["gbc"] = M["gbf64"]

        mk_marks = []
        mk_v0 = S("s_vec").v

        @block.gpsimd
        def _(g):
            for kt in range(KT):
                buf = (kt % 2) * 512
                if kt >= 2:
                    g.wait_ge(S("s_vec").h, mk_v0 + 2 * (kt - 2) + 1)
                mk_marks.append(s_mk[kt % 2].inc(
                    g.dma_start(out=mtile[:, buf:buf + 512],
                                in_=mT_d.ap()[kt * 128:(kt + 1) * 128, :]),
                    16))
        bm_gt0 = S("s_gt").v

        @block.tensor
        def _(t):
            t.wait_ge(S("s_init").h, M["init"])
            for kt in range(KT):
                t.wait_ge(S("s_vec").h, mk_v0 + 2 * kt + 1)
                pbank = ps[4] if kt % 2 == 0 else ps[5]
                S("s_gt").inc(t.matmul(pbank[0:4, 0:16], gsel[:], red1[:],
                                       start=True, stop=True))

        @block.vector
        def _(v):
            v.drain()
            for kt in range(KT):
                buf = (kt % 2) * 512
                v.wait_ge(s_mk[kt % 2].h, mk_marks[kt])
                S("s_vec").inc(v.tensor_reduce(
                    red1[:],
                    mtile[:, buf:buf + 512].rearrange("p (nb i) -> p nb i", i=32),
                    AX.X, ALU.add))
                pbank = ps[4] if kt % 2 == 0 else ps[5]
                v.wait_ge(S("s_gt").h, bm_gt0 + kt + 1)
                S("s_vec").inc(v.tensor_copy(bm4[:, kt * 16:(kt + 1) * 16],
                                             pbank[0:4, 0:16]))
            v.drain()
            v.tensor_scalar(bm4[:], bm4[:], 0.5, None, ALU.is_ge)
            S("s_vec").inc(v.drain())
        M["bmb"] = S("s_vec").v

        def prep_matrix(mat_idx, src_d, dst_sb, agq_in, c1col, g_bcast, masked):
            gs_base = S("s_gs").v
            wt_marks = []
            wslot = [wtile[:, 0:512], wtile[:, 512:1024],
                     ostg[0][:, 0:512]]

            @block.sync
            def _(sy):
                for kt in range(KT):
                    gidx = gs_base + kt
                    if gidx < 3:
                        sy.wait_ge(S("s_vec").h, M["mrows"])
                    if gidx == 2:
                        sy.wait_ge(S("s_vec").h, M["gbf64"])
                    if gidx == 32:
                        sy.wait_ge(S("s_bias").h, 80)
                    if gidx - 2 > 0:
                        sy.wait_ge(S("s_gs").h, gidx - 2)
                    wt_marks.append(s_wt[gidx % 3].inc(
                        sy.dma_start(out=wslot[gidx % 3],
                                     in_=src_d.ap()[kt * 128:(kt + 1) * 128, :]),
                        16))

            em_gt0 = S("s_gt").v

            if masked:
                @block.tensor
                def _(t):
                    t.wait_ge(S("s_vec").h, M["bmb"])
                    for kt in range(KT):
                        pbank = ps[6] if kt % 2 == 0 else ps[7]
                        if kt >= 2:
                            t.wait_ge(S("s_gs").h, gs_base + kt - 1)
                        S("s_gt").inc(t.matmul(
                            pbank[:, 0:16], ex4[:],
                            bm4[0:4, kt * 16:(kt + 1) * 16],
                            start=True, stop=True))

            @block.vector
            def _(v):
                v.drain()
                v.wait_ge(S("s_vec").h, M["gbf64"])
                for kt in range(KT):
                    gidx = gs_base + kt
                    v.wait_ge(s_wt[gidx % 3].h, wt_marks[kt])
                    dst = dst_sb[:, kt * 512:(kt + 1) * 512]
                    wt = wslot[gidx % 3]
                    if masked:
                        pbank = ps[6] if kt % 2 == 0 else ps[7]
                        v.wait_ge(S("s_gt").h, em_gt0 + kt + 1)
                        v.tensor_copy(em16[:], pbank[:, 0:16])
                        v.drain()
                        v.tensor_tensor(
                            wt.rearrange("p (nb i) -> p nb i", i=32),
                            wt.rearrange("p (nb i) -> p nb i", i=32),
                            em16[:].unsqueeze(2).broadcast_to([128, 16, 32]),
                            ALU.mult)
                        v.drain()
                    v.tensor_scalar(wt, wt, c1col[:, kt:kt + 1], None, ALU.mult)
                    v.drain()
                    S("s_gs").inc(v.tensor_tensor(dst, wt, g_bcast[:], ALU.mult))

            gsv = S("s_gs").v
            sc = sapc if mat_idx == 0 else sbpc

            @block.tensor
            def _(t):
                t.wait_ge(S("s_gs").h, gsv)
                for kt in range(KT):
                    S("s_gt").inc(t.matmul(ps[0][0:1, :], sc[:, kt:kt + 1],
                                           dst_sb[:, kt * 512:(kt + 1) * 512],
                                           start=(kt == 0), stop=(kt == KT - 1)))
            bv = S("s_gt").v

            @block.vector
            def _(v):
                v.wait_ge(S("s_gt").h, bv)
                if mat_idx == 1:
                    v.wait_ge(S("s_bias").h, 80)
                S("s_vec").inc(v.tensor_copy(brow, ps[0][0:1, :]))
            bvv = S("s_vec").v
            M[f"brow_{mat_idx}"] = bvv

            bag_in = baga_i if mat_idx == 0 else bagf_i
            bag_out = baga_o if mat_idx == 0 else bagf_o
            agq_out = wagq_o if mat_idx == 0 else fagq_o

            @block.gpsimd
            def _(g):
                for q in range(4):
                    g.wait_ge(S("s_gs").h, gs_base + 8 * (q + 1))
                    mk = S("s_bias").inc(
                        g.dma_start(out=agq_in[q].ap(),
                                    in_=dst_sb[:, q * QW:(q + 1) * QW]), 16)
                    g.wait_ge(S("s_bias").h, mk)
                    S("s_ag").inc(g.collective_compute(
                        "AllGather", ALU.bypass, replica_groups=rg,
                        ins=[agq_in[q].ap()], outs=[agq_out[q].ap()]), 1)
                    M[f"agq_{mat_idx}_{q}"] = S("s_ag").v
                g.wait_ge(S("s_vec").h, bvv)
                mk = S("s_bias").inc(
                    g.dma_start(out=bag_in.ap().unsqueeze(0), in_=brow), 16)
                g.wait_ge(S("s_bias").h, mk)
                S("s_ag").inc(g.collective_compute(
                    "AllGather", ALU.bypass, replica_groups=rg,
                    ins=[bag_in.ap()], outs=[bag_out.ap()]), 1)
            M[f"ag_{mat_idx}"] = S("s_ag").v

        prep_matrix(0, wT_d, wcol[0], wagq_i, ca1c, ga_b, True)
        prep_matrix(1, fT_d, wcol[1], fagq_i, cb1c, gb_b, False)
        M["prep_done"] = S("s_bias").v

        M["ag"] = M["ag_1"]

        def emit_phase3():
            mm_after = S("s_mm").v
            ev_after = S("s_ev").v
            a_gt0 = S("s_gt").v
            a_v0 = S("s_vec").v
            a_marks = []

            @block.scalar
            def _(sy):
                sy.wait_ge(S("s_mm").h, mm_after)
                for q in range(4):
                    if q >= 2:
                        sy.wait_ge(S("s_gt").h, a_gt0 + (q - 1) * 8)
                    a_marks.append(s_smp[q % 2].inc(
                        sy.dma_start(out=smfr[q % 2],
                                     in_=A_d.ap()[:, q * 1024:(q + 1) * 1024]), 16))

            @block.tensor
            def _(t):
                t.wait_ge(S("s_ev").h, ev_after)
                for q in range(4):
                    t.wait_ge(s_smp[q % 2].h, a_marks[q])
                    for j in range(8):
                        kt = q * 8 + j
                        pbank = ps[1] if kt % 2 == 0 else ps[2]
                        if kt >= 2:
                            t.wait_ge(S("s_vec").h, a_v0 + kt - 1)
                        S("s_gt").inc(t.transpose(
                            pbank[:, 0:64], smfr[q % 2][:, j * 128:(j + 1) * 128],
                            idf[0:64, 0:64]))

            @block.vector
            def _(v):
                v.wait_ge(S("s_vec").h, M["gbf64"])
                for kt in range(KT):
                    pbank = ps[1] if kt % 2 == 0 else ps[2]
                    v.wait_ge(S("s_gt").h, a_gt0 + kt + 1)
                    S("s_vec").inc(v.tensor_scalar(
                        SgT[:, kt * 72:kt * 72 + 64], pbank[:, 0:64],
                        cb1c[:, kt:kt + 1], None, ALU.mult))
            M["A_T"] = S("s_vec").v

            gw_gt0 = S("s_gt").v
            gw_v0 = S("s_vec").v
            gw_marks = []

            @block.scalar
            def _(sy):
                for q in range(4):
                    if q < 2:
                        sy.wait_ge(S("s_gt").h, a_gt0 + 32)
                    else:
                        sy.wait_ge(S("s_gt").h, gw_gt0 + (q - 1) * 8)
                    gw_marks.append(s_smp[q % 2].inc(
                        sy.dma_start(out=smfr[q % 2][0:8, :],
                                     in_=gw_d.ap()[:, q * 1024:(q + 1) * 1024]), 16))

            @block.tensor
            def _(t):
                t.wait_ge(S("s_vec").h, gw_v0)
                for q in range(4):
                    t.wait_ge(s_smp[q % 2].h, gw_marks[q])
                    for j in range(8):
                        kt = q * 8 + j
                        pbank = ps[1] if kt % 2 == 0 else ps[2]
                        if kt >= 2:
                            t.wait_ge(S("s_vec").h, gw_v0 + kt - 1)
                        S("s_gt").inc(t.transpose(
                            pbank[:, 0:8], smfr[q % 2][0:8, j * 128:(j + 1) * 128],
                            idf[0:8, 0:8]))

            @block.vector
            def _(v):
                for kt in range(KT):
                    pbank = ps[1] if kt % 2 == 0 else ps[2]
                    v.wait_ge(S("s_gt").h, gw_gt0 + kt + 1)
                    S("s_vec").inc(v.tensor_scalar(
                        SgT[:, kt * 72 + 64:kt * 72 + 72], pbank[:, 0:8],
                        cb1c[:, kt:kt + 1], None, ALU.mult))
            M["Sg"] = S("s_vec").v

            bf_v0 = S("s_vec").v
            bf_marks = []

            @block.scalar
            def _(sy):
                for q in range(4):
                    if q < 2:
                        sy.wait_ge(S("s_gt").h, gw_gt0 + 32)
                    else:
                        sy.wait_ge(S("s_vec").h, bf_v0 + q - 1)
                    bf_marks.append(s_smp[q % 2].inc(
                        sy.dma_start(out=smfr[q % 2],
                                     in_=Bf_d.ap()[:, q * 1024:(q + 1) * 1024]), 16))

            @block.vector
            def _(v):
                v.drain()
                for q in range(4):
                    v.wait_ge(s_smp[q % 2].h, bf_marks[q])
                    S("s_vec").inc(v.tensor_tensor(
                        BfT[:, q * 1024:(q + 1) * 1024], smfr[q % 2],
                        gbf64[:, q * 1024:(q + 1) * 1024], ALU.mult))
            M["BfT"] = S("s_vec").v

            bA_gt0 = S("s_gt").v

            @block.tensor
            def _(t):
                t.wait_ge(S("s_vec").h, M["Sg"])
                for kt in range(KT):
                    S("s_gt").inc(t.matmul(ps[3][0:1, 0:72], sbpc[:, kt:kt + 1],
                                           SgT[:, kt * 72:(kt + 1) * 72],
                                           start=(kt == 0), stop=(kt == KT - 1)))
            M["bA_ps"] = S("s_gt").v

            @block.vector
            def _(v):
                v.wait_ge(S("s_gt").h, M["bA_ps"])
                S("s_vec").inc(v.tensor_copy(btmp, ps[3][0:1, 0:72]))
            M["bA_sb"] = S("s_vec").v

            @block.tensor
            def _(t):
                t.wait_ge(S("s_vec").h, M["bA_sb"])
                S("s_gt").inc(t.transpose(ps[4][0:72, 0:1], btmp,
                                          idf[0:1, 0:1]))
            M["bA_T"] = S("s_gt").v

            @block.vector
            def _(v):
                v.wait_ge(S("s_gt").h, M["bA_T"])
                S("s_vec").inc(v.tensor_copy(bA_c[:], ps[4][0:72, 0:1]))
            M["bA"] = S("s_vec").v


        emit_phase3()

        def xsb(st_i):
            return x_sb[:, st_i * D:(st_i + 1) * D]

        def xnT_blk(st_i, kt):
            off = (st_i * KT + kt) * 128
            return xnT[:, off:off + 128]

        oc_total = 2 * NG * st_n * 0 + 0
        oc = [0]

        def xq_mark(b, gh):
            return 16 + (b - 1) * NG * 16 + (gh + 1) * 16

        for b in range(nsub):
            x_marks = [s_x[i].v for i in range(st_n)]

            for layer in range(2):
                var_base = S("s_var").v
                nb_base = S("s_nb").v
                std_base0 = S("s_std").v

                @block.vector
                def _(v, b=b, layer=layer, x_marks=tuple(x_marks)):
                    v.drain()
                    for st_i in range(st_n):
                        if layer == 0 and b == 0:
                            v.wait_ge(s_x[st_i].h, x_marks[st_i])
                            for cch in range(8):
                                v.bn_stats(
                                    stt[:, cch * 6:(cch + 1) * 6],
                                    xsb(st_i)[:, cch * 512:(cch + 1) * 512])
                            v.drain()
                            src = stt[:]
                        elif layer == 0:
                            src = stt3[:, st_i * 48:(st_i + 1) * 48]
                        else:
                            src = stt2[:, st_i * 48:(st_i + 1) * 48]
                        means = src.rearrange("p (c t) -> p c t", t=3)[:, :, 1]
                        cvs = src.rearrange("p (c t) -> p c t", t=3)[:, :, 2]
                        c0 = st_i * 4
                        v.tensor_reduce(mvar[:, c0:c0 + 1], means, AX.X,
                                        ALU.add, negate=True)
                        v.tensor_reduce(mvar[:, c0 + 1:c0 + 2], cvs, AX.X,
                                        ALU.add)
                        v.tensor_tensor(lsc[:, 0:16], means, means, ALU.mult)
                        v.drain()
                        v.tensor_reduce(lsc[:, 16:17], lsc[:, 0:16], AX.X,
                                        ALU.add)
                        v.tensor_scalar(mvar[:, c0:c0 + 1], mvar[:, c0:c0 + 1],
                                        1.0 / 16, None, ALU.mult)
                        v.tensor_scalar(mvar[:, c0 + 1:c0 + 2],
                                        mvar[:, c0 + 1:c0 + 2], 1.0 / D, None,
                                        ALU.mult)
                        v.drain()
                        v.tensor_scalar(lsc[:, 16:17], lsc[:, 16:17],
                                        1.0 / 16, None, ALU.mult)
                        v.tensor_tensor(lsc[:, 0:1], mvar[:, c0:c0 + 1],
                                        mvar[:, c0:c0 + 1], ALU.mult)
                        v.drain()
                        v.tensor_tensor(mvar[:, c0 + 1:c0 + 2],
                                        mvar[:, c0 + 1:c0 + 2],
                                        lsc[:, 16:17], ALU.add)
                        v.drain()
                        S("s_var").inc(v.tensor_tensor(
                            mvar[:, c0 + 1:c0 + 2], mvar[:, c0 + 1:c0 + 2],
                            lsc[:, 0:1], ALU.subtract))
                        v.wait_ge(S("s_std").h, S("s_std").v + st_i + 1)
                        v.reciprocal(mvar[:, c0 + 2:c0 + 3],
                                     mvar[:, c0 + 1:c0 + 2])
                        v.drain()
                        v.tensor_tensor(mvar[:, c0 + 3:c0 + 4],
                                        mvar[:, c0:c0 + 1],
                                        mvar[:, c0 + 2:c0 + 3], ALU.mult)
                        S("s_nb").inc(v.drain())

                norm_base = S("s_norm").v
                tp_base = S("s_tp").v
                xnt_base = S("s_xnt").v
                std_base = std_base0

                @block.scalar
                def _(s, var_base=var_base, std_base=std_base):
                    for st_i in range(st_n):
                        s.wait_ge(S("s_var").h, var_base + st_i + 1)
                        c0 = st_i * 4
                        S("s_std").inc(
                            s.activation(mvar[:, c0 + 1:c0 + 2],
                                         mvar[:, c0 + 1:c0 + 2], AF.Sqrt,
                                         bias=epsc[:]))

                @block.scalar
                def _(s, b=b, layer=layer, nb_base=nb_base,
                      norm_base=norm_base, tp_base=tp_base,
                      xnt_base=xnt_base):
                    def emit_norm(st_i):
                        s.wait_ge(S("s_nb").h, nb_base + st_i + 1)
                        if st_i >= 2:
                            s.wait_ge(S("s_tp").h, tp_base + (st_i - 1) * 32)
                        c0 = st_i * 4
                        ins = s.activation(xn_st[st_i % 2][:], xsb(st_i),
                                           AF.Identity,
                                           bias=mvar[:, c0 + 3:c0 + 4],
                                           scale=mvar[:, c0 + 2:c0 + 3])
                        S("s_norm").inc(ins)

                    emit_norm(0)
                    if st_n > 1:
                        emit_norm(1)
                    for st_i in range(st_n):
                        for gi in range(8):
                            gidx = st_i * 8 + gi
                            pbank = ps[6 + (gi % 2)]
                            s.wait_ge(S("s_tp").h, tp_base + (gidx + 1) * 4)
                            off = (st_i * KT + gi * 4) * 128
                            S("s_xnt").inc(
                                s.copy(xnT[:, off:off + 512],
                                       pbank[:].bitcast(BF16)[:, 0:512]))
                        if st_i + 2 < st_n:
                            emit_norm(st_i + 2)

                @block.tensor
                def _(t, b=b, layer=layer, norm_base=norm_base,
                      tp_base=tp_base, xnt_base=xnt_base):
                    for st_i in range(st_n):
                        t.wait_ge(S("s_norm").h, norm_base + st_i + 1)
                        for gi in range(8):
                            gidx = st_i * 8 + gi
                            pbank = ps[6 + (gi % 2)]
                            if gidx >= 2:
                                t.wait_ge(S("s_xnt").h, xnt_base + gidx - 1)
                            for j in range(4):
                                kt = gi * 4 + j
                                S("s_tp").inc(t.transpose(
                                    pbank[:].bitcast(BF16)[:, j * 128:(j + 1) * 128],
                                    xn_st[st_i % 2][:, kt * 128:(kt + 1) * 128],
                                    idb[:]))
                S("s_xnt").v = xnt_base + st_n * 8
                xnt_done = S("s_xnt").v

                if layer == 1:
                    g_gt0 = S("s_gt").v

                    @block.tensor
                    def _(t, b=b, xnt_done=xnt_done):
                        t.wait_ge(S("s_xnt").h, xnt_done)
                        if b == 0:
                            t.wait_ge(S("s_vec").h, M["bA"])
                        for kt in range(KT):
                            rhs = xnT[:].rearrange("p (s k c) -> p s k c",
                                                   s=st_n, k=KT)[:, :, kt, :]
                            S("s_gt").inc(t.matmul(
                                ps[0][0:72, 0:sub],
                                SgT[:, kt * 72:(kt + 1) * 72],
                                rhs, start=(kt == 0), stop=(kt == KT - 1)))
                    g_ps = S("s_gt").v

                    @block.scalar
                    def _(s, b=b, g_ps=g_ps):
                        s.wait_ge(S("s_gt").h, g_ps)
                        S("s_gv").inc(
                            s.activation(hterT[:], ps[0][0:64, 0:sub],
                                         AF.Identity, bias=bA_c[0:64, :]))
                        S("s_gv").inc(
                            s.activation(logit[:], ps[0][64:72, 0:sub],
                                         AF.Identity, bias=bA_c[64:72, :]))
                    g_ev = S("s_gv").v

                    lgT_gt0 = S("s_gt").v
                    lgT_v0 = S("s_gv").v

                    @block.tensor
                    def _(t, b=b, g_ev=g_ev):
                        t.wait_ge(S("s_gv").h, g_ev)
                        for st_i in range(st_n):
                            pbank = ps[1] if st_i % 2 == 0 else ps[2]
                            if st_i >= 2:
                                t.wait_ge(S("s_gv").h, lgT_v0 + st_i - 1)
                            S("s_gt").inc(t.transpose(
                                pbank[:, 0:8],
                                logit[:, st_i * 128:(st_i + 1) * 128],
                                idf[0:8, 0:8]))

                    @block.vector
                    def _(v, b=b):
                        for st_i in range(st_n):
                            pbank = ps[1] if st_i % 2 == 0 else ps[2]
                            v.wait_ge(S("s_gt").h, lgT_gt0 + st_i + 1)
                            S("s_gv").inc(v.tensor_copy(
                                lg_tm[:, st_i * 8:(st_i + 1) * 8],
                                pbank[:, 0:8]))
                        v.drain()
                        for st_i in range(st_n):
                            l = lg_tm[:, st_i * 8:(st_i + 1) * 8]
                            wdc = wd_tm[:, st_i * 8:(st_i + 1) * 8]
                            l2 = lg2[:, st_i * 8:(st_i + 1) * 8]
                            m1 = ggs[:, st_i * 4:st_i * 4 + 1]
                            nm1 = ggs[:, st_i * 4 + 1:st_i * 4 + 2]
                            m2 = ggs[:, st_i * 4 + 2:st_i * 4 + 3]
                            v.tensor_reduce(m1, l, AX.X, ALU.max)
                            v.drain()
                            v.tensor_scalar(nm1, m1, -1.0, None, ALU.mult)
                            v.tensor_scalar(wdc, l, m1, None, ALU.is_ge)
                            v.drain()
                            v.scalar_tensor_tensor(l2, wdc, -1e9, l,
                                                   ALU.mult, ALU.add)
                            v.drain()
                            v.tensor_reduce(m2, l2, AX.X, ALU.max)
                            v.drain()
                            v.tensor_scalar(wdc, l, m2, None, ALU.is_ge)
                            S("s_gv").inc(v.drain())
                    keep_base = S("s_gv").v - st_n

                    @block.scalar
                    def _(s, b=b, keep_base=keep_base):
                        for st_i in range(st_n):
                            s.wait_ge(S("s_gv").h, keep_base + st_i + 1)
                            l = lg_tm[:, st_i * 8:(st_i + 1) * 8]
                            nm1 = ggs[:, st_i * 4 + 1:st_i * 4 + 2]
                            S("s_gs").inc(
                                s.activation(l, l, AF.Exp, bias=nm1, scale=1.0))
                    exp_base = S("s_gs").v - st_n

                    @block.vector
                    def _(v, b=b, exp_base=exp_base):
                        for st_i in range(st_n):
                            v.wait_ge(S("s_gs").h, exp_base + st_i + 1)
                            l = lg_tm[:, st_i * 8:(st_i + 1) * 8]
                            wdc = wd_tm[:, st_i * 8:(st_i + 1) * 8]
                            den = ggs[:, st_i * 4 + 3:st_i * 4 + 4]
                            v.tensor_tensor(wdc, wdc, l, ALU.mult)
                            v.drain()
                            v.tensor_reduce(den, wdc, AX.X, ALU.add)
                            v.drain()
                            v.reciprocal(den, den)
                            v.drain()
                            v.tensor_scalar(wdc, wdc, den, None, ALU.mult)
                            S("s_gv").inc(v.drain())
                    wd_base = S("s_gv").v - st_n

                    wdT_gt0 = S("s_gt").v
                    wdT_v0 = S("s_gv").v

                    @block.tensor
                    def _(t, b=b, wd_base=wd_base):
                        for st_i in range(st_n):
                            pbank = ps[1] if st_i % 2 == 0 else ps[2]
                            t.wait_ge(S("s_gv").h, wd_base + st_i + 1)
                            if st_i >= 2:
                                t.wait_ge(S("s_gv").h, wdT_v0 + st_i - 1)
                            S("s_gt").inc(t.transpose(
                                pbank[0:8, 0:128],
                                wd_tm[:, st_i * 8:(st_i + 1) * 8],
                                idf[:]))

                    @block.vector
                    def _(v, b=b):
                        for st_i in range(st_n):
                            pbank = ps[1] if st_i % 2 == 0 else ps[2]
                            v.wait_ge(S("s_gt").h, wdT_gt0 + st_i + 1)
                            S("s_gv").inc(v.tensor_copy(
                                wdT[:, st_i * 128:(st_i + 1) * 128],
                                pbank[0:8, 0:128]))
                    wdT_done = S("s_gv").v

                    @block.tensor
                    def _(t, b=b, wdT_done=wdT_done):
                        t.wait_ge(S("s_gv").h, wdT_done)
                        S("s_gt").inc(t.matmul(ps[3][0:64, 0:sub], ex8[:],
                                               wdT[:], start=True, stop=True))
                    ex_ps = S("s_gt").v

                    @block.vector
                    def _(v, b=b, ex_ps=ex_ps):
                        v.wait_ge(S("s_gt").h, ex_ps)
                        S("s_gv").inc(v.tensor_tensor(hsT[:], hterT[:],
                                                      ps[3][0:64, 0:sub],
                                                      ALU.mult))
                    M[f"hsT_{b}"] = S("s_gv").v

                ag_src = wagq_o if layer == 0 else fagq_o
                mm_base = S("s_mm").v
                ev_base = S("s_ev").v
                phase_idx = b * 2 + layer

                bl_marks = []
                out_base_g = S("s_out").v

                @block.gpsimd
                def _(g, b=b, layer=layer, phase_idx=phase_idx,
                      out_base_g=out_base_g):
                    if phase_idx <= 1:
                        g.wait_ge(S("s_ag").h, M[f"ag_{layer}"])
                    bo = (baga_o if layer == 0 else bagf_o).ap()\
                        .rearrange("(g j) -> g j", g=8)

                    def emit_outs(gh_o):
                        for st_i in range(st_n):
                            i = gh_o * st_n + st_i
                            g.wait_ge(S("s_out").h, out_base_g + i + 1)
                            r0 = b * sub + st_i * 128
                            gi_o = oc[0] + i
                            ins = g.dma_start(
                                out=y_d.ap()[r0:r0 + 128,
                                             gh_o * 512:(gh_o + 1) * 512],
                                in_=ostg[gi_o % 2][:])
                            s_yp[gi_o % 2].inc(ins, 16)
                            if b + 1 < nsub:
                                r1 = (b + 1) * sub + st_i * 128
                                s_x[st_i].inc(g.dma_start(
                                    out=xsb(st_i)[:,
                                                  gh_o * 512:(gh_o + 1) * 512],
                                    in_=x_d.ap()[r1:r1 + 128,
                                                 gh_o * 512:(gh_o + 1) * 512]),
                                    16)

                    for gh in range(NG):
                        g_gh = phase_idx * NG + gh
                        if g_gh >= 2:
                            g.wait_ge(S("s_mm").h, (g_gh - 1) * st_n)
                        bl_marks.append(s_bl[gh % 2].inc(
                            g.dma_start(out=bbuf[gh % 2][:],
                                        in_=bo[gh:gh + 1, :]), 16))
                        if layer == 1 and gh >= 1:
                            emit_outs(gh - 1)
                    if layer == 1:
                        emit_outs(NG - 1)

                wc_marks = []

                @block.sync
                def _(sy, b=b, layer=layer, mm_base=mm_base,
                      phase_idx=phase_idx):
                    for gh in range(NG):
                        buf = gh % 2
                        if phase_idx == 0 and gh == 0:
                            sy.wait_ge(S("s_bias").h, 80)
                        if phase_idx == 0 and gh == 1:
                            sy.wait_ge(S("s_bias").h, 160)
                        g_gh = phase_idx * NG + gh
                        if g_gh >= 2:
                            sy.wait_ge(S("s_mm").h, (g_gh - 1) * st_n)
                        qm = []
                        for q in range(4):
                            if b == 0:
                                sy.wait_ge(S("s_ag").h,
                                           M[f"agq_{layer}_{q}"])
                            qm.append(s_wcp[buf].inc(
                                sy.dma_start(
                                    out=wcol[buf][:, q * QW:(q + 1) * QW],
                                    in_=ag_src[q].ap()[gh * 128:(gh + 1) * 128,
                                                       :]), 16))
                        wc_marks.append(qm)

                @block.tensor
                def _(t, b=b, layer=layer, ev_base=ev_base,
                      out_base0=S("s_out").v, xnt_done=xnt_done,
                      phase_idx=phase_idx, bl_marks=bl_marks,
                      wc_marks=wc_marks):
                    t.wait_ge(S("s_xnt").h, xnt_done)
                    if layer == 1:
                        t.wait_ge(S("s_gv").h, M[f"hsT_{b}"])
                        if b == 0:
                            t.wait_ge(S("s_vec").h, M["BfT"])
                    for gh in range(NG):
                        buf = gh % 2
                        if gh >= 2:
                            if layer == 0:
                                t.wait_ge(S("s_ev").h,
                                          ev_base + (gh - 1) * st_n)
                            else:
                                t.wait_ge(S("s_out").h,
                                          out_base0 + (gh - 1) * st_n)
                        bset = 0 if gh % 2 == 0 else 4
                        for kk in range(KT):
                            if kk % 8 == 0:
                                t.wait_ge(s_wcp[buf].h,
                                          wc_marks[gh][kk // 8])
                            for st_i in range(st_n):
                                t.matmul(ps[bset + st_i][:],
                                         xnT_blk(st_i, kk),
                                         wcol[buf][:, kk * 512:(kk + 1) * 512],
                                         start=(kk == 0), stop=False)
                        t.wait_ge(s_bl[gh % 2].h, bl_marks[gh])
                        for st_i in range(st_n):
                            if layer == 1:
                                t.matmul(ps[bset + st_i][:],
                                         hsT[:, st_i * 128:(st_i + 1) * 128],
                                         BfT[:, gh * 512:(gh + 1) * 512],
                                         start=False, stop=False)
                            S("s_mm").inc(t.matmul(ps[bset + st_i][:],
                                                   onesb[:], bbuf[gh % 2][:],
                                                   start=False, stop=True))

                @block.vector
                def _(v, b=b, layer=layer, mm_base=mm_base):
                    v.drain()
                    for gh in range(NG):
                        bset = 0 if gh % 2 == 0 else 4
                        for st_i in range(st_n):
                            v.wait_ge(S("s_mm").h,
                                      mm_base + gh * st_n + st_i + 1)
                            pps = ps[bset + st_i]
                            dst = xsb(st_i)[:, gh * 512:(gh + 1) * 512]
                            if layer == 0:
                                S("s_ev").inc(
                                    v.tensor_tensor(dst, pps[:], dst, ALU.add))
                            else:
                                i = oc[0] + gh * st_n + st_i
                                if i >= 2:
                                    v.wait_ge(s_yp[i % 2].h, 16 * (i // 2))
                                ob = ostg[i % 2]
                                ins = v.tensor_tensor(ob[:], pps[:], dst,
                                                      ALU.add)
                                S("s_out").inc(ins)
                        if layer == 0:
                            v.drain()
                            for st_i in range(st_n):
                                c6 = (st_i * 8 + gh) * 6
                                v.bn_stats(
                                    stt2[:, c6:c6 + 6],
                                    xsb(st_i)[:, gh * 512:(gh + 1) * 512])
                        elif b + 1 < nsub and gh >= 1:
                            gl = gh - 1
                            for st_i in range(st_n):
                                v.wait_ge(s_x[st_i].h, xq_mark(b + 1, gl))
                                c6 = (st_i * 8 + gl) * 6
                                v.bn_stats(
                                    stt3[:, c6:c6 + 6],
                                    xsb(st_i)[:, gl * 512:(gl + 1) * 512])
                    if layer == 1 and b + 1 < nsub:
                        for st_i in range(st_n):
                            v.wait_ge(s_x[st_i].h, xq_mark(b + 1, NG - 1))
                            c6 = (st_i * 8 + NG - 1) * 6
                            v.bn_stats(
                                stt3[:, c6:c6 + 6],
                                xsb(st_i)[:, (NG - 1) * 512:NG * 512])

                if layer == 1:
                    oc[0] += NG * st_n

        @block.gpsimd
        def _(g):
            g.wait_ge(s_yp[0].h, s_yp[0].v)
            g.wait_ge(s_yp[1].h, s_yp[1].v)

    return nc


def _host_inputs(L0_attn_data, L0_attn_weight, L0_attn_mask, L0_ffn_W_base,
                 L0_ffn_gate_W, L0_ffn_A, L0_ffn_B, L0_c, L0_wmod,
                 tok=TOK):
    x = np.ascontiguousarray(np.asarray(L0_attn_data, dtype=np.float32)
                             .reshape(-1, D))
    assert x.shape[0] == NCORES * tok
    wT = np.ascontiguousarray(np.asarray(L0_attn_weight, np.float32).T)
    mT = np.ascontiguousarray(np.asarray(L0_attn_mask, np.int32).T)
    fT = np.ascontiguousarray(np.asarray(L0_ffn_W_base, np.float32).T)
    wmod = np.asarray(L0_wmod, np.float32)
    cvec = np.asarray(L0_c, np.float32).reshape(D)
    gw = np.ascontiguousarray(np.asarray(L0_ffn_gate_W, np.float32))
    Afl = np.ascontiguousarray(np.asarray(L0_ffn_A, np.float32).reshape(ER, D))
    Bf = np.ascontiguousarray(np.asarray(L0_ffn_B, np.float32)
                              .transpose(0, 2, 1).reshape(ER, D))
    identb = np.eye(128, dtype=np.float32).astype(ml_dtypes.bfloat16)
    identf = np.eye(128, dtype=np.float32)
    grpsel = np.zeros((128, 4), np.float32)
    grpsel[np.arange(128), np.arange(128) // 32] = 1.0
    expand4 = np.ascontiguousarray(grpsel.T)
    expand8 = np.zeros((8, 64), np.float32)
    expand8[np.arange(64) // 8, np.arange(64)] = 2.0
    onesb = np.ones((1, 128), np.float32).astype(ml_dtypes.bfloat16)
    onesf = np.ones((1, 128), np.float32)

    in_maps = []
    for c in range(NCORES):
        n0 = c * NSL
        in_maps.append({
            "x": x[c * tok:(c + 1) * tok],
            "wt": np.ascontiguousarray(wT[:, n0:n0 + NSL]),
            "maskt": np.ascontiguousarray(mT[:, n0:n0 + NSL]),
            "wbt": np.ascontiguousarray(fT[:, n0:n0 + NSL]),
            "wmod": np.ascontiguousarray(wmod[c * WMK:(c + 1) * WMK]),
            "csl": np.ascontiguousarray(
                cvec[c * WMK:(c + 1) * WMK].reshape(4, 128)),
            "gwn": gw, "afl": Afl, "bfl": Bf,
            "identb": identb, "identf": identf, "grpsel": grpsel,
            "expand4": expand4, "expand8": expand8, "onesb": onesb,
            "onesf": onesf,
            "osel": np.eye(8, dtype=np.float32)[:, c:c + 1].copy(),
        })
    return in_maps


def kernel(L0_attn_data, L0_attn_weight, L0_attn_mask, L0_ffn_W_base,
           L0_ffn_gate_W, L0_ffn_A, L0_ffn_B, L0_c, L0_wmod):
    in_maps = _host_inputs(L0_attn_data, L0_attn_weight, L0_attn_mask,
                           L0_ffn_W_base, L0_ffn_gate_W, L0_ffn_A, L0_ffn_B,
                           L0_c, L0_wmod)
    nc = build_program()
    res = bass_utils.run_bass_kernel_spmd(nc, in_maps,
                                          core_ids=list(range(NCORES)))
    y = np.concatenate([res.results[c]["y"] for c in range(NCORES)], axis=0)
    return y.reshape(1, SEQ, D).astype(np.float32)
